# version 24
# baseline (speedup 1.0000x reference)
import numpy as np
import ml_dtypes
import concourse.bass as bass
import concourse.mybir as mybir
from concourse.bass_utils import run_bass_kernel_spmd

F32 = mybir.dt.float32
BF16 = mybir.dt.bfloat16
AF = mybir.ActivationFunctionType
ALU = mybir.AluOpType
AX = mybir.AxisListType
BF = ml_dtypes.bfloat16

ENGINES = ("sync", "tensor", "vector", "scalar", "gpsimd")

D = 1024
T = 2048
NT = T // 128
KC = D // 128
EPS = 1.0e-5


class Sem:
    __slots__ = ("h", "v", "name")

    def __init__(self, h, name):
        self.h = h
        self.v = 0
        self.name = name


class Prog:
    def __init__(self, nc):
        self.nc = nc
        self.q = {n: [] for n in ENGINES}

    def sem(self, name):
        return Sem(self.nc.alloc_semaphore(name), name)

    def wait(self, eng, sem, val=None):
        v = sem.v if val is None else val
        self.q[eng].append(lambda e, _s=sem, _v=v: e.wait_ge(_s.h, _v))

    def op(self, eng, builder, inc=None):
        if inc is not None:
            inc.v += 1

        def f(e, _b=builder, _i=inc):
            ins = _b(e)
            if _i is not None:
                ins.then_inc(_i.h, 1)
        self.q[eng].append(f)

    def fop(self, eng, builder, sem):
        self.op(eng, builder, inc=sem)
        self.wait(eng, sem)

    def dma(self, eng, out, in_, inc=None, **kw):
        if inc is not None:
            inc.v += 16

        def f(e, _o=out, _in=in_, _i=inc, _kw=kw):
            ins = e.dma_start(out=_o, in_=_in, **_kw)
            if _i is not None:
                ins.then_inc(_i.h, 16)
        self.q[eng].append(f)

    def build_block(self):
        nc = self.nc
        with nc.Block() as block:
            @block.sync
            def _(e):
                for f in self.q["sync"]:
                    f(e)

            @block.tensor
            def _(e):
                for f in self.q["tensor"]:
                    f(e)

            @block.vector
            def _(e):
                for f in self.q["vector"]:
                    f(e)

            @block.scalar
            def _(e):
                for f in self.q["scalar"]:
                    f(e)

            @block.gpsimd
            def _(e):
                for f in self.q["gpsimd"]:
                    f(e)


def build_prep(p, io):
    nc = p.nc
    t_x, t_g, t_b, t_rW, t_rb = io["x"], io["ln_g"], io["ln_b"], io["rW"], io["rb"]
    t_xn, t_xnT, t_zT, t_wr, t_wrT = io["xn"], io["xnT"], io["zT"], io["wr"], io["wrT"]

    identb_c = nc.inline_tensor(np.eye(128, dtype=BF), name="prep_idb").ap()
    identf_c = nc.inline_tensor(np.eye(128, dtype=np.float32), name="prep_idf").ap()
    onesf_c = nc.inline_tensor(np.ones((1, 128), dtype=np.float32), name="prep_onesf").ap()

    xt = [nc.alloc_sbuf_tensor("pp_xt0", [128, D], F32),
          nc.alloc_sbuf_tensor("pp_xt1", [128, D], F32)]
    xnf = nc.alloc_sbuf_tensor("pp_xnf", [128, D], F32)
    xnb = nc.alloc_sbuf_tensor("pp_xnb", [128, D], BF16)
    zf = nc.alloc_sbuf_tensor("pp_zf", [128, D], F32)
    zb = nc.alloc_sbuf_tensor("pp_zb", [128, D], BF16)
    ztf = nc.alloc_sbuf_tensor("pp_ztf", [128, D], F32)
    sqd = nc.alloc_sbuf_tensor("pp_sqd", [128, D], F32)
    stat = nc.alloc_sbuf_tensor("pp_stat", [128, 16], F32)
    gb = nc.alloc_sbuf_tensor("pp_gb", [128, D], F32)
    bb = nc.alloc_sbuf_tensor("pp_bb", [128, D], F32)
    grow = nc.alloc_sbuf_tensor("pp_grow", [1, D], F32)
    brow = nc.alloc_sbuf_tensor("pp_brow", [1, D], F32)
    rbrow = nc.alloc_sbuf_tensor("pp_rbrow", [1, 8], F32)
    rw = nc.alloc_sbuf_tensor("pp_rw", [128, KC * 8], F32)
    onesf = nc.alloc_sbuf_tensor("pp_onesf", [1, 128], F32)
    idb = nc.alloc_sbuf_tensor("pp_idb", [128, 128], BF16)
    idf = nc.alloc_sbuf_tensor("pp_idf", [128, 128], F32)
    rt = nc.alloc_sbuf_tensor("pp_rt", [128, 8], F32)
    rt2 = nc.alloc_sbuf_tensor("pp_rt2", [128, 8], F32)
    wv = nc.alloc_sbuf_tensor("pp_wv", [128, 8], F32)
    wvT = nc.alloc_sbuf_tensor("pp_wvT", [8, 128], F32)
    stgx = nc.alloc_sbuf_tensor("pp_stgx", [128, D], BF16)
    stgz = nc.alloc_sbuf_tensor("pp_stgz", [128, D], BF16)

    ps_tb = nc.alloc_psum_tensor("pp_ps_tb", [128, D], BF16)
    ps_tz = nc.alloc_psum_tensor("pp_ps_tz", [128, D], BF16)
    ps_tf = nc.alloc_psum_tensor("pp_ps_tf", [128, D], F32)
    ps_r = nc.alloc_psum_tensor("pp_ps_r", [128, 8], F32)
    ps_wT = nc.alloc_psum_tensor("pp_ps_wT", [8, 128], F32)

    ld = p.sem("pp_ld")
    pe = p.sem("pp_pe")
    dv = p.sem("pp_dv")
    ac = p.sem("pp_ac")
    st = p.sem("pp_st")

    p.dma("sync", grow[:], t_g[None, :], inc=ld)
    p.dma("sync", brow[:], t_b[None, :], inc=ld)
    p.dma("sync", rbrow[:], t_rb[None, :], inc=ld)
    p.dma("sync", rw[:].rearrange("p (k e) -> p k e", k=KC),
          t_rW.rearrange("(k p) e -> p k e", p=128), inc=ld)
    p.dma("sync", idb[:], identb_c, inc=ld)
    p.dma("sync", idf[:], identf_c, inc=ld)
    p.dma("sync", onesf[:], onesf_c, inc=ld)
    ld_pre = ld.v

    p.wait("vector", ld, ld_pre)
    p.op("vector", lambda e: e.memset(stat[:, 15:16], EPS), inc=dv)
    p.wait("tensor", ld, ld_pre)
    for h in range(2):
        p.op("tensor", lambda e, _h=h: e.matmul(ps_tf[:, _h * 512:(_h + 1) * 512], onesf[:],
                                                grow[0:1, _h * 512:(_h + 1) * 512], start=True, stop=True),
             inc=pe if h == 1 else None)
    p.wait("vector", pe)
    p.op("vector", lambda e: e.tensor_copy(gb[:], ps_tf[:]), inc=dv)
    dv_gb = dv.v
    p.wait("tensor", dv, dv_gb)
    for h in range(2):
        p.op("tensor", lambda e, _h=h: e.matmul(ps_tf[:, _h * 512:(_h + 1) * 512], onesf[:],
                                                brow[0:1, _h * 512:(_h + 1) * 512], start=True, stop=True),
             inc=pe if h == 1 else None)
    p.wait("vector", pe)
    p.op("vector", lambda e: e.tensor_copy(bb[:], ps_tf[:]), inc=dv)

    M1, R1, M2, R2, SU, SS, SC, NG = range(8)
    dv_xn_hist = {}
    dv_evx_prev = 0
    dv_evz_prev = 0
    dv_ztf_prev = 0
    st_xn_prev = 0
    st_xnT_prev = 0
    st_zT_prev = 0
    st_wr_prev = 0
    ac_xnb_prev = 0
    ac_sq2_prev = 0
    ac_zb_prev = 0
    ac_rexp_prev = 0
    pe_tx_prev = 0
    pe_tz_prev = 0
    pe_ztf_prev = 0

    for i in range(NT):
        xti = xt[i % 2]
        if i >= 2:
            p.wait("sync", dv, dv_xn_hist[i - 2])
        p.dma("sync", xti[:], t_x[128 * i:128 * (i + 1), :], inc=ld)
        ld_i = ld.v

        p.wait("vector", ld, ld_i)
        p.fop("vector", lambda e, _x=xti: e.reduce_sum(stat[:, SU:SU + 1], _x[:], axis=AX.X), dv)
        p.wait("scalar", ld, ld_i)
        p.op("scalar", lambda e, _x=xti: e.activation(out=sqd[:], in_=_x[:], func=AF.Square,
                                                      accum_out=stat[:, SS:SS + 1]), inc=ac)
        p.wait("vector", ac)
        p.wait("vector", ac, ac_rexp_prev)
        p.op("vector", lambda e: e.tensor_scalar_mul(stat[:, M1:M1 + 1], stat[:, SU:SU + 1], 1.0 / D))
        p.op("vector", lambda e: e.tensor_scalar_mul(stat[:, SC:SC + 1], stat[:, SS:SS + 1], 1.0 / D))
        p.fop("vector", lambda e: e.tensor_scalar_mul(stat[:, NG:NG + 1], stat[:, M1:M1 + 1], -1.0), dv)
        p.op("vector", lambda e: e.scalar_tensor_tensor(stat[:, SC:SC + 1], stat[:, M1:M1 + 1],
                                                        stat[:, NG:NG + 1], stat[:, SC:SC + 1],
                                                        ALU.mult, ALU.add), inc=dv)
        p.wait("scalar", dv)
        p.op("scalar", lambda e: e.activation(out=stat[:, R1:R1 + 1], in_=stat[:, SC:SC + 1], func=AF.Ln,
                                              bias=stat[:, 15:16], scale=1.0), inc=ac)
        p.wait("scalar", ac)
        p.op("scalar", lambda e: e.activation(out=stat[:, R1:R1 + 1], in_=stat[:, R1:R1 + 1], func=AF.Exp,
                                              scale=-0.5), inc=ac)
        p.wait("vector", ac)
        p.wait("vector", st, st_xn_prev)
        p.wait("vector", ac, max(ac_xnb_prev, ac_sq2_prev))
        p.op("vector", lambda e, _x=xti: e.tensor_scalar(xnf[:], _x[:], stat[:, M1:M1 + 1], stat[:, R1:R1 + 1],
                                                         ALU.subtract, ALU.mult), inc=dv)
        dv_xn = dv.v
        dv_xn_hist[i] = dv_xn
        p.wait("gpsimd", dv, dv_xn)
        p.dma("gpsimd", t_xn[128 * i:128 * (i + 1), :], xnf[:], inc=st)
        st_xn_prev = st.v
        p.wait("scalar", dv, dv_xn)
        p.wait("scalar", pe, pe_tx_prev)
        p.op("scalar", lambda e: e.activation(out=xnb[:], in_=xnf[:], func=AF.Copy), inc=ac)
        ac_xnb = ac.v
        ac_xnb_prev = ac_xnb

        p.fop("vector", lambda e: e.reduce_sum(stat[:, SU:SU + 1], xnf[:], axis=AX.X), dv)
        p.op("vector", lambda e: e.tensor_scalar_mul(stat[:, M2:M2 + 1], stat[:, SU:SU + 1], 1.0 / D), inc=dv)
        p.wait("scalar", dv)
        p.op("scalar", lambda e: e.activation(out=sqd[:], in_=xnf[:], func=AF.Square,
                                              accum_out=stat[:, SS:SS + 1]), inc=ac)
        ac_sq2_prev = ac.v
        p.wait("vector", ac)
        p.op("vector", lambda e: e.tensor_scalar_mul(stat[:, SC:SC + 1], stat[:, SS:SS + 1], 1.0 / D))
        p.fop("vector", lambda e: e.tensor_scalar_mul(stat[:, NG:NG + 1], stat[:, M2:M2 + 1], -1.0), dv)
        p.op("vector", lambda e: e.scalar_tensor_tensor(stat[:, SC:SC + 1], stat[:, M2:M2 + 1],
                                                        stat[:, NG:NG + 1], stat[:, SC:SC + 1],
                                                        ALU.mult, ALU.add), inc=dv)
        p.wait("scalar", dv)
        p.op("scalar", lambda e: e.activation(out=stat[:, R2:R2 + 1], in_=stat[:, SC:SC + 1], func=AF.Ln,
                                              bias=stat[:, 15:16], scale=1.0), inc=ac)
        p.wait("scalar", ac)
        p.op("scalar", lambda e: e.activation(out=stat[:, R2:R2 + 1], in_=stat[:, R2:R2 + 1], func=AF.Exp,
                                              scale=-0.5), inc=ac)
        p.wait("vector", ac)
        p.wait("vector", ac, ac_zb_prev)
        p.wait("vector", pe, pe_ztf_prev)
        p.op("vector", lambda e: e.tensor_scalar(zf[:], xnf[:], stat[:, M2:M2 + 1], stat[:, R2:R2 + 1],
                                                 ALU.subtract, ALU.mult))
        p.op("vector", lambda e: e.tensor_mul(zf[:], zf[:], gb[:]))
        p.op("vector", lambda e: e.tensor_add(zf[:], zf[:], bb[:]), inc=dv)
        dv_z = dv.v
        p.wait("scalar", dv, dv_z)
        p.wait("scalar", pe, pe_tz_prev)
        p.op("scalar", lambda e: e.activation(out=zb[:], in_=zf[:], func=AF.Copy), inc=ac)
        ac_zb = ac.v
        ac_zb_prev = ac_zb

        p.wait("tensor", ac, ac_xnb)
        p.wait("tensor", dv, dv_evx_prev)
        for k in range(KC):
            p.op("tensor", lambda e, _k=k: e.transpose(ps_tb[:, 128 * _k:128 * (_k + 1)],
                                                       xnb[:, 128 * _k:128 * (_k + 1)], idb[:]),
                 inc=pe if k == KC - 1 else None)
        pe_tx_prev = pe.v
        p.wait("vector", pe)
        p.wait("vector", st, st_xnT_prev)
        p.op("vector", lambda e: e.tensor_copy(stgx[:], ps_tb[:]), inc=dv)
        dv_evx_prev = dv.v
        p.wait("gpsimd", dv, dv_evx_prev)
        p.dma("gpsimd", t_xnT[:, 1024 * i:1024 * (i + 1)], stgx[:], inc=st)
        st_xnT_prev = st.v

        p.wait("tensor", ac, ac_zb)
        p.wait("tensor", dv, dv_evz_prev)
        for k in range(KC):
            p.op("tensor", lambda e, _k=k: e.transpose(ps_tz[:, 128 * _k:128 * (_k + 1)],
                                                       zb[:, 128 * _k:128 * (_k + 1)], idb[:]),
                 inc=pe if k == KC - 1 else None)
        pe_tz_prev = pe.v
        p.wait("vector", pe)
        p.wait("vector", st, st_zT_prev)
        p.op("vector", lambda e: e.tensor_copy(stgz[:], ps_tz[:]), inc=dv)
        dv_evz_prev = dv.v
        p.wait("gpsimd", dv, dv_evz_prev)
        p.dma("gpsimd", t_zT[:, 1024 * i:1024 * (i + 1)], stgz[:], inc=st)
        st_zT_prev = st.v

        p.wait("tensor", dv, max(dv_z, dv_ztf_prev))
        for k in range(KC):
            p.op("tensor", lambda e, _k=k: e.transpose(ps_tf[:, 128 * _k:128 * (_k + 1)],
                                                       zf[:, 128 * _k:128 * (_k + 1)], idf[:]),
                 inc=pe if k == KC - 1 else None)
        pe_ztf_prev = pe.v
        p.wait("vector", pe)
        p.op("vector", lambda e: e.tensor_copy(ztf[:], ps_tf[:]), inc=dv)
        dv_ztf = dv.v
        dv_ztf_prev = dv.v

        p.wait("tensor", dv, dv_ztf)
        p.wait("tensor", ac, ac_rexp_prev)
        for k in range(KC):
            p.op("tensor", lambda e, _k=k: e.matmul(ps_r[:], ztf[:, 128 * _k:128 * (_k + 1)],
                                                    rw[:, 8 * _k:8 * (_k + 1)], start=(_k == 0), stop=False))
        p.op("tensor", lambda e: e.matmul(ps_r[:], onesf[:, 0:128], rbrow[:], start=False, stop=True), inc=pe)
        pe_r = pe.v
        p.wait("vector", pe, pe_r)
        p.fop("vector", lambda e: e.reduce_max(stat[:, SC:SC + 1], ps_r[:], axis=AX.X), dv)
        p.op("vector", lambda e: e.tensor_scalar_mul(stat[:, SC:SC + 1], stat[:, SC:SC + 1], -1.0), inc=dv)
        p.wait("scalar", dv)
        p.op("scalar", lambda e: e.activation(out=rt[:], in_=ps_r[:], func=AF.Exp,
                                              bias=stat[:, SC:SC + 1], scale=1.0), inc=ac)
        ac_rexp_prev = ac.v
        p.wait("vector", ac)
        p.fop("vector", lambda e: e.reduce_max(stat[:, SC:SC + 1], rt[:], axis=AX.X), dv)
        p.fop("vector", lambda e: e.tensor_scalar(rt2[:], rt[:], stat[:, SC:SC + 1], None, ALU.is_equal), dv)
        p.fop("vector", lambda e: e.scalar_tensor_tensor(rt2[:], rt2[:], -1e30, rt[:], ALU.mult, ALU.add), dv)
        p.fop("vector", lambda e: e.reduce_max(stat[:, SC:SC + 1], rt2[:], axis=AX.X), dv)
        p.fop("vector", lambda e: e.tensor_scalar(rt2[:], rt[:], stat[:, SC:SC + 1], None, ALU.is_ge), dv)
        p.wait("vector", st, st_wr_prev)
        p.fop("vector", lambda e: e.tensor_mul(wv[:], rt[:], rt2[:]), dv)
        p.fop("vector", lambda e: e.reduce_sum(stat[:, SC:SC + 1], wv[:], axis=AX.X), dv)
        p.fop("vector", lambda e: e.reciprocal(stat[:, SC:SC + 1], stat[:, SC:SC + 1]), dv)
        p.op("vector", lambda e: e.tensor_scalar_mul(wv[:], wv[:], stat[:, SC:SC + 1]), inc=dv)
        dv_w = dv.v
        p.wait("tensor", dv, dv_w)
        p.op("tensor", lambda e: e.transpose(ps_wT[:], wv[:], idf[:]), inc=pe)
        p.wait("vector", pe)
        p.op("vector", lambda e: e.tensor_copy(wvT[:], ps_wT[:]), inc=dv)
        dv_wT = dv.v
        p.wait("gpsimd", dv, dv_wT)
        p.dma("gpsimd", t_wr[128 * i:128 * (i + 1), :], wv[:], inc=st)
        p.dma("gpsimd", t_wrT[:, 128 * i:128 * (i + 1)], wvT[:], inc=st)
        st_wr_prev = st.v
        if "dbg" in io:
            p.op("vector", lambda e: e.tensor_copy(stat[:, 8:15], rt[:, 0:7]), inc=dv)
            p.wait("gpsimd", dv)
            p.dma("gpsimd", io["dbg"][i], stat[:], inc=st)

    p.wait("gpsimd", st)
    p.wait("gpsimd", ld)
    return st


def ap3(t, off, dims):
    row = t.shape[1]
    return bass.AP(t, off, [[row, t.shape[0]]] + dims)


def build_moe(p, io):
    nc = p.nc
    identf_c = nc.inline_tensor(np.eye(128, dtype=np.float32), name="moe_idf").ap()

    zTs = nc.alloc_sbuf_tensor("mo_zT", [128, NT * 1024], BF16)
    mixed = nc.alloc_sbuf_tensor("mo_mx", [128, NT * 1024], F32)
    wrs = nc.alloc_sbuf_tensor("mo_wr", [128, NT * 8], F32)
    wrTs = nc.alloc_sbuf_tensor("mo_wrT", [8, T], F32)
    b2s = nc.alloc_sbuf_tensor("mo_b2", [8, 1024], F32)
    b1s = nc.alloc_sbuf_tensor("mo_b1", [8, 2048], F32)
    b1T = nc.alloc_sbuf_tensor("mo_b1T", [128, 16 * 8], F32)
    idf = nc.alloc_sbuf_tensor("mo_idf", [128, 128], F32)
    W1s = nc.alloc_sbuf_tensor("mo_W1", [128, 8 * 2048], BF16)
    W2s = nc.alloc_sbuf_tensor("mo_W2", [128, 16 * 1024], BF16)
    hid = nc.alloc_sbuf_tensor("mo_hid", [128, 16 * 512], BF16)
    g0s = nc.alloc_sbuf_tensor("mo_g0", [128, 1024], BF16)

    ps_g1 = [nc.alloc_psum_tensor("mo_ps1a", [128, 512], F32),
             nc.alloc_psum_tensor("mo_ps1b", [128, 512], F32)]
    ps_g2 = [nc.alloc_psum_tensor("mo_ps2a", [128, 512], F32),
             nc.alloc_psum_tensor("mo_ps2b", [128, 512], F32)]
    ps_b2 = nc.alloc_psum_tensor("mo_psb2", [128, 512], F32)
    ps_t8 = nc.alloc_psum_tensor("mo_pst8", [128, 8], F32)

    ld = p.sem("mo_ld")
    lw = p.sem("mo_lw")
    pe = p.sem("mo_pe")
    dv = p.sem("mo_dv")
    ac = p.sem("mo_ac")
    st = p.sem("mo_st")

    p.dma("sync", zTs[:], io["zT"], inc=ld)
    p.dma("sync", mixed[:].rearrange("p (i c) -> p i c", i=NT),
          io["xn"].rearrange("(i p) c -> p i c", p=128), inc=ld)
    p.dma("sync", wrs[:].rearrange("p (i e) -> p i e", i=NT),
          io["wr"].rearrange("(i p) e -> p i e", p=128), inc=ld)
    p.dma("sync", wrTs[:], io["wrT"], inc=ld)
    p.dma("sync", b2s[:], io["b2"], inc=ld)
    p.dma("sync", b1s[:], io["b1"], inc=ld)
    p.dma("sync", idf[:], identf_c, inc=ld)
    ld0 = ld.v

    p.wait("tensor", ld, ld0)
    evprev = 0
    for t_ in range(16):
        p.wait("tensor", dv, evprev)
        p.op("tensor", lambda e, _t=t_: e.transpose(ps_t8[:, 0:8], b1s[:, 128 * _t:128 * (_t + 1)],
                                                    idf[0:8, 0:8]), inc=pe)
        p.wait("vector", pe)
        p.op("vector", lambda e, _t=t_: e.tensor_copy(b1T[:, 8 * _t:8 * (_t + 1)], ps_t8[:, 0:8]), inc=dv)
        evprev = dv.v

    pe_g1_done = 0
    pe_g2_done = 0
    acmark1 = {0: 0, 1: 0}
    dvmark2 = {0: 0, 1: 0}
    for e_ in range(8):
        p.wait("gpsimd", pe, pe_g1_done)
        p.dma("gpsimd", W1s[:].rearrange("p (c f) -> p c f", c=8),
              io["W1"][1024 * e_:1024 * (e_ + 1), :].rearrange("(c p) f -> p c f", p=128), inc=lw)
        lw_w1 = lw.v
        p.wait("gpsimd", pe, pe_g2_done)
        p.dma("gpsimd", W2s[:].rearrange("p (c f) -> p c f", c=16),
              io["W2"][2048 * e_:2048 * (e_ + 1), :].rearrange("(c p) f -> p c f", p=128), inc=lw)
        lw_e = lw.v
        p.wait("tensor", lw, lw_w1)
        p.wait("tensor", ld, ld0)
        for T2 in range(4):
            pe_gemm2_prev = pe_g2_done
            for hft in range(16):
                bk = hft % 2
                pb = ps_g1[bk]
                p.wait("tensor", ac, acmark1[bk])
                for c in range(KC):
                    p.op("tensor", lambda e, _c=c, _h=hft, _pb=pb, _T=T2: e.matmul(
                        _pb[:], W1s[:, _c * 2048 + 128 * _h:_c * 2048 + 128 * (_h + 1)],
                        ap3(zTs, 4096 * _T + 128 * _c, [[1024, 4], [1, 128]]),
                        start=(_c == 0), stop=(_c == KC - 1)), inc=pe if c == KC - 1 else None)
                pe_mm = pe.v
                p.wait("scalar", pe, pe_mm)
                p.wait("scalar", pe, pe_gemm2_prev)
                p.op("scalar", lambda e, _h=hft, _e=e_, _pb=pb: e.activation(
                    out=hid[:, 512 * _h:512 * (_h + 1)], in_=_pb[:], func=AF.Gelu,
                    bias=b1T[:, 8 * _h + _e:8 * _h + _e + 1], scale=1.0), inc=ac)
                acmark1[bk] = ac.v
            ac_hid_done = ac.v
            pe_g1_done = pe.v
            p.wait("tensor", ac, ac_hid_done)
            if T2 == 0:
                p.wait("tensor", lw, lw_e)
            for tt in range(4):
                i_glob = 4 * T2 + tt
                for nh in range(2):
                    bk = (2 * tt + nh) % 2
                    pb = ps_g2[bk]
                    p.wait("tensor", dv, dvmark2[bk])
                    for hk in range(16):
                        p.op("tensor", lambda e, _k=hk, _t=tt, _n=nh, _pb=pb: e.matmul(
                            _pb[:], hid[:, 512 * _k + 128 * _t:512 * _k + 128 * (_t + 1)],
                            W2s[:, 1024 * _k + 512 * _n:1024 * _k + 512 * (_n + 1)],
                            start=(_k == 0), stop=(_k == 15)), inc=pe if hk == 15 else None)
                    pe_mm2 = pe.v
                    p.wait("vector", pe, pe_mm2)
                    p.op("vector", lambda e, _i=i_glob, _n=nh, _e=e_, _pb=pb: e.scalar_tensor_tensor(
                        mixed[:, 1024 * _i + 512 * _n:1024 * _i + 512 * (_n + 1)], _pb[:],
                        wrs[:, 8 * _i + _e:8 * _i + _e + 1],
                        mixed[:, 1024 * _i + 512 * _n:1024 * _i + 512 * (_n + 1)],
                        ALU.mult, ALU.add), inc=dv)
                    dvmark2[bk] = dv.v
            pe_g2_done = pe.v

    dv_mx_done = dv.v
    st_prev = 0
    dvmark_b2 = 0
    p.wait("tensor", dv, dv_mx_done)
    for i in range(NT):
        for nh in range(2):
            p.wait("tensor", dv, dvmark_b2)
            p.op("tensor", lambda e, _i=i, _n=nh: e.matmul(
                ps_b2[:], wrTs[:, 128 * _i:128 * (_i + 1)], b2s[:, 512 * _n:512 * (_n + 1)],
                start=True, stop=True), inc=pe)
            pe_b2 = pe.v
            p.wait("vector", pe, pe_b2)
            p.op("vector", lambda e, _i=i, _n=nh: e.tensor_add(
                mixed[:, 1024 * _i + 512 * _n:1024 * _i + 512 * (_n + 1)],
                mixed[:, 1024 * _i + 512 * _n:1024 * _i + 512 * (_n + 1)], ps_b2[:]), inc=dv)
            dvmark_b2 = dv.v
        dv_i = dv.v
        p.wait("scalar", dv, dv_i)
        p.wait("scalar", st, st_prev)
        p.op("scalar", lambda e, _i=i: e.activation(out=g0s[:], in_=mixed[:, 1024 * _i:1024 * (_i + 1)],
                                                    func=AF.Silu), inc=ac)
        ac_i = ac.v
        p.wait("gpsimd", ac, ac_i)
        p.dma("gpsimd", io["g0"][128 * i:128 * (i + 1), :], g0s[:], inc=st)
        st_prev = st.v

    p.wait("gpsimd", st)
    p.wait("gpsimd", ld)
    p.wait("gpsimd", lw)


H0 = 16
SEG = 512
NSEG = 4


def build_attn0(p, io, nseg=NSEG, upto=99):
    nc = p.nc
    identb_c = nc.inline_tensor(np.eye(128, dtype=BF), name="a0_idbc").ap()
    id16_c = nc.inline_tensor(np.eye(16, dtype=np.float32), name="a0_id16c").ap()
    idf_c = nc.inline_tensor(np.eye(128, dtype=np.float32), name="a0_idfc").ap()
    onesb_c = nc.inline_tensor(np.ones((1, 128), dtype=BF), name="a0_onesbc").ap()

    xnTs = nc.alloc_sbuf_tensor("a0_xnT", [128, 4 * 1024], BF16)
    Wqs = nc.alloc_sbuf_tensor("a0_Wq", [128, 8 * 1024], BF16)
    Wks = nc.alloc_sbuf_tensor("a0_Wk", [128, 8 * 1024], BF16)
    Wvs = nc.alloc_sbuf_tensor("a0_Wv", [128, 8 * 1024], BF16)
    Wos = nc.alloc_sbuf_tensor("a0_Wo", [128, 8 * 1024], BF16)
    qh = nc.alloc_sbuf_tensor("a0_qh", [64, H0 * SEG], BF16)
    kh = nc.alloc_sbuf_tensor("a0_kh", [64, H0 * SEG], BF16)
    vh = nc.alloc_sbuf_tensor("a0_vh", [128, H0 * SEG], BF16)
    sqh = nc.alloc_sbuf_tensor("a0_sqh", [64, H0 * SEG], BF16)
    ekh = nc.alloc_sbuf_tensor("a0_ekh", [64, H0 * SEG], BF16)
    vtok = nc.alloc_sbuf_tensor("a0_vtok", [128, H0 * 4 * 64], BF16)
    sqtok = nc.alloc_sbuf_tensor("a0_sqtok", [128, H0 * 4 * 64], BF16)
    PT = [nc.alloc_sbuf_tensor("a0_PT0", [128, 4 * SEG], BF16),
          nc.alloc_sbuf_tensor("a0_PT1", [128, 4 * SEG], BF16)]
    mem = nc.alloc_sbuf_tensor("a0_mem", [64, H0 * 64], F32)
    memb = nc.alloc_sbuf_tensor("a0_memb", [64, 64], BF16)
    zcs = nc.alloc_sbuf_tensor("a0_zc", [64, H0], F32)
    zcseg = nc.alloc_sbuf_tensor("a0_zcseg", [64, H0], F32)
    gate = nc.alloc_sbuf_tensor("a0_gate", [64, H0], F32)
    acol = nc.alloc_sbuf_tensor("a0_acol", [64, H0], F32)
    acolT = nc.alloc_sbuf_tensor("a0_acolT", [16, 64], F32)
    gcT = nc.alloc_sbuf_tensor("a0_gcT", [16, 64], F32)
    G2all = nc.alloc_sbuf_tensor("a0_G2all", [128, H0 * 64], F32)
    G1sb = nc.alloc_sbuf_tensor("a0_G1sb", [128, 64], F32)
    ssum = nc.alloc_sbuf_tensor("a0_ssum", [128, H0 * 4], F32)
    rcol = nc.alloc_sbuf_tensor("a0_rcol", [128, H0 * 4], F32)
    mstat = nc.alloc_sbuf_tensor("a0_mst", [128, 16], F32)
    linv = nc.alloc_sbuf_tensor("a0_linv", [128, 4], F32)
    mstatb = nc.alloc_sbuf_tensor("a0_mstb", [128, 4], BF16)
    nrow = nc.alloc_sbuf_tensor("a0_nrow", [1, SEG], BF16)
    sedump = nc.alloc_sbuf_tensor("a0_sed", [128, 512], BF16)
    onesb = nc.alloc_sbuf_tensor("a0_onesb", [1, 128], BF16)
    idb = nc.alloc_sbuf_tensor("a0_idb", [128, 128], BF16)
    id16 = nc.alloc_sbuf_tensor("a0_id16", [16, 16], F32)
    idf = nc.alloc_sbuf_tensor("a0_idf", [128, 128], F32)
    bet = nc.alloc_sbuf_tensor("a0_bet", [16, 64], F32)
    t1s = nc.alloc_sbuf_tensor("a0_t1s", [128, 64], F32)
    t2s = nc.alloc_sbuf_tensor("a0_t2s", [128, 64], F32)
    atts = nc.alloc_sbuf_tensor("a0_atts", [128, 64], BF16)
    xs = nc.alloc_sbuf_tensor("a0_xs", [128, 1024], F32)
    g0s = nc.alloc_sbuf_tensor("a0_g0s", [128, 1024], BF16)
    x1s = nc.alloc_sbuf_tensor("a0_x1s", [128, 1024], F32)
    tmwo = nc.alloc_sbuf_tensor("a0_tmwo", [128, 512], F32)

    ps_sc = [nc.alloc_psum_tensor("a0_psA", [128, 512], F32),
             nc.alloc_psum_tensor("a0_psB", [128, 512], F32)]
    ps_an = nc.alloc_psum_tensor("a0_psAN", [128, 512], F32)
    ps_mem = nc.alloc_psum_tensor("a0_psM", [64, 64], F32)
    ps_tr = [nc.alloc_psum_tensor("a0_psT0", [128, 128], BF16),
             nc.alloc_psum_tensor("a0_psT1", [128, 128], BF16)]
    ps_wo = nc.alloc_psum_tensor("a0_psW", [128, 512], F32)
    ps_g = nc.alloc_psum_tensor("a0_psG", [128, 128], F32)

    ld = p.sem("a0_ld")
    pe = p.sem("a0_pe")
    dv = p.sem("a0_dv")
    ac = p.sem("a0_ac")
    gp = p.sem("a0_gp")
    st = p.sem("a0_st")

    for Wsb, key in ((Wqs, "Wq"), (Wks, "Wk"), (Wvs, "Wv"), (Wos, "Wo")):
        p.dma("gpsimd", Wsb[:].rearrange("p (c f) -> p c f", c=8),
              io[key].rearrange("(c p) f -> p c f", p=128), inc=ld)
    p.dma("sync", idb[:], identb_c, inc=ld)
    p.dma("sync", id16[:], id16_c, inc=ld)
    p.dma("sync", idf[:], idf_c, inc=ld)
    p.dma("sync", onesb[:], onesb_c, inc=ld)
    p.dma("sync", bet[:], io["betas"], inc=ld)
    ld0 = ld.v
    wc = None
    if "W_out" in io:
        wbs = [nc.alloc_sbuf_tensor("a0_wb%d" % k, [128, 1024], BF16) for k in range(2)]
        wc = p.sem("a0_wc")
        wt = p.sem("a0_wt")
        ws = p.sem("a0_ws")
        p.dma("gpsimd", io["scr_wout"], io["W_out"], inc=wc)
        p.dma("gpsimd", io["scr_win"], io["W_in"], inc=wc)

    p.wait("tensor", ld, ld0)
    p.op("tensor", lambda e: e.transpose(ps_g[0:64, 0:16], bet[:], id16[:]), inc=pe)
    p.wait("scalar", pe)
    p.op("scalar", lambda e: e.activation(out=gate[:], in_=ps_g[0:64, 0:16], func=AF.Sigmoid), inc=ac)
    p.wait("vector", ac)
    p.fop("vector", lambda e: e.tensor_scalar(acol[:], gate[:], -1.0, 1.0, ALU.mult, ALU.add), dv)
    p.wait("tensor", dv)
    p.op("tensor", lambda e: e.transpose(ps_g[0:16, 0:64], acol[:], idf[0:64, 0:64]), inc=pe)
    p.wait("vector", pe)
    p.op("vector", lambda e: e.tensor_copy(gcT[:], ps_g[0:16, 0:64]), inc=dv)
    dv_gcT = dv.v
    p.wait("tensor", dv, dv_gcT)
    pmark_g = 0
    for h in range(H0):
        p.wait("tensor", dv, pmark_g)
        p.op("tensor", lambda e, _h=h: e.matmul(ps_g[:, 0:64],
                                                bass.AP(id16, _h, [[16, 16], [0, 128]]),
                                                gcT[:], start=True, stop=True), inc=pe)
        p.wait("vector", pe)
        p.op("vector", lambda e, _h=h: e.tensor_copy(G2all[:, 64 * _h:64 * (_h + 1)], ps_g[:, 0:64]), inc=dv)
        pmark_g = dv.v
    p.op("vector", lambda e: e.memset(mem[:], 0.0))
    p.op("vector", lambda e: e.memset(zcs[:], 0.0), inc=dv)

    mk_sc = [(0, 0), (0, 0)]
    mk_sc_eng = ["both", "both"]
    mk_attd = 0
    mk_num = 0
    pt_rd = [0, 0]
    hs_acPT = [0, 0]
    hs_acl = [0, 0]
    mk_mem = 0
    mk_tr = [(0, "dv"), (0, "dv")]
    tr_i = 0
    mk_wo = 0
    mk_g1 = 0
    GEMS = ((Wqs, qh), (Wks, kh), (Wvs, vh))
    xnt_free_prev = 0

    for n in range(nseg):
        if n == 2 and wc is not None:
            p.wait("sync", wc, 16)
            for k in range(16):
                if k >= 2:
                    p.wait("sync", ws, 16 * (k - 1))
                p.dma("sync", wbs[k % 2][:], io["scr_wout"][:, 128 * k:128 * (k + 1)], transpose=True, inc=wt)
                p.wait("gpsimd", wt, 16 * (k + 1))
                p.dma("gpsimd", io["scr_woutT"][128 * k:128 * (k + 1), :], wbs[k % 2][:], inc=ws)
        pe_wo_prev = pe.v
        p.wait("sync", pe, xnt_free_prev)
        p.dma("sync", xnTs[:], io["xnT"][:, 4096 * n:4096 * (n + 1)], inc=ld)
        ld_seg = ld.v
        p.wait("tensor", ld, ld_seg)
        for gi, (Wsb, dst) in enumerate(GEMS):
            for ft in range(8):
                bk = ft % 2
                p.wait("tensor", dv, mk_sc[bk][0])
                p.wait("tensor", ac, mk_sc[bk][1])
                for c in range(KC):
                    p.op("tensor", lambda e, _c=c, _f=ft, _W=Wsb, _n=n, _b=bk: e.matmul(
                        ps_sc[_b][:], _W[:, _c * 1024 + 128 * _f:_c * 1024 + 128 * (_f + 1)],
                        ap3(xnTs, 128 * _c, [[1024, 4], [1, 128]]),
                        start=(_c == 0), stop=(_c == KC - 1)), inc=pe if c == KC - 1 else None)
                pe_mm = pe.v
                p.wait("vector", pe, pe_mm)
                p.op("vector", lambda e, _f=ft, _d=dst, _b2=bk: e.tensor_copy(
                    bass.AP(_d, 2 * _f, [[H0 * SEG, 64], [SEG, 16], [16, 32]]),
                    bass.AP(ps_sc[_b2], 0, [[512, 64], [32, 16], [1, 32]])), inc=dv)
                p.wait("scalar", pe, pe_mm)
                p.op("scalar", lambda e, _f=ft, _d=dst, _b2=bk: e.activation(
                    out=bass.AP(_d, 2 * _f + 1, [[H0 * SEG, 64], [SEG, 16], [16, 32]]),
                    in_=bass.AP(ps_sc[_b2], 64 * 512, [[512, 64], [32, 16], [1, 32]]),
                    func=AF.Copy), inc=ac)
                mk_sc[bk] = (dv.v, ac.v)
                mk_sc_eng[bk] = "both"
        xnt_free_prev = pe.v
        dv_qkv = dv.v
        ac_qkv = ac.v
        if upto < 2:
            if "dbg_qh" in io:
                p.wait("gpsimd", dv, dv_qkv)
                p.wait("gpsimd", ac, ac_qkv)
                p.dma("gpsimd", io["dbg_qh"], qh[:], inc=st)
                p.dma("gpsimd", io["dbg_kh"], kh[:], inc=st)
                p.dma("gpsimd", io["dbg_vh"], vh[0:64, :], inc=st)
            continue

        p.wait("vector", dv, dv_qkv)
        p.wait("vector", ac, ac_qkv)
        p.fop("vector", lambda e: e.tensor_scalar_min(sqh[:], kh[:], 0.0), dv)
        p.wait("scalar", dv)
        p.op("scalar", lambda e: e.activation(out=sqh[:], in_=sqh[:], func=AF.Exp), inc=ac)
        p.wait("vector", ac)
        p.fop("vector", lambda e: e.scalar_tensor_tensor(ekh[:], kh[:], 0.0, sqh[:], ALU.max, ALU.add), dv)
        dv_ek = dv.v
        p.fop("vector", lambda e: e.tensor_reduce(zcseg[:], bass.AP(ekh, 0, [[H0 * SEG, 64], [SEG, H0], [1, SEG]]),
                                                  axis=AX.X, op=ALU.add), dv)
        p.fop("vector", lambda e: e.tensor_add(zcs[:], zcs[:], zcseg[:]), dv)
        if upto < 3:
            if "dbg_qh" in io:
                p.wait("gpsimd", dv, dv.v)
                p.dma("gpsimd", io["dbg_qh"], sqh[:], inc=st)
                p.dma("gpsimd", io["dbg_kh"], ekh[:], inc=st)
            continue

        p.wait("tensor", dv, dv_qkv)
        p.wait("tensor", ac, ac_qkv)
        for h in range(H0):
            for lt in range(4):
                p.wait("tensor", ac if mk_tr[tr_i][1] == "ac" else dv, mk_tr[tr_i][0])
                p.op("tensor", lambda e, _h=h, _l=lt, _t=tr_i: e.transpose(
                    ps_tr[_t][:, 0:64], vh[0:64, SEG * _h + 128 * _l:SEG * _h + 128 * (_l + 1)],
                    bass.AP(idb, 0, [[128, 64], [1, 64]])), inc=pe)
                p.wait("scalar", pe)
                p.op("scalar", lambda e, _h=h, _l=lt, _t=tr_i: e.activation(
                    out=vtok[:, (4 * _h + _l) * 64:(4 * _h + _l + 1) * 64], in_=ps_tr[_t][:, 0:64],
                    func=AF.Copy), inc=ac)
                mk_tr[tr_i] = (ac.v, "ac")
                tr_i ^= 1
        p.fop("vector", lambda e: e.tensor_scalar_min(ekh[:], qh[:], 0.0), dv)
        p.wait("scalar", dv)
        p.op("scalar", lambda e: e.activation(out=ekh[:], in_=ekh[:], func=AF.Exp), inc=ac)
        p.wait("vector", ac)
        p.fop("vector", lambda e: e.scalar_tensor_tensor(sqh[:], qh[:], 0.0, ekh[:], ALU.max, ALU.add), dv)
        dv_elu = dv.v
        p.fop("vector", lambda e: e.reciprocal(acol[:], zcs[:]), dv)
        p.fop("vector", lambda e: e.tensor_mul(acol[:], acol[:], gate[:]), dv)
        dv_acol = dv.v
        p.wait("tensor", dv, dv_acol)
        p.op("tensor", lambda e: e.transpose(ps_g[0:16, 0:64], acol[:], idf[0:64, 0:64]), inc=pe)
        p.wait("vector", pe)
        p.fop("vector", lambda e: e.tensor_copy(acolT[:], ps_g[0:16, 0:64]), dv)
        dv_acolT = dv.v
        p.wait("tensor", dv, dv_elu)
        for h in range(H0):
            for lt in range(4):
                p.wait("tensor", ac if mk_tr[tr_i][1] == "ac" else dv, mk_tr[tr_i][0])
                p.op("tensor", lambda e, _h=h, _l=lt, _t=tr_i: e.transpose(
                    ps_tr[_t][:, 0:64], sqh[0:64, SEG * _h + 128 * _l:SEG * _h + 128 * (_l + 1)],
                    bass.AP(idb, 0, [[128, 64], [1, 64]])), inc=pe)
                p.wait("scalar", pe)
                p.op("scalar", lambda e, _h=h, _l=lt, _t=tr_i: e.activation(
                    out=sqtok[:, (4 * _h + _l) * 64:(4 * _h + _l + 1) * 64], in_=ps_tr[_t][:, 0:64],
                    func=AF.Copy, accum_out=ssum[:, 4 * _h + _l:4 * _h + _l + 1]), inc=ac)
                mk_tr[tr_i] = (ac.v, "ac")
                tr_i ^= 1
        ac_tok = ac.v
        p.wait("vector", ac, ac_tok)
        p.fop("vector", lambda e: e.reciprocal(rcol[:], ssum[:]), dv)

        if upto < 4:
            if "dbg_qh" in io:
                p.wait("gpsimd", ac, ac.v)
                p.dma("gpsimd", io["dbg_tok1"], sqtok[:], inc=st)
                p.dma("gpsimd", io["dbg_tok2"], vtok[:], inc=st)
            continue
        for hh in range(H0 + 1):
            if hh < H0:
                h = hh
                r = h % 2
                for s4 in range(4):
                    bk = s4 % 2
                    p.wait("tensor", dv, mk_sc[bk][0])
                    p.wait("tensor", ac, mk_sc[bk][1])
                    p.op("tensor", lambda e, _h=h, _s=s4, _b=bk: e.matmul(
                        ps_sc[_b][:], qh[:, SEG * _h + 128 * _s:SEG * _h + 128 * (_s + 1)],
                        kh[:, SEG * _h:SEG * (_h + 1)], start=True, stop=True), inc=pe)
                    pe_s = pe.v
                    p.wait("vector", pe, pe_s)
                    p.fop("vector", lambda e, _s=s4, _b=bk: e.reduce_max(mstat[:, _s:_s + 1], ps_sc[_b][:], axis=AX.X), dv)
                    p.fop("vector", lambda e, _s=s4: e.tensor_scalar_mul(mstat[:, 8 + _s:9 + _s], mstat[:, _s:_s + 1], -0.125), dv)
                    dv_b = dv.v
                    p.wait("scalar", dv, dv_b)
                    lc = (4 if r == 0 else 12) + s4
                    p.op("scalar", lambda e, _s=s4, _b=bk, _lc=lc: e.activation(
                        out=sedump[:], in_=ps_sc[_b][:], func=AF.Exp, bias=mstat[:, 8 + _s:9 + _s], scale=0.125,
                        accum_out=mstat[:, _lc:_lc + 1]), inc=ac)
                    mk_sc[bk] = (dv.v, ac.v)
                hs_acl[r] = ac.v
            if hh > 0:
                hp = hh - 1
                rp = hp % 2
                sv_ac_PT = hs_acPT[rp]
                p.wait("tensor", ac, sv_ac_PT)
                p.wait("tensor", dv, mk_attd)
                for s4 in range(4):
                    for t4 in range(4):
                        p.op("tensor", lambda e, _h=hp, _s=s4, _t=t4, _r=rp: e.matmul(
                            ps_an[:, 64 * _s:64 * (_s + 1)],
                            PT[_r][:, 512 * _t + 128 * _s:512 * _t + 128 * (_s + 1)],
                            vtok[:, (4 * _h + _t) * 64:(4 * _h + _t + 1) * 64],
                            start=(_t == 0), stop=(_t == 3)), inc=pe if (s4 == 3 and t4 == 3) else None)
                pe_attd = pe.v
                pt_rd[rp] = pe_attd
                p.wait("tensor", dv, mk_mem)
                for lt in range(4):
                    p.op("tensor", lambda e, _h=hp, _l=lt: e.matmul(
                        ps_mem[:], sqtok[:, (4 * _h + _l) * 64:(4 * _h + _l + 1) * 64],
                        vtok[:, (4 * _h + _l) * 64:(4 * _h + _l + 1) * 64],
                        start=(_l == 0), stop=(_l == 3)), inc=pe if lt == 3 else None)
                pe_mem = pe.v
                p.wait("vector", pe, pe_mem)
                p.fop("vector", lambda e, _h=hp: e.tensor_add(mem[:, 64 * _h:64 * (_h + 1)],
                                                              mem[:, 64 * _h:64 * (_h + 1)], ps_mem[:]), dv)
                p.fop("vector", lambda e, _h=hp: e.tensor_copy(memb[:], mem[:, 64 * _h:64 * (_h + 1)]), dv)
                dv_memb = dv.v
                mk_mem = dv_memb
                p.wait("tensor", dv, max(dv_acolT, mk_g1))
                p.op("tensor", lambda e, _h=hp: e.matmul(ps_g[:, 0:64],
                                                         bass.AP(id16, _h, [[16, 16], [0, 128]]),
                                                         acolT[:], start=True, stop=True), inc=pe)
                pe_g1 = pe.v
                p.wait("vector", pe, pe_g1)
                p.fop("vector", lambda e: e.tensor_copy(G1sb[:], ps_g[:, 0:64]), dv)
                mk_g1 = dv.v
            if hh < H0:
                p.fop("vector", lambda e: e.tensor_scalar_mul(mstatb[:], mstat[:, 8:12], 8.0), dv)
                dv_nm = dv.v
                for s4 in range(4):
                    p.wait("tensor", dv, dv_nm)
                    p.wait("tensor", ac if mk_tr[tr_i][1] == "ac" else dv, mk_tr[tr_i][0])
                    p.op("tensor", lambda e, _s=s4, _t=tr_i: e.transpose(
                        ps_tr[_t][0:1, 0:128], mstatb[:, _s:_s + 1], idb[:]), inc=pe)
                    p.wait("vector", pe)
                    p.op("vector", lambda e, _s=s4, _t=tr_i: e.tensor_copy(
                        nrow[0:1, 128 * _s:128 * (_s + 1)], ps_tr[_t][0:1, 0:128]), inc=dv)
                    mk_tr[tr_i] = (dv.v, "dv")
                    tr_i ^= 1
                dv_nrow = dv.v
                for t4 in range(4):
                    bk = t4 % 2
                    p.wait("tensor", dv, mk_sc[bk][0])
                    p.wait("tensor", ac, mk_sc[bk][1])
                    p.wait("tensor", dv, dv_nrow)
                    p.op("tensor", lambda e, _h=h, _t=t4, _b=bk: e.matmul(
                        ps_sc[_b][:], kh[:, SEG * _h + 128 * _t:SEG * _h + 128 * (_t + 1)],
                        qh[:, SEG * _h:SEG * (_h + 1)], start=True, stop=False))
                    p.op("tensor", lambda e, _b=bk: e.matmul(
                        ps_sc[_b][:], onesb[:], nrow[:], start=False, stop=True), inc=pe)
                    pe_s2 = pe.v
                    p.wait("scalar", pe, pe_s2)
                    p.wait("scalar", pe, pt_rd[r])
                    p.op("scalar", lambda e, _t=t4, _b=bk, _r=r: e.activation(
                        out=PT[_r][:, 512 * _t:512 * (_t + 1)], in_=ps_sc[_b][:], func=AF.Exp, scale=0.125), inc=ac)
                    mk_sc[bk] = (dv.v, ac.v)
                hs_acPT[r] = ac.v
                if "dbg_PT" in io and n == 0 and h == 0:
                    p.wait("gpsimd", ac, ac.v)
                    p.dma("gpsimd", io["dbg_PT"], PT[0][:], inc=st)
                    p.dma("gpsimd", io["dbg_mst"], mstat[:], inc=st)
                    p.dma("gpsimd", io["dbg_nrow"], nrow[:], inc=st)
            if hh > 0:
                p.wait("tensor", dv, dv_memb)
                p.wait("tensor", dv, mk_num)
                for s4 in range(4):
                    p.op("tensor", lambda e, _h=hp, _s=s4: e.matmul(
                        ps_an[:, 256 + 64 * _s:256 + 64 * (_s + 1)],
                        sqh[:, SEG * _h + 128 * _s:SEG * _h + 128 * (_s + 1)],
                        memb[:], start=True, stop=True), inc=pe if s4 == 3 else None)
                pe_num = pe.v
                p.wait("vector", pe, pe_attd)
                p.wait("vector", pe, pe_num)
                p.wait("vector", ac, hs_acl[rp])
                lc0 = 4 if rp == 0 else 12
                p.fop("vector", lambda e, _l=lc0: e.reciprocal(linv[:], mstat[:, _l:_l + 4]), dv)
                for s4 in range(4):
                    p.fop("vector", lambda e, _h=hp, _s=s4: e.scalar_tensor_tensor(
                        t1s[:], ps_an[:, 256 + 64 * _s:256 + 64 * (_s + 1)], rcol[:, 4 * _h + _s:4 * _h + _s + 1],
                        G1sb[:], ALU.mult, ALU.mult), dv)
                    p.fop("vector", lambda e, _h=hp, _s=s4: e.scalar_tensor_tensor(
                        t2s[:], ps_an[:, 64 * _s:64 * (_s + 1)], linv[:, _s:_s + 1],
                        G2all[:, 64 * _h:64 * (_h + 1)], ALU.mult, ALU.mult), dv)
                    p.fop("vector", lambda e: e.tensor_add(atts[:], t1s[:], t2s[:]), dv)
                    dv_att = dv.v
                    if "dbg_PT" in io and n == 0 and hp == 0 and s4 == 0:
                        dbst1 = nc.alloc_sbuf_tensor("a0_dbst1", [128, 64], F32)
                        dbst2 = nc.alloc_sbuf_tensor("a0_dbst2", [128, 64], F32)
                        p.op("vector", lambda e: e.tensor_copy(dbst1[:], ps_an[:, 0:64]))
                        p.op("vector", lambda e: e.tensor_copy(dbst2[:], ps_an[:, 256:320]), inc=dv)
                        p.wait("gpsimd", dv)
                        p.dma("gpsimd", io["dbg_t1"], t1s[:], inc=st)
                        p.dma("gpsimd", io["dbg_t2"], t2s[:], inc=st)
                        p.dma("gpsimd", io["dbg_attd"], dbst1[:], inc=st)
                        p.dma("gpsimd", io["dbg_num"], dbst2[:], inc=st)
                        p.dma("gpsimd", io["dbg_G1"], G1sb[:], inc=st)
                        p.dma("gpsimd", io["dbg_G2"], G2all[:, 0:64], inc=st)
                        p.dma("gpsimd", io["dbg_rc"], rcol[:], inc=st)
                        p.dma("gpsimd", io["dbg_atts"], atts[:], inc=st)
                        p.wait("vector", st, st.v)
                    p.wait("tensor", dv, dv_att)
                    p.wait("tensor", ac if mk_tr[tr_i][1] == "ac" else dv, mk_tr[tr_i][0])
                    p.wait("tensor", pe, pe_wo_prev)
                    p.op("tensor", lambda e, _t=tr_i: e.transpose(ps_tr[_t][0:64, :], atts[:], idb[:]), inc=pe)
                    p.wait("scalar", pe)
                    p.op("scalar", lambda e, _h=hp, _s=s4, _t=tr_i: e.activation(
                        out=vh[0:64, SEG * _h + 128 * _s:SEG * _h + 128 * (_s + 1)], in_=ps_tr[_t][0:64, :],
                        func=AF.Copy))
                    if s4 == 0:
                        p.op("scalar", lambda e, _h=hp, _t=tr_i: e.activation(
                            out=vh[64:128, SEG * _h:SEG * _h + 127], in_=ps_tr[_t][0:64, 1:128],
                            func=AF.Copy), inc=ac)
                    else:
                        p.op("scalar", lambda e, _h=hp, _s=s4, _t=tr_i: e.activation(
                            out=vh[64:128, SEG * _h + 128 * _s - 1:SEG * _h + 128 * (_s + 1) - 1],
                            in_=ps_tr[_t][0:64, :], func=AF.Copy), inc=ac)
                    mk_tr[tr_i] = (ac.v, "ac")
                    tr_i ^= 1
                mk_attd = dv.v
                mk_num = dv.v
        dv_attT = dv.v
        ac_attT = ac.v
        if upto < 7:
            if "dbg_qh" in io:
                p.wait("gpsimd", dv, dv_attT)
                p.wait("gpsimd", ac, ac_attT)
                p.dma("gpsimd", io["dbg_qh"], vh[0:64, :], inc=st)
            continue

        st_x1_prev = st.v
        for pt in range(4):
            i_glob = 4 * n + pt
            p.dma("sync", xs[:], io["x"][128 * i_glob:128 * (i_glob + 1), :], inc=ld)
            p.dma("sync", g0s[:], io["g0"][128 * i_glob:128 * (i_glob + 1), :], inc=ld)
            ld_t = ld.v
            for nh in range(2):
                p.wait("tensor", dv, max(dv_attT, mk_wo))
                p.wait("tensor", ac, ac_attT)
                for rr in range(8):
                    p.op("tensor", lambda e, _p=pt, _n=nh, _r=rr: e.matmul(
                        ps_wo[:], bass.AP(vh, SEG * 4 * _p + 2 * _r, [[H0 * SEG, 128], [SEG, 4], [16, 32]]),
                        Wos[:, 1024 * _r + 512 * _n:1024 * _r + 512 * (_n + 1)],
                        start=(_r == 0), stop=(_r == 7)), inc=pe if rr == 7 else None)
                pe_wo = pe.v
                p.wait("vector", pe, pe_wo)
                p.wait("vector", ld, ld_t)
                p.wait("vector", st, st_x1_prev)
                p.fop("vector", lambda e, _n=nh: e.tensor_mul(tmwo[:], ps_wo[:], g0s[:, 512 * _n:512 * (_n + 1)]), dv)
                p.op("vector", lambda e, _n=nh: e.tensor_add(x1s[:, 512 * _n:512 * (_n + 1)],
                                                             xs[:, 512 * _n:512 * (_n + 1)], tmwo[:]), inc=dv)
                mk_wo = dv.v
            dv_x1 = dv.v
            p.wait("gpsimd", dv, dv_x1)
            p.dma("gpsimd", io["x1"][128 * i_glob:128 * (i_glob + 1), :], x1s[:], inc=st)
            st_x1_prev = st.v
            p.wait("sync", dv, dv_x1)

    p.wait("gpsimd", st)
    p.wait("gpsimd", ld)
    if wc is not None:
        p.wait("gpsimd", wc, 32)
        p.wait("gpsimd", ws, 16 * 16)



NH = 32
DST = 128
HD = 64
DIN = 2048
QQ = 128
CH = 512
NCH = T // CH
OPK = ["ps0", "ps1", "psA", "psB"]
LNEPS = 1.0e-5
RMSEPS = 1.0e-5
A_FLOOR = 1.0e-4
TWO_PI = float(2.0 * np.pi)
INV_2PI = float(1.0 / (2.0 * np.pi))
MAGIC = 12582912.0


def build_l1a(p, io, nch=NCH, upto=99):
    nc = p.nc
    idb_c = nc.inline_tensor(np.eye(128, dtype=BF), name="l1_idbc").ap()
    idf_c = nc.inline_tensor(np.eye(128, dtype=np.float32), name="l1_idfc").ap()
    sel32_c = nc.inline_tensor(np.eye(32, dtype=np.float32), name="l1_sel32c").ap()
    _t1 = np.where(np.tril(np.ones((128, 128)), -1) > 0, -1e30, 0.0).astype(BF)
    _t2 = np.where(np.tril(np.ones((128, 128)), 0) > 0, -1e30, 0.0).astype(BF)
    tri1_c = nc.inline_tensor(_t1, name="l1_tri1c").ap()
    tri2_c = nc.inline_tensor(_t2, name="l1_tri2c").ap()
    sel32b_c = nc.inline_tensor(np.eye(32, dtype=BF), name="l1_sel32bc").ap()
    onesf_c = nc.inline_tensor(np.ones((1, 128), dtype=np.float32), name="l1_onesfc").ap()

    A = lambda *a, **k: nc.alloc_sbuf_tensor(*a, **k)
    WinT = A("l1_WinT", [128, 8 * 4480], BF16)
    xnTs = A("l1_xnT", [128, 4 * 1024], BF16)
    zxTs = A("l1_zx", [128, 32 * 512], BF16)
    xDg = A("l1_xDg", [128, 16 * 512], BF16)
    Xtok = A("l1_Xtok", [128, 2 * 2048], BF16)
    wot = [A("l1_wot0", [128, 1024], BF16), A("l1_wot1", [128, 1024], BF16)]
    dts = A("l1_dts", [32, 512], F32)
    ars = A("l1_ars", [32, 512], F32)
    trs = A("l1_trs", [32, 512], F32)
    rps = A("l1_rps", [32, 512], F32)
    dtv = A("l1_dtv", [32, 512], F32)
    gam = A("l1_gam", [32, 512], F32)
    adt = A("l1_adt", [32, 512], F32)
    ssc = A("l1_ssc", [32, 512], F32)
    lna = A("l1_lna", [32, 512], F32)
    lng = A("l1_lng", [32, 520], F32)
    angs = A("l1_angs", [32, 512], F32)
    tmp32 = A("l1_t32a", [32, 512], F32)
    angc = A("l1_angc", [32, 1], F32)
    scl1 = A("l1_scl1", [32, 8], F32)
    bct = A("l1_bct", [128, 4 * 256], BF16)
    bcd = A("l1_bcd", [128, 2 * 512], BF16)
    cw = [A("l1_cw0", [128, 128], BF16), A("l1_cw1", [128, 128], BF16)]
    bv = [A("l1_bv0", [128, 128], BF16), A("l1_bv1", [128, 128], BF16)]
    msk1 = [A("l1_msk1a", [128, 128], BF16), A("l1_msk1b", [128, 128], BF16)]
    msk2 = [A("l1_msk2a", [128, 128], BF16), A("l1_msk2b", [128, 128], BF16)]
    cmask = [A("l1_cmaska", [128, 128], BF16), A("l1_cmaskb", [128, 128], BF16)]
    mskd = [A("l1_mskda", [128, 128], BF16), A("l1_mskdb", [128, 128], BF16)]
    ypre = A("l1_ypre", [128, 128], BF16)
    h_sb = A("l1_h", [128, NH * 64], F32)
    h_bb = [A("l1_hbba", [128, 64], BF16), A("l1_hbbb", [128, 64], BF16)]
    v1T = A("l1_v1T", [128, 32], F32)
    v2T = A("l1_v2T", [128, 32], F32)
    expSm = A("l1_expSm", [32, 128], F32)
    expST = A("l1_expST", [128, 32], F32)
    estot = A("l1_estot", [128, 32], F32)
    esrow = A("l1_esrow", [1, 32], F32)
    wT = A("l1_wT", [128, 32], F32)
    Dbc = A("l1_Dbc", [128, 32], F32)
    nwe = {nm: A("l1_nw" + nm, [128, 32], F32) for nm in ("Be", "Bo", "Ce", "Co")}
    rowb = A("l1_rowb", [1, 128], BF16)
    xprow = A("l1_xprow", [1, 2048], BF16)
    xpS = A("l1_xpS", [1, 2048], BF16)
    xpS2 = A("l1_xpS2", [1, 2048], BF16)
    srow = A("l1_srow", [1, 32], F32)
    srow2 = A("l1_srow2", [1, 32], F32)
    drow = A("l1_drow", [1, 32], F32)
    ctk = A("l1_ctk", [128, 4 * 32], F32)
    stk = A("l1_stk", [128, 4 * 32], F32)
    rc = {nm: A("l1_rc" + nm, [128, 32], F32)
          for nm in ("cEB", "sOB", "sEB", "cOB", "cEC", "sOC", "sEC", "cOC")}
    bcst = A("l1_bcst", [128, 8], F32)
    vcol = [A("l1_vcola", [128, 1], F32), A("l1_vcolb", [128, 1], F32)]
    epscol = A("l1_eps", [128, 1], F32)
    onecol = A("l1_one", [128, 1], F32)
    nwcol = A("l1_nwc", [128, 1], F32)
    nwcol2 = A("l1_nwc2", [128, 1], F32)
    idb = A("l1_idb", [128, 128], BF16)
    idf = A("l1_idf", [128, 128], F32)
    sel32 = A("l1_sel32", [32, 32], F32)
    tri1 = A("l1_tri1", [128, 128], BF16)
    tri2 = A("l1_tri2", [128, 128], BF16)
    sel32b = A("l1_sel32b", [32, 32], BF16)
    stil = A("l1_stil", [32, 128], F32)
    shib = A("l1_shib", [32, 128], BF16)
    slob = A("l1_slob", [32, 128], BF16)
    onesf = A("l1_onesf", [1, 128], F32)
    dtbc = A("l1_dtbc", [32, 1], F32)
    dsk = A("l1_dsk", [32, 1], F32)
    lnx = A("l1_lnx", [128, 1024], F32)
    lnst = A("l1_lnst", [128, 8], F32)
    bcf = lnx
    dmp = A("l1_dmp", [128, 512], F32)
    xnbh = A("l1_xnbh", [128, 512], BF16)
    dmpb = A("l1_dmpb", [128, 512], BF16)
    dmpb2 = A("l1_dmpb2", [128, 512], BF16)

    ps0 = nc.alloc_psum_tensor("l1_ps0", [128, 512], F32)
    ps1 = nc.alloc_psum_tensor("l1_ps1", [128, 512], F32)
    psA = nc.alloc_psum_tensor("l1_psA", [128, 512], F32)
    psB = nc.alloc_psum_tensor("l1_psB", [128, 512], F32)
    psS = nc.alloc_psum_tensor("l1_psS", [128, 128], F32)
    psY0 = nc.alloc_psum_tensor("l1_psY0", [128, 128], F32)
    psY1 = nc.alloc_psum_tensor("l1_psY1", [128, 128], F32)
    ps_trb = nc.alloc_psum_tensor("l1_pstrb", [128, 128], BF16)
    ps_gf = psA
    pss = {"ps0": ps0, "ps1": ps1, "psA": psA, "psB": psB,
           "psS": psS, "psY0": psY0, "psY1": psY1, "ps_trb": ps_trb}

    ld = p.sem("l1_ld")
    lw = p.sem("l1_lw")
    pe = p.sem("l1_pe")
    dv = p.sem("l1_dv")
    ac = p.sem("l1_ac")
    gp = p.sem("l1_gp")
    st = p.sem("l1_st")

    mk = {k: (0, dv) for k in pss}
    mk["ps_gf"] = (0, dv)

    def pwait(eng, key):
        v, s = mk[key]
        p.wait(eng, s, v)

    def pset(key, sem):
        mk[key] = (sem.v, sem)

    p.dma("sync", idb[:], idb_c, inc=ld)
    p.dma("sync", idf[:], idf_c, inc=ld)
    p.dma("sync", sel32[:], sel32_c, inc=ld)
    p.dma("sync", tri1[:], tri1_c, inc=ld)
    p.dma("sync", tri2[:], tri2_c, inc=ld)
    p.dma("sync", sel32b[:], sel32b_c, inc=ld)
    p.dma("sync", onesf[:], onesf_c, inc=ld)
    p.dma("sync", dtbc[:], io["dt_bias"][:, None], inc=ld)
    p.dma("sync", dsk[:], io["D_skip"][:, None], inc=ld)
    for c in range(8):
        p.dma("sync", WinT[:, 4480 * c:4480 * (c + 1)], io["scr_win"][:, 128 * c:128 * (c + 1)],
              transpose=True, inc=lw)

    if "dbg_o" in io:
        p.wait("gpsimd", st, st.v)
        p.dma("gpsimd", io["dbg_o"], io["scr_woutT"][0:128, :], inc=st)
    p.wait("tensor", ld, ld.v)
    p.op("tensor", lambda e: e.transpose(ps_gf[0:1, 0:32], dsk[:], idf[0:32, 0:32]), inc=pe)
    p.wait("vector", pe)
    p.fop("vector", lambda e: e.tensor_copy(drow[:], ps_gf[0:1, 0:32]), dv)
    p.wait("tensor", dv)
    p.op("tensor", lambda e: e.matmul(ps_gf[:, 0:32], onesf[:], drow[:], start=True, stop=True), inc=pe)
    p.wait("vector", pe)
    p.fop("vector", lambda e: e.tensor_copy(Dbc[:], ps_gf[:, 0:32]), dv)
    for nm, src_ap in (("Be", io["B_norm_w"][0:64:2][None, :]), ("Bo", io["B_norm_w"][1:64:2][None, :]),
                       ("Ce", io["C_norm_w"][0:64:2][None, :]), ("Co", io["C_norm_w"][1:64:2][None, :])):
        p.wait("sync", pe, pe.v)
        p.dma("sync", drow[:], src_ap, inc=ld, allow_slow_non_contiguous=True)
        p.wait("tensor", ld, ld.v)
        p.wait("tensor", dv, dv.v)
        p.op("tensor", lambda e: e.matmul(ps_gf[:, 0:32], onesf[:], drow[:], start=True, stop=True), inc=pe)
        p.wait("vector", pe)
        p.fop("vector", lambda e, _n=nm: e.tensor_copy(nwe[_n][:], ps_gf[:, 0:32]), dv)
    p.dma("sync", ssc[0:32, 0:128], io["B_bias"], inc=ld)
    p.dma("sync", lna[0:32, 0:128], io["C_bias"], inc=ld)
    p.dma("sync", nwcol[:], io["B_norm_w"][:, None], inc=ld)
    p.dma("sync", nwcol2[:], io["C_norm_w"][:, None], inc=ld)
    p.wait("vector", ld, ld.v)
    p.fop("vector", lambda e: e.tensor_mul(ssc[0:32, 0:128], ssc[0:32, 0:128], lna[0:32, 0:128]), dv)
    p.wait("tensor", dv)
    p.op("tensor", lambda e: e.transpose(ps_gf[:, 0:32], ssc[0:32, 0:128], idf[0:32, 0:32]), inc=pe)
    p.wait("vector", pe)
    p.fop("vector", lambda e: e.tensor_copy(wT[:], ps_gf[:, 0:32]), dv)
    p.fop("vector", lambda e: e.tensor_mul(nwcol[:], nwcol[:], nwcol2[:]), dv)
    p.op("vector", lambda e: e.memset(nwcol[0:64, :], 1.0), inc=dv)
    p.wait("vector", dv)
    p.fop("vector", lambda e: e.tensor_scalar_mul(wT[:], wT[:], nwcol[:]), dv)
    p.op("vector", lambda e: e.memset(epscol[:], RMSEPS), inc=dv)
    p.op("vector", lambda e: e.memset(onecol[:], 1.0), inc=dv)
    p.op("vector", lambda e: e.memset(h_sb[:], 0.0), inc=dv)
    p.op("vector", lambda e: e.memset(angc[:], 0.0), inc=dv)
    p.op("vector", lambda e: e.memset(xprow[:], 0.0), inc=dv)
    p.op("vector", lambda e: e.memset(rowb[:], 0.0), inc=dv)
    p.op("vector", lambda e: e.memset(lng[0:32, 512:520], -30.0), inc=dv)
    pset("psA", dv)

    M1, R1, SU, SS, SC, NG, EPC = 0, 1, 2, 3, 4, 5, 6
    p.op("vector", lambda e: e.memset(lnst[:, EPC:EPC + 1], LNEPS), inc=dv)
    st_ln = {0: 0, 1: 0}
    dv_lnx_prev = 0
    ac_lnx_prev = 0
    pe_xnbh_prev = 0
    for i in range(NT):
        p.wait("sync", dv, dv_lnx_prev)
        p.wait("sync", ac, ac_lnx_prev)
        p.dma("sync", lnx[:], io["x1"][128 * i:128 * (i + 1), :], inc=ld)
        ld_i = ld.v
        p.wait("vector", ld, ld_i)
        p.fop("vector", lambda e: e.reduce_sum(lnst[:, SU:SU + 1], lnx[:], axis=AX.X), dv)
        p.wait("scalar", ld, ld_i)
        p.wait("scalar", dv, dv.v)
        p.op("scalar", lambda e: e.activation(out=dmp[:], in_=lnx[:, 0:512], func=AF.Square,
                                              accum_out=lnst[:, SS:SS + 1]))
        p.op("scalar", lambda e: e.activation(out=dmp[:], in_=lnx[:, 512:1024], func=AF.Square,
                                              accum_out=lnst[:, 7:8]), inc=ac)
        ac_lnx_prev = ac.v
        p.wait("vector", ac)
        p.op("vector", lambda e: e.tensor_scalar_mul(lnst[:, M1:M1 + 1], lnst[:, SU:SU + 1], 1.0 / D))
        p.fop("vector", lambda e: e.tensor_add(lnst[:, SC:SC + 1], lnst[:, SS:SS + 1], lnst[:, 7:8]), dv)
        p.op("vector", lambda e: e.tensor_scalar_mul(lnst[:, SC:SC + 1], lnst[:, SC:SC + 1], 1.0 / D))
        p.fop("vector", lambda e: e.tensor_scalar_mul(lnst[:, NG:NG + 1], lnst[:, M1:M1 + 1], -1.0), dv)
        p.fop("vector", lambda e: e.scalar_tensor_tensor(lnst[:, SC:SC + 1], lnst[:, M1:M1 + 1],
                                                         lnst[:, NG:NG + 1], lnst[:, SC:SC + 1],
                                                         ALU.mult, ALU.add), dv)
        p.wait("scalar", dv)
        p.op("scalar", lambda e: e.activation(out=lnst[:, R1:R1 + 1], in_=lnst[:, SC:SC + 1], func=AF.Ln,
                                              bias=lnst[:, EPC:EPC + 1], scale=1.0), inc=ac)
        p.wait("scalar", ac)
        p.op("scalar", lambda e: e.activation(out=lnst[:, R1:R1 + 1], in_=lnst[:, R1:R1 + 1], func=AF.Exp,
                                              scale=-0.5), inc=ac)
        p.wait("vector", ac)
        for half in range(2):
            db = (dmpb, dmpb2)[half]
            p.wait("vector", pe, pe_xnbh_prev)
            p.fop("vector", lambda e, _h=half: e.tensor_scalar(
                xnbh[:], lnx[:, 512 * _h:512 * (_h + 1)], lnst[:, M1:M1 + 1], lnst[:, R1:R1 + 1],
                ALU.subtract, ALU.mult), dv)
            dv_xn = dv.v
            if half == 1:
                dv_lnx_prev = dv_xn
            p.wait("tensor", dv, dv_xn)
            for k in range(4):
                pwait("tensor", "ps_trb")
                p.op("tensor", lambda e, _k=k: e.transpose(ps_trb[:, 0:128], xnbh[:, 128 * _k:128 * (_k + 1)],
                                                            idb[:]), inc=pe)
                p.wait("vector", pe)
                p.wait("vector", st, st_ln[half])
                p.op("vector", lambda e, _k=k, _d=db: e.tensor_copy(_d[:, 128 * _k:128 * (_k + 1)], ps_trb[:, 0:128]),
                     inc=dv)
                pset("ps_trb", dv)
            pe_xnbh_prev = pe.v
            p.wait("gpsimd", dv, dv.v)
            p.dma("gpsimd", io["xn1T"][:, 1024 * i + 512 * half:1024 * i + 512 * (half + 1)], db[:], inc=st)
            st_ln[half] = st.v
    if upto < 1:
        p.wait("gpsimd", st)
        return

    lw_wt = lw.v
    p.wait("tensor", lw, lw_wt)
    dv_gated_prev = 0
    pe_op_prev = 0
    gp_xd_prev = 0
    pemark_wb = {0: 0, 1: 0}
    mark_cw = [0, 0]
    mark_msk = [0, 0]
    mark_cmask = [0, 0]
    mark_mskd = [0, 0]
    mark_hbb = [0, 0]
    mark_bv = [0, 0]
    mark_vcol = [0, 0]
    mark_ypre = 0
    mk_xtok = {0: (0, dv), 1: (0, dv)}
    xtok_ev = {0: (dv, 0), 1: (dv, 0)}
    ld_carry = ld.v
    for ci in range(nch):
        p.wait("sync", pe, pe.v)
        p.dma("sync", xnTs[:], io["xn1T"][:, 4096 * ci:4096 * (ci + 1)], inc=ld)
        ld_ch = ld.v

        p.wait("tensor", ld, ld_ch)
        for mt in range(32):
            key = "ps0" if mt % 2 == 0 else "ps1"
            pb = pss[key]
            pwait("tensor", key)
            for c in range(KC):
                p.op("tensor", lambda e, _c=c, _m=mt, _pb=pb: e.matmul(
                    _pb[:], WinT[:, 4480 * _c + 128 * _m:4480 * _c + 128 * (_m + 1)],
                    ap3(xnTs, 128 * _c, [[1024, 4], [1, 128]]),
                    start=(_c == 0), stop=(_c == KC - 1)), inc=pe if c == KC - 1 else None)
            pe_mm = pe.v
            p.wait("scalar", pe, pe_mm)
            p.wait("scalar", dv, dv_gated_prev)
            p.wait("scalar", gp, gp_xd_prev)
            p.op("scalar", lambda e, _m=mt, _pb=pb: e.activation(
                out=zxTs[:, 512 * _m:512 * (_m + 1)], in_=_pb[:],
                func=(AF.Silu if _m < 16 else AF.Copy)), inc=ac)
            mk[key] = (ac.v, ac)
        ac_zx = ac.v

        for j, dst in ((0, dts), (1, ars), (2, trs), (3, rps)):
            key = "psA" if j % 2 == 0 else "psB"
            pb = pss[key]
            pwait("tensor", key)
            for c in range(KC):
                p.op("tensor", lambda e, _c=c, _j=j, _pb=pb: e.matmul(
                    _pb[0:32, :], WinT[:, 4480 * _c + 4352 + 32 * _j:4480 * _c + 4352 + 32 * (_j + 1)],
                    ap3(xnTs, 128 * _c, [[1024, 4], [1, 128]]),
                    start=(_c == 0), stop=(_c == KC - 1)), inc=pe if c == KC - 1 else None)
            pe_mm = pe.v
            p.wait("vector", pe, pe_mm)
            p.fop("vector", lambda e, _d=dst, _pb=pb: e.tensor_copy(_d[:], _pb[0:32, :]), dv)
            mk[key] = (dv.v, dv)

        p.fop("vector", lambda e: e.tensor_scalar(tmp32[:], dts[:], dtbc[:], None, ALU.add), dv)
        p.fop("vector", lambda e: e.scalar_tensor_tensor(gam[:], tmp32[:], -1.0, tmp32[:], ALU.mult, ALU.max), dv)
        p.wait("scalar", dv)
        p.op("scalar", lambda e: e.activation(out=gam[:], in_=gam[:], func=AF.Exp, scale=-1.0), inc=ac)
        p.wait("scalar", ac)
        p.op("scalar", lambda e: e.activation(out=gam[:], in_=gam[:], func=AF.Ln,
                                              bias=onecol[0:32, 0:1], scale=1.0), inc=ac)
        p.wait("vector", ac)
        p.fop("vector", lambda e: e.scalar_tensor_tensor(dtv[:], tmp32[:], 0.0, gam[:], ALU.max, ALU.add), dv)
        p.fop("vector", lambda e: e.scalar_tensor_tensor(gam[:], ars[:], -1.0, ars[:], ALU.mult, ALU.max), dv)
        p.wait("scalar", dv)
        p.op("scalar", lambda e: e.activation(out=gam[:], in_=gam[:], func=AF.Exp, scale=-1.0), inc=ac)
        p.wait("scalar", ac)
        p.op("scalar", lambda e: e.activation(out=gam[:], in_=gam[:], func=AF.Ln,
                                              bias=onecol[0:32, 0:1], scale=1.0), inc=ac)
        p.wait("vector", ac)
        p.fop("vector", lambda e: e.scalar_tensor_tensor(gam[:], ars[:], 0.0, gam[:], ALU.max, ALU.add), dv)
        p.fop("vector", lambda e: e.tensor_scalar_add(gam[:], gam[:], A_FLOOR), dv)
        p.fop("vector", lambda e: e.tensor_mul(adt[:], dtv[:], gam[:]), dv)
        p.fop("vector", lambda e: e.tensor_scalar_mul(adt[:], adt[:], -1.0), dv)
        p.op("vector", lambda e: e.memset(tmp32[:], 1.0), inc=dv)
        p.wait("vector", dv)
        p.fop("vector", lambda e: e.tensor_tensor_scan(ssc[:], tmp32[:], adt[:], 0.0, ALU.mult, ALU.add), dv)
        p.wait("scalar", dv)
        p.op("scalar", lambda e: e.activation(out=trs[:], in_=trs[:], func=AF.Sigmoid), inc=ac)
        p.wait("vector", ac)
        p.fop("vector", lambda e: e.tensor_scalar(tmp32[:], trs[:], -0.5, 1.0, ALU.mult, ALU.add), dv)
        p.fop("vector", lambda e: e.tensor_mul(tmp32[:], tmp32[:], dtv[:]), dv)
        p.wait("scalar", dv)
        p.op("scalar", lambda e: e.activation(out=lna[:, 0:512], in_=tmp32[:], func=AF.Ln), inc=ac)
        p.wait("vector", ac)
        p.fop("vector", lambda e: e.tensor_mul(gam[:], trs[:], dtv[:]), dv)
        p.fop("vector", lambda e: e.tensor_scalar_mul(gam[:], gam[:], 0.5), dv)
        p.wait("scalar", dv)
        p.op("scalar", lambda e: e.activation(out=lng[:, 0:512], in_=gam[:], func=AF.Ln), inc=ac)
        p.wait("vector", ac)
        p.fop("vector", lambda e: e.tensor_sub(lna[:, 0:512], lna[:, 0:512], ssc[:]), dv)
        p.fop("vector", lambda e: e.tensor_sub(lng[:, 0:512], lng[:, 0:512], ssc[:]), dv)
        p.op("vector", lambda e: e.memset(tmp32[:], 1.0), inc=dv)
        p.wait("vector", dv)
        p.fop("vector", lambda e: e.tensor_tensor_scan(angs[:], tmp32[:], rps[:], angc[:], ALU.mult, ALU.add), dv)
        p.fop("vector", lambda e: e.tensor_copy(angc[:], angs[:, 511:512]), dv)
        p.fop("vector", lambda e: e.tensor_scalar_mul(tmp32[:], angs[:], INV_2PI), dv)
        p.fop("vector", lambda e: e.tensor_scalar(dtv[:], tmp32[:], MAGIC, MAGIC, ALU.add, ALU.subtract), dv)
        p.fop("vector", lambda e: e.tensor_sub(dtv[:], tmp32[:], dtv[:]), dv)
        p.fop("vector", lambda e: e.tensor_scalar_mul(dtv[:], dtv[:], TWO_PI), dv)
        p.wait("scalar", dv)
        p.op("scalar", lambda e: e.activation(out=dts[:], in_=dtv[:], func=AF.Sin), inc=ac)
        p.fop("vector", lambda e: e.tensor_scalar_add(tmp32[:], tmp32[:], 0.25), dv)
        p.fop("vector", lambda e: e.tensor_scalar(dtv[:], tmp32[:], MAGIC, MAGIC, ALU.add, ALU.subtract), dv)
        p.fop("vector", lambda e: e.tensor_sub(dtv[:], tmp32[:], dtv[:]), dv)
        p.fop("vector", lambda e: e.tensor_scalar_mul(dtv[:], dtv[:], TWO_PI), dv)
        p.wait("scalar", dv)
        p.op("scalar", lambda e: e.activation(out=ars[:], in_=dtv[:], func=AF.Sin), inc=ac)
        ac_cs = ac.v

        p.wait("tensor", ac, ac_cs)
        p.wait("vector", ld, ld_carry)
        for tt in range(4):
            for dst_, src_ in ((ctk, ars), (stk, dts)):
                pwait("tensor", "psA")
                p.op("tensor", lambda e, _t=tt, _s=src_: e.transpose(
                    ps_gf[:, 0:32], _s[:, 128 * _t:128 * (_t + 1)], idf[0:32, 0:32]), inc=pe)
                p.wait("vector", pe)
                p.fop("vector", lambda e, _t=tt, _d=dst_: e.tensor_copy(_d[:, 32 * _t:32 * (_t + 1)],
                                                                        ps_gf[:, 0:32]), dv)
                pset("psA", dv)
            key = "psA" if tt % 2 == 0 else "psB"
            pb = pss[key]
            pwait("tensor", key)
            for c in range(KC):
                p.op("tensor", lambda e, _c=c, _t=tt, _pb=pb: e.matmul(
                    _pb[:, 0:256], xnTs[:, 1024 * _t + 128 * _c:1024 * _t + 128 * (_c + 1)],
                    WinT[:, 4480 * _c + 4096:4480 * _c + 4352],
                    start=(_c == 0), stop=(_c == KC - 1)), inc=pe if c == KC - 1 else None)
            pe_bc = pe.v
            p.wait("scalar", pe, pe_bc)
            p.op("scalar", lambda e, _pb=pb: e.activation(out=dmp[:, 0:128], in_=_pb[:, 0:128],
                                                          func=AF.Square, accum_out=bcst[:, 0:1]))
            p.op("scalar", lambda e, _pb=pb: e.activation(out=dmp[:, 128:256], in_=_pb[:, 128:256],
                                                          func=AF.Square, accum_out=bcst[:, 1:2]), inc=ac)
            p.wait("vector", ac)
            p.fop("vector", lambda e: e.tensor_scalar_mul(bcst[:, 2:4], bcst[:, 0:2], 1.0 / 128.0), dv)
            p.wait("scalar", dv)
            p.op("scalar", lambda e: e.activation(out=bcst[:, 2:4], in_=bcst[:, 2:4], func=AF.Ln,
                                                  bias=epscol[:, 0:1], scale=1.0), inc=ac)
            p.wait("scalar", ac)
            p.op("scalar", lambda e: e.activation(out=bcst[:, 2:4], in_=bcst[:, 2:4], func=AF.Exp,
                                                  scale=-0.5), inc=ac)
            p.wait("vector", ac)
            p.fop("vector", lambda e, _pb=pb: e.tensor_scalar_mul(bcf[:, 0:128], _pb[:, 0:128],
                                                                  bcst[:, 2:3]), dv)
            p.fop("vector", lambda e, _pb=pb: e.tensor_scalar_mul(bcf[:, 128:256], _pb[:, 128:256],
                                                                  bcst[:, 3:4]), dv)
            mk[key] = (dv.v, dv)
            for dstc, a_, b_ in (("cEB", ctk, "Be"), ("sOB", stk, "Bo"), ("sEB", stk, "Be"), ("cOB", ctk, "Bo"),
                                 ("cEC", ctk, "Ce"), ("sOC", stk, "Co"), ("sEC", stk, "Ce"), ("cOC", ctk, "Co")):
                p.op("vector", lambda e, _d=dstc, _a=a_, _b=b_, _t=tt: e.tensor_mul(
                    rc[_d][:], _a[:, 32 * _t:32 * (_t + 1)], nwe[_b][:]))
            p.fop("vector", lambda e: e.engine_nop(), dv)
            for half, (ce, so, se, co) in ((0, ("cEB", "sOB", "sEB", "cOB")), (1, ("cEC", "sOC", "sEC", "cOC"))):
                base = 128 * half
                ev = bass.AP(bcf, base, [[1024, 128], [2, 32]])
                od = bass.AP(bcf, base + 1, [[1024, 128], [2, 32]])
                t_ev = bass.AP(bct, 256 * tt + base, [[4 * 256, 128], [2, 32]])
                t_od = bass.AP(bct, 256 * tt + base + 1, [[4 * 256, 128], [2, 32]])
                p.fop("vector", lambda e, _e=ev, _c=ce: e.tensor_mul(dmp[:, 0:32], _e, rc[_c][:]), dv)
                p.fop("vector", lambda e, _o=od, _s=so: e.tensor_mul(dmp[:, 32:64], _o, rc[_s][:]), dv)
                p.fop("vector", lambda e, _t=t_ev: e.tensor_sub(_t, dmp[:, 0:32], dmp[:, 32:64]), dv)
                p.fop("vector", lambda e, _e=ev, _s=se: e.tensor_mul(dmp[:, 0:32], _e, rc[_s][:]), dv)
                p.fop("vector", lambda e, _o=od, _c=co: e.tensor_mul(dmp[:, 32:64], _o, rc[_c][:]), dv)
                p.fop("vector", lambda e, _t=t_od: e.tensor_add(_t, dmp[:, 0:32], dmp[:, 32:64]), dv)
                p.fop("vector", lambda e, _t=tt, _b=base: e.tensor_copy(
                    bct[:, 256 * _t + _b + 64:256 * _t + _b + 128], bcf[:, _b + 64:_b + 128]), dv)
            dv_bct = dv.v
            for half in range(2):
                pwait("tensor", "ps_trb")
                p.wait("tensor", dv, dv_bct)
                p.op("tensor", lambda e, _t=tt, _h=half: e.transpose(
                    ps_trb[:, 0:128], bct[:, 256 * _t + 128 * _h:256 * _t + 128 * (_h + 1)], idb[:]), inc=pe)
                p.wait("vector", pe)
                p.op("vector", lambda e, _t=tt, _h=half: e.tensor_copy(
                    bcd[:, 512 * _h + 128 * _t:512 * _h + 128 * (_t + 1)], ps_trb[:, 0:128]), inc=dv)
                pset("ps_trb", dv)
            ht = tt // 2
            if tt % 2 == 0:
                v0h, s0h = mk_xtok[ht]
                p.wait("tensor", s0h, v0h)
            for it in range(8 * (tt % 2), 8 * (tt % 2) + 8):
                pwait("tensor", "ps_trb")
                p.op("tensor", lambda e, _t=ht, _i=it: e.transpose(
                    ps_trb[:, 0:128], zxTs[:, 512 * (16 + _i) + 128 * _t:512 * (16 + _i) + 128 * (_t + 1)],
                    idb[:]), inc=pe)
                p.wait("scalar", pe)
                p.op("scalar", lambda e, _s=ht, _i=it: e.activation(
                    out=Xtok[:, 2048 * _s + 128 * _i:2048 * _s + 128 * (_i + 1)], in_=ps_trb[:, 0:128],
                    func=AF.Copy), inc=ac)
                pset("ps_trb", ac)
            xtok_ev[ht] = (ac, ac.v)
        dv_bcd = dv.v

        p.wait("gpsimd", pe, pe_op_prev)
        p.wait("gpsimd", ac, ac_zx)
        for it in range(16):
            p.op("gpsimd", lambda e, _i=it: e.tensor_scalar_mul(
                xDg[0:64, 512 * _i:512 * (_i + 1)], zxTs[0:64, 512 * (16 + _i):512 * (17 + _i)],
                Dbc[0:64, 2 * _i:2 * _i + 1]))
            p.op("gpsimd", lambda e, _i=it: e.tensor_scalar_mul(
                xDg[64:128, 512 * _i:512 * (_i + 1)], zxTs[64:128, 512 * (16 + _i):512 * (17 + _i)],
                Dbc[64:128, 2 * _i + 1:2 * _i + 2]), inc=gp)
        gp_xd = gp.v
        gp_xd_prev = gp_xd

        if upto < 5:
            if "dbg_a" in io and ci == 0:
                p.wait("gpsimd", dv, dv.v)
                p.wait("gpsimd", ac, ac.v)
                p.dma("gpsimd", io["dbg_a"], ssc[:], inc=st)
                p.dma("gpsimd", io["dbg_b"], lna[:, 0:512], inc=st)
                p.dma("gpsimd", io["dbg_c"], lng[:, 0:512], inc=st)
                p.dma("gpsimd", io["dbg_d"], angs[:], inc=st)
                p.dma("gpsimd", io["dbg_e"], bct[:], inc=st)
                p.dma("gpsimd", io["dbg_f"], Xtok[:, 0:2048], inc=st)
                p.dma("gpsimd", io["dbg_g"], zxTs[:, 0:512], inc=st)
                p.dma("gpsimd", io["dbg_h"], bcd[:], inc=st)
                p.dma("gpsimd", io["dbg_i"], dts[:], inc=st)
                p.dma("gpsimd", io["dbg_j"], ars[:], inc=st)
                p.dma("gpsimd", io["dbg_k"], ctk[:], inc=st)
                p.dma("gpsimd", io["dbg_l"], stk[:], inc=st)
                p.dma("gpsimd", io["dbg_m"], rc["cEB"][:], inc=st)
                p.dma("gpsimd", io["dbg_n"], bcf[:, 0:256], inc=st)
            continue

        ld_xprow_hist = {}
        for tt in range(4):
            sl = tt % 2
            if tt >= 2:
                p.wait("tensor", ac, ac_zx)
                p.wait("tensor", ld, ld_carry)
                v0, s0_ = mk_xtok[sl]
                p.wait("tensor", s0_, v0)
                for it in range(16):
                    pwait("tensor", "ps_trb")
                    p.op("tensor", lambda e, _t=tt, _i=it: e.transpose(
                        ps_trb[:, 0:128], zxTs[:, 512 * (16 + _i) + 128 * _t:512 * (16 + _i) + 128 * (_t + 1)],
                        idb[:]), inc=pe)
                    p.wait("scalar", pe)
                    p.op("scalar", lambda e, _s=sl, _i=it: e.activation(
                        out=Xtok[:, 2048 * _s + 128 * _i:2048 * _s + 128 * (_i + 1)], in_=ps_trb[:, 0:128],
                        func=AF.Copy), inc=ac)
                    pset("ps_trb", ac)
                xtok_ev[sl] = (ac, ac.v)
            xe_sem, xe_val = xtok_ev[sl]
            if tt == 0:
                p.fop("vector", lambda e, _t=tt: e.tensor_copy(stil[:], ssc[:, 128 * _t:128 * (_t + 1)]), dv)
            else:
                p.fop("vector", lambda e, _t=tt: e.tensor_scalar(
                    stil[:], ssc[:, 128 * _t:128 * (_t + 1)], ssc[:, 128 * _t - 1:128 * _t], None,
                    ALU.subtract), dv)
            p.fop("vector", lambda e: e.tensor_copy(shib[:], stil[:]), dv)
            p.fop("vector", lambda e: e.tensor_sub(slob[:], stil[:], shib[:]), dv)
            p.wait("scalar", dv)
            p.op("scalar", lambda e: e.activation(out=expSm[:], in_=stil[:], func=AF.Exp), inc=ac)
            p.wait("tensor", ac)
            pwait("tensor", "psA")
            p.op("tensor", lambda e: e.transpose(ps_gf[:, 0:32], expSm[:], idf[0:32, 0:32]), inc=pe)
            p.wait("vector", pe)
            p.fop("vector", lambda e: e.tensor_copy(expST[:], ps_gf[:, 0:32]), dv)
            pset("psA", dv)
            p.fop("vector", lambda e: e.tensor_copy(scl1[0:32, 0:1], expSm[:, 127:128]), dv)
            p.wait("tensor", dv)
            pwait("tensor", "psA")
            p.op("tensor", lambda e: e.transpose(ps_gf[0:1, 0:32], scl1[0:32, 0:1], idf[0:32, 0:32]), inc=pe)
            p.wait("vector", pe)
            p.fop("vector", lambda e: e.tensor_copy(esrow[:], ps_gf[0:1, 0:32]), dv)
            p.wait("tensor", dv)
            p.op("tensor", lambda e: e.matmul(ps_gf[:, 0:32], onesf[:], esrow[:], start=True, stop=True), inc=pe)
            p.wait("vector", pe)
            p.fop("vector", lambda e: e.tensor_copy(estot[:], ps_gf[:, 0:32]), dv)
            pset("psA", dv)
            p.wait("vector", ac, ac.v)
            for dst_, src_t, off in ((v1T, lna, 0), (v2T, lng, 1)):
                if tt == 0:
                    p.fop("vector", lambda e, _s=src_t, _o=off, _t=tt: e.tensor_copy(
                        stil[:], _s[:, 128 * _t + _o:128 * _t + _o + 128]), dv)
                else:
                    p.fop("vector", lambda e, _s=src_t, _o=off, _t=tt: e.tensor_scalar(
                        stil[:], _s[:, 128 * _t + _o:128 * _t + _o + 128],
                        ssc[:, 128 * _t - 1:128 * _t], None, ALU.add), dv)
                pwait("tensor", "psA")
                p.wait("tensor", dv, dv.v)
                p.op("tensor", lambda e: e.transpose(ps_gf[:, 0:32], stil[:], idf[0:32, 0:32]), inc=pe)
                p.wait("vector", pe)
                p.fop("vector", lambda e, _d=dst_: e.tensor_copy(_d[:], ps_gf[:, 0:32]), dv)
                pset("psA", dv)
            if tt > 0:
                p.wait("scalar", dv, dv_bcd)
                p.wait("scalar", pe, pe.v)
                p.dma("scalar", rowb[:], bct[127:128, 256 * (tt - 1):256 * (tt - 1) + 128], inc=ld)
                pxe_sem, pxe_val = xtok_ev[(tt - 1) % 2]
                p.wait("scalar", pxe_sem, pxe_val)
                p.dma("scalar", xprow[:], Xtok[127:128, 2048 * ((tt - 1) % 2):2048 * ((tt - 1) % 2) + 2048], inc=ld)
            ld_rows = ld.v
            p.fop("vector", lambda e, _t=tt: e.tensor_copy(scl1[0:32, 1:2], adt[:, 128 * _t:128 * _t + 1]), dv)
            p.wait("scalar", dv)
            p.op("scalar", lambda e: e.activation(out=scl1[0:32, 2:3], in_=scl1[0:32, 1:2], func=AF.Exp,
                                                  scale=-1.0), inc=ac)
            p.wait("vector", ac)
            p.fop("vector", lambda e, _t=tt: e.tensor_mul(scl1[0:32, 3:4], gam[:, 128 * _t:128 * _t + 1],
                                                          scl1[0:32, 2:3]), dv)
            p.fop("vector", lambda e: e.tensor_mul(scl1[0:32, 4:5], scl1[0:32, 3:4], expSm[:, 127:128]), dv)
            dv_scol = dv.v
            for cidx, dstrow in ((3, srow), (4, srow2)):
                p.wait("tensor", dv, dv_scol)
                pwait("tensor", "psA")
                p.op("tensor", lambda e, _c=cidx: e.transpose(ps_gf[0:1, 0:32], scl1[0:32, _c:_c + 1],
                                                              idf[0:32, 0:32]), inc=pe)
                p.wait("vector", pe)
                p.fop("vector", lambda e, _d=dstrow: e.tensor_copy(_d[:], ps_gf[0:1, 0:32]), dv)
                pset("psA", dv)
            p.wait("vector", ld, ld_rows)
            p.wait("vector", pe, pe.v)
            p.fop("vector", lambda e: e.tensor_mul(xpS2[:], xprow[:],
                                                   bass.AP(srow, 0, [[32, 1], [1, 32], [0, 64]])), dv)
            p.fop("vector", lambda e: e.tensor_mul(xpS[:], xprow[:],
                                                   bass.AP(srow2, 0, [[32, 1], [1, 32], [0, 64]])), dv)
            dv_rows = dv.v

            hsM = {}
            at = {}
            def emit_a_tail():
                p.wait("vector", pe, at["pe_sp"])
                p.wait("vector", gp, at["gp_cm"])
                p.wait("vector", pe, mark_mskd[at["r"]])
                p.op("vector", lambda e, _r=at["r"], _ps=at["psp"]: e.tensor_mul(
                    mskd[_r][:], _ps[:, 128:256] if _r == 1 else _ps[:, 0:128],
                    cmask[_r][:]), inc=dv)
                mk[at["key_sp"]] = (dv.v, dv)
                mark_cmask[at["r"]] = dv.v
                dv_mskd = dv.v
                p.wait("vector", pe, at["pe_cor"])
                p.wait("vector", pe, mark_hbb[at["r"]])
                p.op("vector", lambda e, _h=at["h"], _r=at["r"], _pY=at["psY"]: e.tensor_add(
                    h_bb[_r][:], _pY[:, 64:128], h_sb[:, 64 * _h:64 * (_h + 1)]), inc=dv)
                dv_hbb = dv.v
                p.wait("vector", pe, mark_bv[at["r"]])
                p.wait("vector", ac, at["ac_vcol"])
                p.op("vector", lambda e, _t=tt, _r=at["r"]: e.tensor_scalar_mul(
                    bv[_r][:], bct[:, 256 * _t:256 * _t + 128], vcol[_r][:]), inc=dv)
                dv_bv = dv.v
                mark_vcol[at["r"]] = dv_bv
                hsM[at["h"]] = dict(dv_mskd=dv_mskd, dv_hbb=dv_hbb, dv_bv=dv_bv, r=at["r"],
                                    key_pY=at["key_pY"], psY=at["psY"], key_hu=at["key_hu"])
                at["done"] = True
            for hh in range(NH + 1):
                if hh < NH:
                    h = hh
                    r = h % 2
                    key_pA = "psA" if r == 0 else "psB"
                    pA = pss[key_pA]
                    psY = psY0 if r == 0 else psY1
                    key_pY = "psY0" if r == 0 else "psY1"
                    key_hu = "ps0" if r == 0 else "ps1"
                    p.wait("scalar", pe, mark_cw[r])
                    p.wait("scalar", dv, dv_bcd)
                    p.op("scalar", lambda e, _h=h, _t=tt, _r=r: e.activation(
                        out=cw[_r][:], in_=bcd[:, 512 + 128 * _t:512 + 128 * (_t + 1)], func=AF.Copy,
                        scale=wT[:, _h:_h + 1]), inc=ac)
                    ac_cw = ac.v
                    pwait("tensor", key_pA)
                    p.op("tensor", lambda e, _h=h, _pb=pA: e.matmul(
                        _pb[:, 0:128], bass.AP(sel32b, _h, [[32, 32], [0, 128]]),
                        shib[:], start=True, stop=False))
                    p.op("tensor", lambda e, _h=h, _pb=pA: e.matmul(
                        _pb[:, 0:128], bass.AP(sel32b, _h, [[32, 32], [0, 128]]),
                        slob[:], start=False, stop=False))
                    p.op("tensor", lambda e, _pb=pA: e.matmul(
                        _pb[:, 0:128], idb[:], tri1[:], start=False, stop=True))
                    p.op("tensor", lambda e, _h=h, _pb=pA: e.matmul(
                        _pb[:, 128:256], bass.AP(sel32b, _h, [[32, 32], [0, 128]]),
                        shib[:], start=True, stop=False))
                    p.op("tensor", lambda e, _h=h, _pb=pA: e.matmul(
                        _pb[:, 128:256], bass.AP(sel32b, _h, [[32, 32], [0, 128]]),
                        slob[:], start=False, stop=False))
                    p.op("tensor", lambda e, _pb=pA: e.matmul(
                        _pb[:, 128:256], idb[:], tri2[:], start=False, stop=True), inc=pe)
                    pe_arg = pe.v
                    p.wait("scalar", pe, pe_arg)
                    p.wait("scalar", gp, mark_msk[r])
                    p.wait("scalar", dv, dv_rows)
                    p.op("scalar", lambda e, _h=h, _r=r, _pb=pA: e.activation(
                        out=msk1[_r][:], in_=_pb[:, 0:128], func=AF.Exp, bias=v1T[:, _h:_h + 1], scale=1.0))
                    p.op("scalar", lambda e, _h=h, _r=r, _pb=pA: e.activation(
                        out=msk2[_r][:], in_=_pb[:, 128:256], func=AF.Exp, bias=v2T[:, _h:_h + 1], scale=1.0), inc=ac)
                    ac_msk = ac.v
                    mk[key_pA] = (ac.v, ac)
                    p.wait("gpsimd", ac, ac_msk)
                    p.wait("gpsimd", dv, mark_cmask[r])
                    p.op("gpsimd", lambda e, _r=r: e.tensor_add(cmask[_r][:], msk1[_r][:], msk2[_r][:]), inc=gp)
                    gp_cm = gp.v
                    mark_msk[r] = gp_cm
                    p.wait("scalar", gp, gp_cm)
                    p.wait("scalar", dv, mark_vcol[r])
                    p.op("scalar", lambda e, _r=r: e.activation(out=vcol[_r][:], in_=cmask[_r][:, 127:128],
                                                                func=AF.Copy), inc=ac)
                    ac_vcol = ac.v
                    key_sp = "psS" if r == 0 else "ps1"
                    psp = psS if r == 0 else ps1
                    pwait("tensor", key_sp)
                    p.wait("tensor", ac, ac_cw)
                    p.op("tensor", lambda e, _t=tt, _r=r, _ps=psp: e.matmul(
                        _ps[:, 128:256] if _r == 1 else _ps[:, 0:128],
                        bcd[:, 128 * _t:128 * (_t + 1)], cw[_r][:], start=True, stop=True), inc=pe)
                    pe_sp = pe.v
                    pwait("tensor", key_pY)
                    p.wait("tensor", dv, dv_rows)
                    p.op("tensor", lambda e, _h=h, _pY=psY: e.matmul(_pY[:, 64:128], rowb[:],
                                                                     xpS2[0:1, 64 * _h:64 * (_h + 1)],
                                                                     start=True, stop=True), inc=pe)
                    pe_cor = pe.v
                    at.update(h=h, r=r, psp=psp, key_sp=key_sp, psY=psY, key_pY=key_pY,
                              key_hu=key_hu, pe_sp=pe_sp, pe_cor=pe_cor, gp_cm=gp_cm,
                              ac_vcol=ac_vcol, done=False)
                if hh >= 1:
                    h = hh - 1
                    S = hsM[h]
                    r = S["r"]
                    psY = S["psY"]
                    phu = pss[S["key_hu"]]
                    p.wait("tensor", dv, S["dv_mskd"])
                    p.wait("tensor", dv, S["dv_hbb"])
                    p.op("tensor", lambda e, _h=h, _t=tt % 2, _r=r, _pY=psY: e.matmul(
                        _pY[:, 0:64], mskd[_r][:], Xtok[:, 2048 * _t + 64 * _h:2048 * _t + 64 * (_h + 1)],
                        start=True, stop=True))
                    p.op("tensor", lambda e, _r=r, _pY=psY: e.matmul(_pY[:, 64:128], cw[_r][:], h_bb[_r][:],
                                                                     start=True, stop=True), inc=pe)
                    pe_y = pe.v
                    mark_mskd[r] = pe_y
                    mark_hbb[r] = pe_y
                    mark_cw[r] = pe_y
                    pwait("tensor", S["key_hu"])
                    p.wait("tensor", dv, S["dv_bv"])
                    p.op("tensor", lambda e, _h=h, _t=tt % 2, _r=r, _pb=phu: e.matmul(
                        _pb[:, 0:64], bv[_r][:], Xtok[:, 2048 * _t + 64 * _h:2048 * _t + 64 * (_h + 1)],
                        start=True, stop=False))
                    p.op("tensor", lambda e, _h=h, _pb=phu: e.matmul(
                        _pb[:, 0:64], rowb[:], xpS[0:1, 64 * _h:64 * (_h + 1)], start=False, stop=True), inc=pe)
                    pe_hu = pe.v
                    mark_bv[r] = pe_hu
                    p.wait("scalar", pe, pe_y)
                    p.wait("scalar", pe, mark_ypre)
                    p.op("scalar", lambda e, _h=h, _pY=psY: e.activation(
                        out=ypre[:, 64 * (_h % 2):64 * (_h % 2) + 64], in_=_pY[:, 0:64],
                        func=AF.Copy), inc=ac)
                    ac_ypc = ac.v
                    p.wait("vector", ac, ac_ypc)
                    p.op("vector", lambda e, _h=h, _pY=psY: e.scalar_tensor_tensor(
                        ypre[:, 64 * (_h % 2):64 * (_h % 2) + 64], _pY[:, 64:128], expST[:, _h:_h + 1],
                        ypre[:, 64 * (_h % 2):64 * (_h % 2) + 64], ALU.mult, ALU.add), inc=dv)
                    dv_ypre = dv.v
                    mk[S["key_pY"]] = (dv.v, dv)
                    if hh < NH:
                        emit_a_tail()
                    p.wait("vector", pe, pe_hu)
                    p.op("vector", lambda e, _h=h, _pb=phu: e.scalar_tensor_tensor(
                        h_sb[:, 64 * _h:64 * (_h + 1)], h_sb[:, 64 * _h:64 * (_h + 1)], estot[:, _h:_h + 1],
                        _pb[:, 0:64], ALU.mult, ALU.add), inc=dv)
                    mk[S["key_hu"]] = (dv.v, dv)
                    if h % 2 == 1:
                        pr = h // 2
                        pwait("tensor", "ps_trb")
                        p.wait("tensor", dv, dv_ypre)
                        p.op("tensor", lambda e: e.transpose(ps_trb[:, 0:128], ypre[:], idb[:]), inc=pe)
                        pe_yt = pe.v
                        mark_ypre = pe_yt
                        p.wait("vector", pe, pe_yt)
                        p.wait("vector", gp, gp_xd)
                        p.fop("vector", lambda e, _p=pr, _t=tt: e.tensor_add(
                            ypre[:], ps_trb[:, 0:128], xDg[:, 512 * _p + 128 * _t:512 * _p + 128 * (_t + 1)]), dv)
                        p.op("vector", lambda e, _p=pr, _t=tt: e.tensor_mul(
                            xDg[:, 512 * _p + 128 * _t:512 * _p + 128 * (_t + 1)], ypre[:],
                            zxTs[:, 512 * _p + 128 * _t:512 * _p + 128 * (_t + 1)]), inc=dv)
                        pset("ps_trb", dv)
                if hh < NH and not at["done"]:
                    emit_a_tail()
            mk_xtok[sl] = (pe.v, pe)
            if tt == 3:
                p.wait("scalar", dv, dv.v)
                cxe_sem, cxe_val = xtok_ev[1]
                p.wait("scalar", cxe_sem, cxe_val)
                p.dma("scalar", rowb[:], bct[127:128, 256 * 3:256 * 3 + 128], inc=ld)
                p.dma("scalar", xprow[:], Xtok[127:128, 2048 * 1:2048 * 2], inc=ld)
                ld_carry = ld.v
        dv_gated = dv.v
        dv_gated_prev = dv_gated

        if upto < 7:
            if "dbg_a" in io and ci == 0:
                p.wait("gpsimd", dv, dv.v)
                p.dma("gpsimd", io["dbg_f"], xDg[:, 0:2048], inc=st)
                p.dma("gpsimd", io["dbg_h"], h_sb[:, 0:1024], inc=st)
                p.dma("gpsimd", io["dbg_g"], xDg[:, 2048:2560], inc=st)
            continue

        for g in range(2):
            for k in range(16):
                wb = wot[k % 2]
                p.wait("sync", pe, pemark_wb[k % 2])
                p.dma("sync", wb[:], io["scr_woutT"][128 * k:128 * (k + 1), :], inc=lw)
                lw_k = lw.v
                p.wait("tensor", lw, lw_k)
                p.wait("tensor", dv, dv_gated)
                for q in range(4):
                    t_loc = 2 * g + q // 2
                    dmh = q % 2
                    key = OPK[q]
                    if k == 0:
                        pwait("tensor", key)
                    p.op("tensor", lambda e, _k=k, _q=q, _t=t_loc, _d=dmh: e.matmul(
                        pss[OPK[_q]][:], xDg[:, 512 * _k + 128 * _t:512 * _k + 128 * (_t + 1)],
                        wot[_k % 2][:, 512 * _d:512 * (_d + 1)],
                        start=(_k == 0), stop=(_k == 15)), inc=pe if q == 3 else None)
                pemark_wb[k % 2] = pe.v
            pe_op = pe.v
            pe_op_prev = pe_op
            p.wait("vector", pe, pe_op)
            for q in range(4):
                t_loc = 2 * g + q // 2
                dmh = q % 2
                p.fop("vector", lambda e, _q=q: e.tensor_copy(dmp[:], pss[OPK[_q]][:]), dv)
                p.wait("gpsimd", dv, dv.v)
                p.dma("gpsimd", io["s1"][128 * (4 * ci + t_loc):128 * (4 * ci + t_loc + 1),
                                         512 * dmh:512 * (dmh + 1)], dmp[:], inc=st)
                pset(OPK[q], dv)
                p.wait("vector", st, st.v)

    p.wait("gpsimd", st)
    p.wait("gpsimd", ld)
    p.wait("gpsimd", lw)


def build_l1f(p, io):
    nc = p.nc
    onesb_c = nc.inline_tensor(np.ones((1, 128), dtype=BF), name="f5_onesbc").ap()
    onesf_c = nc.inline_tensor(np.ones((1, 128), dtype=np.float32), name="f5_onesfc").ap()
    idf_c = nc.inline_tensor(np.eye(128, dtype=np.float32), name="f5_idf").ap()
    A = lambda *a, **k: nc.alloc_sbuf_tensor(*a, **k)
    kst = A("f5_kst", [128, 8 * 1024], F32)
    wqs = A("f5_wqs", [128, 8 * 1024], BF16)
    asb = A("f5_asb", [128, 8 * 16], BF16)
    bsb = A("f5_bsb", [16, 1024], BF16)
    brow = A("f5_brow", [1, 1024], BF16)
    onesb = A("f5_onesb", [1, 128], BF16)
    onesf = A("f5_onesf", [1, 128], F32)
    xnt = A("f5_xnt", [128, 1024], BF16)
    xnt2 = A("f5_xnt2", [128, 1024], BF16)
    xaTs = A("f5_xaT", [16, 128], BF16)
    g1s = A("f5_g1s", [128, 1024], BF16)
    x1s = A("f5_x1s", [128, 1024], F32)
    s1s = A("f5_s1s", [128, 1024], F32)
    x2s = A("f5_x2s", [128, 1024], F32)
    tmps = A("f5_tmp", [128, 512], F32)
    mcol = A("f5_mcol", [128, 4], F32)
    mrow = A("f5_mrow", [1, 128], F32)
    idf128 = A("f5_idf128", [128, 128], F32)

    ps_y = nc.alloc_psum_tensor("f5_psy", [128, 512], F32)
    ps_y2 = nc.alloc_psum_tensor("f5_psy2", [128, 512], F32)
    ps_xa = nc.alloc_psum_tensor("f5_psxa", [16, 128], F32)
    ps_t = nc.alloc_psum_tensor("f5_pst", [128, 128], F32)

    ld = p.sem("f5_ld")
    pe = p.sem("f5_pe")
    dv = p.sem("f5_dv")
    ac = p.sem("f5_ac")
    st = p.sem("f5_st")

    p.dma("sync", kst[:].rearrange("p (c f) -> p c f", c=8),
          io["kernel"].rearrange("(c p) f -> p c f", p=128), inc=ld)
    p.dma("gpsimd", asb[:].rearrange("p (c r) -> p c r", c=8),
          io["a"].rearrange("(c p) r -> p c r", p=128), inc=ld)
    p.dma("gpsimd", bsb[:], io["b"], inc=ld)
    p.dma("gpsimd", brow[:], io["bias"][None, :], inc=ld)
    p.dma("sync", onesb[:], onesb_c, inc=ld)
    p.dma("sync", onesf[:], onesf_c, inc=ld)
    p.dma("sync", idf128[:], idf_c, inc=ld)
    ld0 = ld.v

    p.wait("vector", ld, ld0)
    p.fop("vector", lambda e: e.reduce_max(mcol[:, 0:1], kst[:], axis=AX.X, apply_absolute_value=True), dv)
    p.wait("tensor", dv)
    p.op("tensor", lambda e: e.transpose(ps_t[0:1, 0:128], mcol[:, 0:1], idf128[:]), inc=pe)
    p.wait("vector", pe)
    p.fop("vector", lambda e: e.tensor_copy(mrow[:], ps_t[0:1, 0:128]), dv)
    p.fop("vector", lambda e: e.reduce_max(mrow[0:1, 0:1], mrow[:], axis=AX.X), dv)
    p.wait("tensor", dv)
    p.op("tensor", lambda e: e.matmul(ps_t[:, 0:1], onesf[:], mrow[0:1, 0:1], start=True, stop=True), inc=pe)
    p.wait("vector", pe)
    p.fop("vector", lambda e: e.tensor_copy(mcol[:, 1:2], ps_t[:, 0:1]), dv)
    p.fop("vector", lambda e: e.tensor_scalar_mul(mcol[:, 2:3], mcol[:, 1:2], 1.0 / 7.0), dv)
    p.fop("vector", lambda e: e.reciprocal(mcol[:, 3:4], mcol[:, 2:3]), dv)
    for cc in range(4):
        sl = slice(2048 * cc, 2048 * (cc + 1))
        p.fop("vector", lambda e, _s=sl: e.tensor_scalar_mul(kst[:, _s], kst[:, _s], mcol[:, 3:4]), dv)
        p.fop("vector", lambda e, _s=sl: e.tensor_scalar(kst[:, _s], kst[:, _s], 12582912.0, 12582912.0,
                                                         ALU.add, ALU.subtract), dv)
        p.fop("vector", lambda e, _s=sl: e.tensor_scalar(kst[:, _s], kst[:, _s], -8.0, 7.0,
                                                         ALU.max, ALU.min), dv)
        p.fop("vector", lambda e, _s=sl: e.tensor_scalar_mul(wqs[:, _s], kst[:, _s], mcol[:, 2:3]), dv)
    dv_wq = dv.v

    st_prev = 0
    dvmark = {"y": 0, "y2": 0, "xa": 0}
    for i in range(NT):
        xt = xnt if i % 2 == 0 else xnt2
        p.wait("sync", pe, pe.v if i >= 2 else 0)
        p.wait("sync", dv, dv.v if i >= 1 else 0)
        p.dma("sync", xt[:], io["xn1T"][:, 1024 * i:1024 * (i + 1)], inc=ld)
        p.dma("sync", x1s[:], io["x1"][128 * i:128 * (i + 1), :], inc=ld)
        p.dma("sync", s1s[:], io["s1"][128 * i:128 * (i + 1), :], inc=ld)
        ld_i = ld.v
        p.wait("tensor", ld, ld_i)
        p.wait("tensor", dv, dv_wq)
        p.wait("tensor", dv, dvmark["xa"])
        for c in range(KC):
            p.op("tensor", lambda e, _c=c, _x=xt: e.matmul(
                ps_xa[:], asb[:, 16 * _c:16 * (_c + 1)], _x[:, 128 * _c:128 * (_c + 1)],
                start=(_c == 0), stop=(_c == KC - 1)), inc=pe if c == KC - 1 else None)
        pe_xa = pe.v
        p.wait("vector", pe, pe_xa)
        p.fop("vector", lambda e: e.tensor_copy(xaTs[:], ps_xa[:]), dv)
        dvmark["xa"] = dv.v
        dv_xa = dv.v
        for nh in range(2):
            pb = ps_y if nh == 0 else ps_y2
            kk = "y" if nh == 0 else "y2"
            p.wait("tensor", dv, dvmark[kk])
            p.wait("tensor", dv, dv_xa)
            for c in range(KC):
                p.op("tensor", lambda e, _c=c, _n=nh, _x=xt, _pb=pb: e.matmul(
                    _pb[:], _x[:, 128 * _c:128 * (_c + 1)], wqs[:, 1024 * _c + 512 * _n:1024 * _c + 512 * (_n + 1)],
                    start=(_c == 0), stop=False))
            p.op("tensor", lambda e, _n=nh, _pb=pb: e.matmul(
                _pb[:], onesb[:], brow[0:1, 512 * _n:512 * (_n + 1)], start=False, stop=False))
            p.op("tensor", lambda e, _n=nh, _pb=pb: e.matmul(
                _pb[:], xaTs[:], bsb[:, 512 * _n:512 * (_n + 1)], start=False, stop=True), inc=pe)
            pe_y = pe.v
            p.wait("scalar", pe, pe_y)
            p.op("scalar", lambda e, _n=nh, _pb=pb: e.activation(
                out=g1s[:, 512 * _n:512 * (_n + 1)], in_=_pb[:], func=AF.Silu), inc=ac)
            dvmark[kk] = ac.v
            mkeng = ac
        ac_g1 = ac.v
        p.wait("vector", ac, ac_g1)
        p.wait("vector", ld, ld_i)
        p.wait("vector", st, st_prev)
        for nh in range(2):
            p.fop("vector", lambda e, _n=nh: e.tensor_mul(tmps[:], s1s[:, 512 * _n:512 * (_n + 1)],
                                                          g1s[:, 512 * _n:512 * (_n + 1)]), dv)
            p.op("vector", lambda e, _n=nh: e.tensor_add(x2s[:, 512 * _n:512 * (_n + 1)],
                                                         x1s[:, 512 * _n:512 * (_n + 1)], tmps[:]), inc=dv)
        dv_x2 = dv.v
        p.wait("gpsimd", dv, dv_x2)
        p.dma("gpsimd", io["out"][128 * i:128 * (i + 1), :], x2s[:], inc=st)
        st_prev = st.v

    p.wait("gpsimd", st)
    p.wait("gpsimd", ld)

CORES = list(range(8))


def _mk_io(nc, specs):
    io = {}
    for name, shape, dt_, kind in specs:
        io[name] = nc.dram_tensor(name, shape, dt_, kind=kind).ap()
    return io


def _barrier(p, bar, scr):
    sv, sp, fs = scr
    tgt = bar.v + 5
    bar.v = tgt
    p.q["vector"].append(lambda e, _b=bar: e.memset(sv[:], 0.0).then_inc(_b.h, 1))
    p.q["scalar"].append(lambda e, _b=bar: e.memzero(fs[:]).then_inc(_b.h, 1))
    p.q["gpsimd"].append(lambda e, _b=bar: e.memset(sv[:, 1:2], 0.0).then_inc(_b.h, 1))
    p.q["tensor"].append(lambda e, _b=bar: e.matmul(sp[:], sv[0:1, 0:1], sv[0:1, 0:4],
                                                    start=True, stop=True).then_inc(_b.h, 1))
    p.q["sync"].append(lambda e, _b=bar: e.nop().then_inc(_b.h, 1))
    for eng in ENGINES:
        p.q[eng].append(lambda e, _b=bar, _t=tgt: e.wait_ge(_b.h, _t))


def kernel(**inputs):
    inp = inputs
    B = 8
    x_full = np.ascontiguousarray(inp["L0_attn_x"], dtype=np.float32)

    nc = bass.Bass("TRN2", target_bir_lowering=False)
    io = _mk_io(nc, [
        ("x", [T, D], F32, "ExternalInput"),
        ("ln_g", [D], F32, "ExternalInput"), ("ln_b", [D], F32, "ExternalInput"),
        ("rW", [D, 8], F32, "ExternalInput"), ("rb", [8], F32, "ExternalInput"),
        ("W1", [8 * 1024, 2048], F32, "ExternalInput"), ("b1", [8, 2048], F32, "ExternalInput"),
        ("W2", [8 * 2048, 1024], F32, "ExternalInput"), ("b2", [8, 1024], F32, "ExternalInput"),
        ("Wq", [D, D], F32, "ExternalInput"), ("Wk", [D, D], F32, "ExternalInput"),
        ("Wv", [D, D], F32, "ExternalInput"), ("Wo", [D, D], F32, "ExternalInput"),
        ("betas", [16, 64], F32, "ExternalInput"),
        ("W_in", [4480, 1024], F32, "ExternalInput"), ("W_out", [1024, 2048], F32, "ExternalInput"),
        ("dt_bias", [32], F32, "ExternalInput"),
        ("B_bias", [32, 128], F32, "ExternalInput"), ("C_bias", [32, 128], F32, "ExternalInput"),
        ("B_norm_w", [128], F32, "ExternalInput"), ("C_norm_w", [128], F32, "ExternalInput"),
        ("D_skip", [32], F32, "ExternalInput"),
        ("kernel", [D, D], F32, "ExternalInput"), ("bias", [D], F32, "ExternalInput"),
        ("a", [D, 16], F32, "ExternalInput"), ("b", [16, D], F32, "ExternalInput"),
        ("xn", [T, D], F32, "Internal"), ("xnT", [128, NT * 1024], BF16, "Internal"),
        ("zT", [128, NT * 1024], BF16, "Internal"), ("wr", [T, 8], F32, "Internal"),
        ("wrT", [8, T], F32, "Internal"), ("g0", [T, D], BF16, "Internal"),
        ("x1", [T, D], F32, "Internal"),
        ("scr_win", [4480, 1024], BF16, "Internal"), ("scr_wout", [1024, 2048], BF16, "Internal"),
        ("scr_woutT", [2048, 1024], BF16, "Internal"),
        ("xn1T", [128, NT * 1024], BF16, "Internal"), ("s1", [T, D], F32, "Internal"),
        ("out", [T, D], F32, "ExternalOutput")])

    p = Prog(nc)
    bar = p.sem("stage_bar")
    scr_v = nc.alloc_sbuf_tensor("bar_scr", [1, 4], F32)
    scr_f = nc.alloc_sbuf_tensor("bar_scr2", [1, 4], F32)
    stage_no = [0]

    def staged(build_fn, sub):
        sb, pb = nc.sbuf_base, nc.psum_base
        build_fn(p, {k: io[k] for k in sub})
        nc.sbuf_base, nc.psum_base = sb, pb
        scr_p = nc.alloc_psum_tensor("bar_psum%d" % stage_no[0], [1, 4], F32)
        stage_no[0] += 1
        nc.psum_base = pb
        _barrier(p, bar, (scr_v, scr_p, scr_f))

    staged(build_prep, ["x", "ln_g", "ln_b", "rW", "rb", "xn", "xnT", "zT", "wr", "wrT"])
    staged(build_moe, ["zT", "xn", "wr", "wrT", "W1", "b1", "W2", "b2", "g0"])
    staged(build_attn0, ["xnT", "Wq", "Wk", "Wv", "Wo", "betas", "g0", "x", "x1",
                         "W_in", "W_out", "scr_win", "scr_wout", "scr_woutT"])
    staged(build_l1a, ["x1", "W_in", "W_out", "dt_bias", "B_bias", "C_bias",
                       "B_norm_w", "C_norm_w", "D_skip", "scr_win", "scr_wout", "scr_woutT",
                       "xn1T", "s1"])
    staged(build_l1f, ["x1", "s1", "xn1T", "kernel", "bias", "a", "b", "out"])
    p.build_block()

    W1f = np.ascontiguousarray(inp["L0_ffn_W1"]).reshape(8 * 1024, 2048)
    W2f = np.ascontiguousarray(inp["L0_ffn_W2"]).reshape(8 * 2048, 1024)
    betas2 = np.ascontiguousarray(inp["L0_attn_betas"]).reshape(16, 64)
    shared = {"ln_g": inp["L0_ffn_ln_g"], "ln_b": inp["L0_ffn_ln_b"], "rW": inp["L0_ffn_rW"],
              "rb": inp["L0_ffn_rb"], "W1": W1f, "b1": inp["L0_ffn_b1"], "W2": W2f,
              "b2": inp["L0_ffn_b2"], "Wq": inp["L0_attn_Wq"], "Wk": inp["L0_attn_Wk"],
              "Wv": inp["L0_attn_Wv"], "Wo": inp["L0_attn_Wo"], "betas": betas2,
              "W_in": inp["L1_attn_W_in"], "W_out": inp["L1_attn_W_out"],
              "dt_bias": inp["L1_attn_dt_bias"], "B_bias": inp["L1_attn_B_bias"],
              "C_bias": inp["L1_attn_C_bias"], "B_norm_w": inp["L1_attn_B_norm_w"],
              "C_norm_w": inp["L1_attn_C_norm_w"], "D_skip": inp["L1_attn_D_skip"],
              "kernel": inp["L1_ffn_kernel"], "bias": inp["L1_ffn_bias"],
              "a": inp["L1_ffn_a"], "b": inp["L1_ffn_b"]}
    maps = [dict(shared, x=x_full[bb]) for bb in range(B)]
    res = run_bass_kernel_spmd(nc, maps, core_ids=CORES).results
    return np.stack([np.asarray(res[bb]["out"], dtype=np.float32) for bb in range(B)], axis=0)


# version 25
# speedup vs baseline: 1.2101x; 1.2101x over previous
import numpy as np
import ml_dtypes
import concourse.bass as bass
import concourse.mybir as mybir
from concourse.bass_utils import run_bass_kernel_spmd

F32 = mybir.dt.float32
BF16 = mybir.dt.bfloat16
AF = mybir.ActivationFunctionType
ALU = mybir.AluOpType
AX = mybir.AxisListType
BF = ml_dtypes.bfloat16

ENGINES = ("sync", "tensor", "vector", "scalar", "gpsimd")

D = 1024
T = 2048
NT = T // 128
KC = D // 128
EPS = 1.0e-5


class Sem:
    __slots__ = ("h", "v", "name")

    def __init__(self, h, name):
        self.h = h
        self.v = 0
        self.name = name


class Prog:
    def __init__(self, nc):
        self.nc = nc
        self.q = {n: [] for n in ENGINES}

    def sem(self, name):
        return Sem(self.nc.alloc_semaphore(name), name)

    def wait(self, eng, sem, val=None):
        v = sem.v if val is None else val
        self.q[eng].append(lambda e, _s=sem, _v=v: e.wait_ge(_s.h, _v))

    def op(self, eng, builder, inc=None):
        if inc is not None:
            inc.v += 1

        def f(e, _b=builder, _i=inc):
            ins = _b(e)
            if _i is not None:
                ins.then_inc(_i.h, 1)
        self.q[eng].append(f)

    def fop(self, eng, builder, sem):
        self.op(eng, builder, inc=sem)
        self.wait(eng, sem)

    def dma(self, eng, out, in_, inc=None, **kw):
        if inc is not None:
            inc.v += 16

        def f(e, _o=out, _in=in_, _i=inc, _kw=kw):
            ins = e.dma_start(out=_o, in_=_in, **_kw)
            if _i is not None:
                ins.then_inc(_i.h, 16)
        self.q[eng].append(f)

    def build_block(self):
        nc = self.nc
        with nc.Block() as block:
            @block.sync
            def _(e):
                for f in self.q["sync"]:
                    f(e)

            @block.tensor
            def _(e):
                for f in self.q["tensor"]:
                    f(e)

            @block.vector
            def _(e):
                for f in self.q["vector"]:
                    f(e)

            @block.scalar
            def _(e):
                for f in self.q["scalar"]:
                    f(e)

            @block.gpsimd
            def _(e):
                for f in self.q["gpsimd"]:
                    f(e)


def build_prep(p, io):
    nc = p.nc
    t_x, t_g, t_b, t_rW, t_rb = io["x"], io["ln_g"], io["ln_b"], io["rW"], io["rb"]
    t_xn, t_xnT, t_zT, t_wr, t_wrT = io["xn"], io["xnT"], io["zT"], io["wr"], io["wrT"]

    identb_c = nc.inline_tensor(np.eye(128, dtype=BF), name="prep_idb").ap()
    identf_c = nc.inline_tensor(np.eye(128, dtype=np.float32), name="prep_idf").ap()
    onesf_c = nc.inline_tensor(np.ones((1, 128), dtype=np.float32), name="prep_onesf").ap()

    xt = [nc.alloc_sbuf_tensor("pp_xt0", [128, D], F32),
          nc.alloc_sbuf_tensor("pp_xt1", [128, D], F32)]
    xnf = nc.alloc_sbuf_tensor("pp_xnf", [128, D], F32)
    xnb = nc.alloc_sbuf_tensor("pp_xnb", [128, D], BF16)
    zf = nc.alloc_sbuf_tensor("pp_zf", [128, D], F32)
    zb = nc.alloc_sbuf_tensor("pp_zb", [128, D], BF16)
    ztf = nc.alloc_sbuf_tensor("pp_ztf", [128, D], F32)
    sqd = nc.alloc_sbuf_tensor("pp_sqd", [128, D], F32)
    stat = nc.alloc_sbuf_tensor("pp_stat", [128, 16], F32)
    gb = nc.alloc_sbuf_tensor("pp_gb", [128, D], F32)
    bb = nc.alloc_sbuf_tensor("pp_bb", [128, D], F32)
    grow = nc.alloc_sbuf_tensor("pp_grow", [1, D], F32)
    brow = nc.alloc_sbuf_tensor("pp_brow", [1, D], F32)
    rbrow = nc.alloc_sbuf_tensor("pp_rbrow", [1, 8], F32)
    rw = nc.alloc_sbuf_tensor("pp_rw", [128, KC * 8], F32)
    onesf = nc.alloc_sbuf_tensor("pp_onesf", [1, 128], F32)
    idb = nc.alloc_sbuf_tensor("pp_idb", [128, 128], BF16)
    idf = nc.alloc_sbuf_tensor("pp_idf", [128, 128], F32)
    rt = nc.alloc_sbuf_tensor("pp_rt", [128, 8], F32)
    rt2 = nc.alloc_sbuf_tensor("pp_rt2", [128, 8], F32)
    wv = nc.alloc_sbuf_tensor("pp_wv", [128, 8], F32)
    wvT = nc.alloc_sbuf_tensor("pp_wvT", [8, 128], F32)
    stgx = nc.alloc_sbuf_tensor("pp_stgx", [128, D], BF16)
    stgz = nc.alloc_sbuf_tensor("pp_stgz", [128, D], BF16)

    ps_tb = nc.alloc_psum_tensor("pp_ps_tb", [128, D], BF16)
    ps_tz = nc.alloc_psum_tensor("pp_ps_tz", [128, D], BF16)
    ps_tf = nc.alloc_psum_tensor("pp_ps_tf", [128, D], F32)
    ps_r = nc.alloc_psum_tensor("pp_ps_r", [128, 8], F32)
    ps_wT = nc.alloc_psum_tensor("pp_ps_wT", [8, 128], F32)

    ld = p.sem("pp_ld")
    pe = p.sem("pp_pe")
    dv = p.sem("pp_dv")
    ac = p.sem("pp_ac")
    st = p.sem("pp_st")

    p.dma("sync", grow[:], t_g[None, :], inc=ld)
    p.dma("sync", brow[:], t_b[None, :], inc=ld)
    p.dma("sync", rbrow[:], t_rb[None, :], inc=ld)
    p.dma("sync", rw[:].rearrange("p (k e) -> p k e", k=KC),
          t_rW.rearrange("(k p) e -> p k e", p=128), inc=ld)
    p.dma("sync", idb[:], identb_c, inc=ld)
    p.dma("sync", idf[:], identf_c, inc=ld)
    p.dma("sync", onesf[:], onesf_c, inc=ld)
    ld_pre = ld.v

    p.wait("vector", ld, ld_pre)
    p.op("vector", lambda e: e.memset(stat[:, 15:16], EPS), inc=dv)
    p.wait("tensor", ld, ld_pre)
    for h in range(2):
        p.op("tensor", lambda e, _h=h: e.matmul(ps_tf[:, _h * 512:(_h + 1) * 512], onesf[:],
                                                grow[0:1, _h * 512:(_h + 1) * 512], start=True, stop=True),
             inc=pe if h == 1 else None)
    p.wait("vector", pe)
    p.op("vector", lambda e: e.tensor_copy(gb[:], ps_tf[:]), inc=dv)
    dv_gb = dv.v
    p.wait("tensor", dv, dv_gb)
    for h in range(2):
        p.op("tensor", lambda e, _h=h: e.matmul(ps_tf[:, _h * 512:(_h + 1) * 512], onesf[:],
                                                brow[0:1, _h * 512:(_h + 1) * 512], start=True, stop=True),
             inc=pe if h == 1 else None)
    p.wait("vector", pe)
    p.op("vector", lambda e: e.tensor_copy(bb[:], ps_tf[:]), inc=dv)

    M1, R1, M2, R2, SU, SS, SC, NG = range(8)
    dv_xn_hist = {}
    dv_evx_prev = 0
    dv_evz_prev = 0
    dv_ztf_prev = 0
    st_xn_prev = 0
    st_xnT_prev = 0
    st_zT_prev = 0
    st_wr_prev = 0
    ac_xnb_prev = 0
    ac_sq2_prev = 0
    ac_zb_prev = 0
    ac_rexp_prev = 0
    pe_tx_prev = 0
    pe_tz_prev = 0
    pe_ztf_prev = 0

    for i in range(NT):
        xti = xt[i % 2]
        if i >= 2:
            p.wait("sync", dv, dv_xn_hist[i - 2])
        p.dma("sync", xti[:], t_x[128 * i:128 * (i + 1), :], inc=ld)
        ld_i = ld.v

        p.wait("vector", ld, ld_i)
        p.fop("vector", lambda e, _x=xti: e.reduce_sum(stat[:, SU:SU + 1], _x[:], axis=AX.X), dv)
        p.wait("scalar", ld, ld_i)
        p.op("scalar", lambda e, _x=xti: e.activation(out=sqd[:], in_=_x[:], func=AF.Square,
                                                      accum_out=stat[:, SS:SS + 1]), inc=ac)
        p.wait("vector", ac)
        p.wait("vector", ac, ac_rexp_prev)
        p.op("vector", lambda e: e.tensor_scalar_mul(stat[:, M1:M1 + 1], stat[:, SU:SU + 1], 1.0 / D))
        p.op("vector", lambda e: e.tensor_scalar_mul(stat[:, SC:SC + 1], stat[:, SS:SS + 1], 1.0 / D))
        p.fop("vector", lambda e: e.tensor_scalar_mul(stat[:, NG:NG + 1], stat[:, M1:M1 + 1], -1.0), dv)
        p.op("vector", lambda e: e.scalar_tensor_tensor(stat[:, SC:SC + 1], stat[:, M1:M1 + 1],
                                                        stat[:, NG:NG + 1], stat[:, SC:SC + 1],
                                                        ALU.mult, ALU.add), inc=dv)
        p.wait("scalar", dv)
        p.op("scalar", lambda e: e.activation(out=stat[:, R1:R1 + 1], in_=stat[:, SC:SC + 1], func=AF.Ln,
                                              bias=stat[:, 15:16], scale=1.0), inc=ac)
        p.wait("scalar", ac)
        p.op("scalar", lambda e: e.activation(out=stat[:, R1:R1 + 1], in_=stat[:, R1:R1 + 1], func=AF.Exp,
                                              scale=-0.5), inc=ac)
        p.wait("vector", ac)
        p.wait("vector", st, st_xn_prev)
        p.wait("vector", ac, max(ac_xnb_prev, ac_sq2_prev))
        p.op("vector", lambda e, _x=xti: e.tensor_scalar(xnf[:], _x[:], stat[:, M1:M1 + 1], stat[:, R1:R1 + 1],
                                                         ALU.subtract, ALU.mult), inc=dv)
        dv_xn = dv.v
        dv_xn_hist[i] = dv_xn
        p.wait("gpsimd", dv, dv_xn)
        p.dma("gpsimd", t_xn[128 * i:128 * (i + 1), :], xnf[:], inc=st)
        st_xn_prev = st.v
        p.wait("scalar", dv, dv_xn)
        p.wait("scalar", pe, pe_tx_prev)
        p.op("scalar", lambda e: e.activation(out=xnb[:], in_=xnf[:], func=AF.Copy), inc=ac)
        ac_xnb = ac.v
        ac_xnb_prev = ac_xnb

        p.fop("vector", lambda e: e.reduce_sum(stat[:, SU:SU + 1], xnf[:], axis=AX.X), dv)
        p.op("vector", lambda e: e.tensor_scalar_mul(stat[:, M2:M2 + 1], stat[:, SU:SU + 1], 1.0 / D), inc=dv)
        p.wait("scalar", dv)
        p.op("scalar", lambda e: e.activation(out=sqd[:], in_=xnf[:], func=AF.Square,
                                              accum_out=stat[:, SS:SS + 1]), inc=ac)
        ac_sq2_prev = ac.v
        p.wait("vector", ac)
        p.op("vector", lambda e: e.tensor_scalar_mul(stat[:, SC:SC + 1], stat[:, SS:SS + 1], 1.0 / D))
        p.fop("vector", lambda e: e.tensor_scalar_mul(stat[:, NG:NG + 1], stat[:, M2:M2 + 1], -1.0), dv)
        p.op("vector", lambda e: e.scalar_tensor_tensor(stat[:, SC:SC + 1], stat[:, M2:M2 + 1],
                                                        stat[:, NG:NG + 1], stat[:, SC:SC + 1],
                                                        ALU.mult, ALU.add), inc=dv)
        p.wait("scalar", dv)
        p.op("scalar", lambda e: e.activation(out=stat[:, R2:R2 + 1], in_=stat[:, SC:SC + 1], func=AF.Ln,
                                              bias=stat[:, 15:16], scale=1.0), inc=ac)
        p.wait("scalar", ac)
        p.op("scalar", lambda e: e.activation(out=stat[:, R2:R2 + 1], in_=stat[:, R2:R2 + 1], func=AF.Exp,
                                              scale=-0.5), inc=ac)
        p.wait("vector", ac)
        p.wait("vector", ac, ac_zb_prev)
        p.wait("vector", pe, pe_ztf_prev)
        p.op("vector", lambda e: e.tensor_scalar(zf[:], xnf[:], stat[:, M2:M2 + 1], stat[:, R2:R2 + 1],
                                                 ALU.subtract, ALU.mult))
        p.op("vector", lambda e: e.tensor_mul(zf[:], zf[:], gb[:]))
        p.op("vector", lambda e: e.tensor_add(zf[:], zf[:], bb[:]), inc=dv)
        dv_z = dv.v
        p.wait("scalar", dv, dv_z)
        p.wait("scalar", pe, pe_tz_prev)
        p.op("scalar", lambda e: e.activation(out=zb[:], in_=zf[:], func=AF.Copy), inc=ac)
        ac_zb = ac.v
        ac_zb_prev = ac_zb

        p.wait("tensor", ac, ac_xnb)
        p.wait("tensor", dv, dv_evx_prev)
        for k in range(KC):
            p.op("tensor", lambda e, _k=k: e.transpose(ps_tb[:, 128 * _k:128 * (_k + 1)],
                                                       xnb[:, 128 * _k:128 * (_k + 1)], idb[:]),
                 inc=pe if k == KC - 1 else None)
        pe_tx_prev = pe.v
        p.wait("vector", pe)
        p.wait("vector", st, st_xnT_prev)
        p.op("vector", lambda e: e.tensor_copy(stgx[:], ps_tb[:]), inc=dv)
        dv_evx_prev = dv.v
        p.wait("gpsimd", dv, dv_evx_prev)
        p.dma("gpsimd", t_xnT[:, 1024 * i:1024 * (i + 1)], stgx[:], inc=st)
        st_xnT_prev = st.v

        p.wait("tensor", ac, ac_zb)
        p.wait("tensor", dv, dv_evz_prev)
        for k in range(KC):
            p.op("tensor", lambda e, _k=k: e.transpose(ps_tz[:, 128 * _k:128 * (_k + 1)],
                                                       zb[:, 128 * _k:128 * (_k + 1)], idb[:]),
                 inc=pe if k == KC - 1 else None)
        pe_tz_prev = pe.v
        p.wait("vector", pe)
        p.wait("vector", st, st_zT_prev)
        p.op("vector", lambda e: e.tensor_copy(stgz[:], ps_tz[:]), inc=dv)
        dv_evz_prev = dv.v
        p.wait("gpsimd", dv, dv_evz_prev)
        p.dma("gpsimd", t_zT[:, 1024 * i:1024 * (i + 1)], stgz[:], inc=st)
        st_zT_prev = st.v

        p.wait("tensor", dv, max(dv_z, dv_ztf_prev))
        for k in range(KC):
            p.op("tensor", lambda e, _k=k: e.transpose(ps_tf[:, 128 * _k:128 * (_k + 1)],
                                                       zf[:, 128 * _k:128 * (_k + 1)], idf[:]),
                 inc=pe if k == KC - 1 else None)
        pe_ztf_prev = pe.v
        p.wait("vector", pe)
        p.op("vector", lambda e: e.tensor_copy(ztf[:], ps_tf[:]), inc=dv)
        dv_ztf = dv.v
        dv_ztf_prev = dv.v

        p.wait("tensor", dv, dv_ztf)
        p.wait("tensor", ac, ac_rexp_prev)
        for k in range(KC):
            p.op("tensor", lambda e, _k=k: e.matmul(ps_r[:], ztf[:, 128 * _k:128 * (_k + 1)],
                                                    rw[:, 8 * _k:8 * (_k + 1)], start=(_k == 0), stop=False))
        p.op("tensor", lambda e: e.matmul(ps_r[:], onesf[:, 0:128], rbrow[:], start=False, stop=True), inc=pe)
        pe_r = pe.v
        p.wait("vector", pe, pe_r)
        p.fop("vector", lambda e: e.reduce_max(stat[:, SC:SC + 1], ps_r[:], axis=AX.X), dv)
        p.op("vector", lambda e: e.tensor_scalar_mul(stat[:, SC:SC + 1], stat[:, SC:SC + 1], -1.0), inc=dv)
        p.wait("scalar", dv)
        p.op("scalar", lambda e: e.activation(out=rt[:], in_=ps_r[:], func=AF.Exp,
                                              bias=stat[:, SC:SC + 1], scale=1.0), inc=ac)
        ac_rexp_prev = ac.v
        p.wait("vector", ac)
        p.fop("vector", lambda e: e.reduce_max(stat[:, SC:SC + 1], rt[:], axis=AX.X), dv)
        p.fop("vector", lambda e: e.tensor_scalar(rt2[:], rt[:], stat[:, SC:SC + 1], None, ALU.is_equal), dv)
        p.fop("vector", lambda e: e.scalar_tensor_tensor(rt2[:], rt2[:], -1e30, rt[:], ALU.mult, ALU.add), dv)
        p.fop("vector", lambda e: e.reduce_max(stat[:, SC:SC + 1], rt2[:], axis=AX.X), dv)
        p.fop("vector", lambda e: e.tensor_scalar(rt2[:], rt[:], stat[:, SC:SC + 1], None, ALU.is_ge), dv)
        p.wait("vector", st, st_wr_prev)
        p.fop("vector", lambda e: e.tensor_mul(wv[:], rt[:], rt2[:]), dv)
        p.fop("vector", lambda e: e.reduce_sum(stat[:, SC:SC + 1], wv[:], axis=AX.X), dv)
        p.fop("vector", lambda e: e.reciprocal(stat[:, SC:SC + 1], stat[:, SC:SC + 1]), dv)
        p.op("vector", lambda e: e.tensor_scalar_mul(wv[:], wv[:], stat[:, SC:SC + 1]), inc=dv)
        dv_w = dv.v
        p.wait("tensor", dv, dv_w)
        p.op("tensor", lambda e: e.transpose(ps_wT[:], wv[:], idf[:]), inc=pe)
        p.wait("vector", pe)
        p.op("vector", lambda e: e.tensor_copy(wvT[:], ps_wT[:]), inc=dv)
        dv_wT = dv.v
        p.wait("gpsimd", dv, dv_wT)
        p.dma("gpsimd", t_wr[128 * i:128 * (i + 1), :], wv[:], inc=st)
        p.dma("gpsimd", t_wrT[:, 128 * i:128 * (i + 1)], wvT[:], inc=st)
        st_wr_prev = st.v
        if "dbg" in io:
            p.op("vector", lambda e: e.tensor_copy(stat[:, 8:15], rt[:, 0:7]), inc=dv)
            p.wait("gpsimd", dv)
            p.dma("gpsimd", io["dbg"][i], stat[:], inc=st)

    p.wait("gpsimd", st)
    p.wait("gpsimd", ld)
    return st


def ap3(t, off, dims):
    row = t.shape[1]
    return bass.AP(t, off, [[row, t.shape[0]]] + dims)


def build_moe(p, io):
    nc = p.nc
    identf_c = nc.inline_tensor(np.eye(128, dtype=np.float32), name="moe_idf").ap()

    zTs = nc.alloc_sbuf_tensor("mo_zT", [128, NT * 1024], BF16)
    mixed = nc.alloc_sbuf_tensor("mo_mx", [128, NT * 1024], F32)
    wrs = nc.alloc_sbuf_tensor("mo_wr", [128, NT * 8], F32)
    wrTs = nc.alloc_sbuf_tensor("mo_wrT", [8, T], F32)
    b2s = nc.alloc_sbuf_tensor("mo_b2", [8, 1024], F32)
    b1s = nc.alloc_sbuf_tensor("mo_b1", [8, 2048], F32)
    b1T = nc.alloc_sbuf_tensor("mo_b1T", [128, 16 * 8], F32)
    idf = nc.alloc_sbuf_tensor("mo_idf", [128, 128], F32)
    W1s = nc.alloc_sbuf_tensor("mo_W1", [128, 8 * 2048], BF16)
    W2s = nc.alloc_sbuf_tensor("mo_W2", [128, 16 * 1024], BF16)
    hid = nc.alloc_sbuf_tensor("mo_hid", [128, 16 * 512], BF16)
    g0s = nc.alloc_sbuf_tensor("mo_g0", [128, 1024], BF16)

    ps_g1 = [nc.alloc_psum_tensor("mo_ps1a", [128, 512], F32),
             nc.alloc_psum_tensor("mo_ps1b", [128, 512], F32)]
    ps_g2 = [nc.alloc_psum_tensor("mo_ps2a", [128, 512], F32),
             nc.alloc_psum_tensor("mo_ps2b", [128, 512], F32)]
    ps_b2 = nc.alloc_psum_tensor("mo_psb2", [128, 512], F32)
    ps_t8 = nc.alloc_psum_tensor("mo_pst8", [128, 8], F32)

    ld = p.sem("mo_ld")
    lw = p.sem("mo_lw")
    pe = p.sem("mo_pe")
    dv = p.sem("mo_dv")
    ac = p.sem("mo_ac")
    st = p.sem("mo_st")

    p.dma("sync", zTs[:], io["zT"], inc=ld)
    p.dma("sync", mixed[:].rearrange("p (i c) -> p i c", i=NT),
          io["xn"].rearrange("(i p) c -> p i c", p=128), inc=ld)
    p.dma("sync", wrs[:].rearrange("p (i e) -> p i e", i=NT),
          io["wr"].rearrange("(i p) e -> p i e", p=128), inc=ld)
    p.dma("sync", wrTs[:], io["wrT"], inc=ld)
    p.dma("sync", b2s[:], io["b2"], inc=ld)
    p.dma("sync", b1s[:], io["b1"], inc=ld)
    p.dma("sync", idf[:], identf_c, inc=ld)
    ld0 = ld.v

    p.wait("tensor", ld, ld0)
    evprev = 0
    for t_ in range(16):
        p.wait("tensor", dv, evprev)
        p.op("tensor", lambda e, _t=t_: e.transpose(ps_t8[:, 0:8], b1s[:, 128 * _t:128 * (_t + 1)],
                                                    idf[0:8, 0:8]), inc=pe)
        p.wait("vector", pe)
        p.op("vector", lambda e, _t=t_: e.tensor_copy(b1T[:, 8 * _t:8 * (_t + 1)], ps_t8[:, 0:8]), inc=dv)
        evprev = dv.v

    pe_g1_done = 0
    pe_g2_done = 0
    acmark1 = {0: 0, 1: 0}
    dvmark2 = {0: 0, 1: 0}
    for e_ in range(8):
        p.wait("gpsimd", pe, pe_g1_done)
        p.dma("gpsimd", W1s[:].rearrange("p (c f) -> p c f", c=8),
              io["W1"][1024 * e_:1024 * (e_ + 1), :].rearrange("(c p) f -> p c f", p=128), inc=lw)
        lw_w1 = lw.v
        p.wait("gpsimd", pe, pe_g2_done)
        p.dma("gpsimd", W2s[:].rearrange("p (c f) -> p c f", c=16),
              io["W2"][2048 * e_:2048 * (e_ + 1), :].rearrange("(c p) f -> p c f", p=128), inc=lw)
        lw_e = lw.v
        p.wait("tensor", lw, lw_w1)
        p.wait("tensor", ld, ld0)
        for T2 in range(4):
            pe_gemm2_prev = pe_g2_done
            for hft in range(16):
                bk = hft % 2
                pb = ps_g1[bk]
                p.wait("tensor", ac, acmark1[bk])
                for c in range(KC):
                    p.op("tensor", lambda e, _c=c, _h=hft, _pb=pb, _T=T2: e.matmul(
                        _pb[:], W1s[:, _c * 2048 + 128 * _h:_c * 2048 + 128 * (_h + 1)],
                        ap3(zTs, 4096 * _T + 128 * _c, [[1024, 4], [1, 128]]),
                        start=(_c == 0), stop=(_c == KC - 1)), inc=pe if c == KC - 1 else None)
                pe_mm = pe.v
                p.wait("scalar", pe, pe_mm)
                p.wait("scalar", pe, pe_gemm2_prev)
                p.op("scalar", lambda e, _h=hft, _e=e_, _pb=pb: e.activation(
                    out=hid[:, 512 * _h:512 * (_h + 1)], in_=_pb[:], func=AF.Gelu,
                    bias=b1T[:, 8 * _h + _e:8 * _h + _e + 1], scale=1.0), inc=ac)
                acmark1[bk] = ac.v
            ac_hid_done = ac.v
            pe_g1_done = pe.v
            p.wait("tensor", ac, ac_hid_done)
            if T2 == 0:
                p.wait("tensor", lw, lw_e)
            for tt in range(4):
                i_glob = 4 * T2 + tt
                for nh in range(2):
                    bk = (2 * tt + nh) % 2
                    pb = ps_g2[bk]
                    p.wait("tensor", dv, dvmark2[bk])
                    for hk in range(16):
                        p.op("tensor", lambda e, _k=hk, _t=tt, _n=nh, _pb=pb: e.matmul(
                            _pb[:], hid[:, 512 * _k + 128 * _t:512 * _k + 128 * (_t + 1)],
                            W2s[:, 1024 * _k + 512 * _n:1024 * _k + 512 * (_n + 1)],
                            start=(_k == 0), stop=(_k == 15)), inc=pe if hk == 15 else None)
                    pe_mm2 = pe.v
                    p.wait("vector", pe, pe_mm2)
                    p.op("vector", lambda e, _i=i_glob, _n=nh, _e=e_, _pb=pb: e.scalar_tensor_tensor(
                        mixed[:, 1024 * _i + 512 * _n:1024 * _i + 512 * (_n + 1)], _pb[:],
                        wrs[:, 8 * _i + _e:8 * _i + _e + 1],
                        mixed[:, 1024 * _i + 512 * _n:1024 * _i + 512 * (_n + 1)],
                        ALU.mult, ALU.add), inc=dv)
                    dvmark2[bk] = dv.v
            pe_g2_done = pe.v

    dv_mx_done = dv.v
    st_prev = 0
    dvmark_b2 = 0
    p.wait("tensor", dv, dv_mx_done)
    for i in range(NT):
        for nh in range(2):
            p.wait("tensor", dv, dvmark_b2)
            p.op("tensor", lambda e, _i=i, _n=nh: e.matmul(
                ps_b2[:], wrTs[:, 128 * _i:128 * (_i + 1)], b2s[:, 512 * _n:512 * (_n + 1)],
                start=True, stop=True), inc=pe)
            pe_b2 = pe.v
            p.wait("vector", pe, pe_b2)
            p.op("vector", lambda e, _i=i, _n=nh: e.tensor_add(
                mixed[:, 1024 * _i + 512 * _n:1024 * _i + 512 * (_n + 1)],
                mixed[:, 1024 * _i + 512 * _n:1024 * _i + 512 * (_n + 1)], ps_b2[:]), inc=dv)
            dvmark_b2 = dv.v
        dv_i = dv.v
        p.wait("scalar", dv, dv_i)
        p.wait("scalar", st, st_prev)
        p.op("scalar", lambda e, _i=i: e.activation(out=g0s[:], in_=mixed[:, 1024 * _i:1024 * (_i + 1)],
                                                    func=AF.Silu), inc=ac)
        ac_i = ac.v
        p.wait("gpsimd", ac, ac_i)
        p.dma("gpsimd", io["g0"][128 * i:128 * (i + 1), :], g0s[:], inc=st)
        st_prev = st.v

    p.wait("gpsimd", st)
    p.wait("gpsimd", ld)
    p.wait("gpsimd", lw)


H0 = 16
SEG = 512
NSEG = 4


def build_attn0(p, io, nseg=NSEG, upto=99):
    nc = p.nc
    identb_c = nc.inline_tensor(np.eye(128, dtype=BF), name="a0_idbc").ap()
    id16_c = nc.inline_tensor(np.eye(16, dtype=np.float32), name="a0_id16c").ap()
    idf_c = nc.inline_tensor(np.eye(128, dtype=np.float32), name="a0_idfc").ap()
    onesb_c = nc.inline_tensor(np.ones((1, 128), dtype=BF), name="a0_onesbc").ap()

    xnTs = nc.alloc_sbuf_tensor("a0_xnT", [128, 4 * 1024], BF16)
    Wqs = nc.alloc_sbuf_tensor("a0_Wq", [128, 8 * 1024], BF16)
    Wks = nc.alloc_sbuf_tensor("a0_Wk", [128, 8 * 1024], BF16)
    Wvs = nc.alloc_sbuf_tensor("a0_Wv", [128, 8 * 1024], BF16)
    Wos = nc.alloc_sbuf_tensor("a0_Wo", [128, 8 * 1024], BF16)
    qh = nc.alloc_sbuf_tensor("a0_qh", [64, H0 * SEG], BF16)
    kh = nc.alloc_sbuf_tensor("a0_kh", [64, H0 * SEG], BF16)
    vh = nc.alloc_sbuf_tensor("a0_vh", [128, H0 * SEG], BF16)
    sqh = nc.alloc_sbuf_tensor("a0_sqh", [64, H0 * SEG], BF16)
    ekh = nc.alloc_sbuf_tensor("a0_ekh", [64, H0 * SEG], BF16)
    vtok = nc.alloc_sbuf_tensor("a0_vtok", [128, H0 * 4 * 64], BF16)
    sqtok = nc.alloc_sbuf_tensor("a0_sqtok", [128, H0 * 4 * 64], BF16)
    PT = [nc.alloc_sbuf_tensor("a0_PT0", [128, 4 * SEG], BF16),
          nc.alloc_sbuf_tensor("a0_PT1", [128, 4 * SEG], BF16)]
    mem = nc.alloc_sbuf_tensor("a0_mem", [64, H0 * 64], F32)
    memb = nc.alloc_sbuf_tensor("a0_memb", [64, 64], BF16)
    zcs = nc.alloc_sbuf_tensor("a0_zc", [64, H0], F32)
    zcseg = nc.alloc_sbuf_tensor("a0_zcseg", [64, H0], F32)
    gate = nc.alloc_sbuf_tensor("a0_gate", [64, H0], F32)
    acol = nc.alloc_sbuf_tensor("a0_acol", [64, H0], F32)
    acolT = nc.alloc_sbuf_tensor("a0_acolT", [16, 64], F32)
    gcT = nc.alloc_sbuf_tensor("a0_gcT", [16, 64], F32)
    G2all = nc.alloc_sbuf_tensor("a0_G2all", [128, H0 * 64], F32)
    G1sb = nc.alloc_sbuf_tensor("a0_G1sb", [128, 64], F32)
    ssum = nc.alloc_sbuf_tensor("a0_ssum", [128, H0 * 4], F32)
    rcol = nc.alloc_sbuf_tensor("a0_rcol", [128, H0 * 4], F32)
    mstat = nc.alloc_sbuf_tensor("a0_mst", [128, 16], F32)
    linv = nc.alloc_sbuf_tensor("a0_linv", [128, 4], F32)
    mstatb = nc.alloc_sbuf_tensor("a0_mstb", [128, 4], BF16)
    nrow = nc.alloc_sbuf_tensor("a0_nrow", [1, SEG], BF16)
    sedump = nc.alloc_sbuf_tensor("a0_sed", [128, 512], BF16)
    onesb = nc.alloc_sbuf_tensor("a0_onesb", [1, 128], BF16)
    idb = nc.alloc_sbuf_tensor("a0_idb", [128, 128], BF16)
    id16 = nc.alloc_sbuf_tensor("a0_id16", [16, 16], F32)
    idf = nc.alloc_sbuf_tensor("a0_idf", [128, 128], F32)
    bet = nc.alloc_sbuf_tensor("a0_bet", [16, 64], F32)
    t1s = nc.alloc_sbuf_tensor("a0_t1s", [128, 64], F32)
    t2s = nc.alloc_sbuf_tensor("a0_t2s", [128, 64], F32)
    atts = nc.alloc_sbuf_tensor("a0_atts", [128, 64], BF16)
    xs = nc.alloc_sbuf_tensor("a0_xs", [128, 1024], F32)
    g0s = nc.alloc_sbuf_tensor("a0_g0s", [128, 1024], BF16)
    x1s = nc.alloc_sbuf_tensor("a0_x1s", [128, 1024], F32)
    tmwo = nc.alloc_sbuf_tensor("a0_tmwo", [128, 512], F32)

    ps_sc = [nc.alloc_psum_tensor("a0_psA", [128, 512], F32),
             nc.alloc_psum_tensor("a0_psB", [128, 512], F32)]
    ps_an = nc.alloc_psum_tensor("a0_psAN", [128, 512], F32)
    ps_mem = nc.alloc_psum_tensor("a0_psM", [64, 64], F32)
    ps_tr = [nc.alloc_psum_tensor("a0_psT0", [128, 128], BF16),
             nc.alloc_psum_tensor("a0_psT1", [128, 128], BF16)]
    ps_wo = nc.alloc_psum_tensor("a0_psW", [128, 512], F32)
    ps_sc.append(ps_wo)
    ps_g = nc.alloc_psum_tensor("a0_psG", [128, 128], F32)

    ld = p.sem("a0_ld")
    pe = p.sem("a0_pe")
    dv = p.sem("a0_dv")
    ac = p.sem("a0_ac")
    gp = p.sem("a0_gp")
    st = p.sem("a0_st")

    for Wsb, key in ((Wqs, "Wq"), (Wks, "Wk"), (Wvs, "Wv"), (Wos, "Wo")):
        p.dma("gpsimd", Wsb[:].rearrange("p (c f) -> p c f", c=8),
              io[key].rearrange("(c p) f -> p c f", p=128), inc=ld)
    p.dma("sync", idb[:], identb_c, inc=ld)
    p.dma("sync", id16[:], id16_c, inc=ld)
    p.dma("sync", idf[:], idf_c, inc=ld)
    p.dma("sync", onesb[:], onesb_c, inc=ld)
    p.dma("sync", bet[:], io["betas"], inc=ld)
    ld0 = ld.v
    wc = None
    if "W_out" in io:
        wbs = [nc.alloc_sbuf_tensor("a0_wb%d" % k, [128, 1024], BF16) for k in range(2)]
        wc = p.sem("a0_wc")
        wt = p.sem("a0_wt")
        ws = p.sem("a0_ws")
        p.dma("gpsimd", io["scr_wout"], io["W_out"], inc=wc)
        p.dma("gpsimd", io["scr_win"], io["W_in"], inc=wc)

    p.wait("tensor", ld, ld0)
    p.op("tensor", lambda e: e.transpose(ps_g[0:64, 0:16], bet[:], id16[:]), inc=pe)
    p.wait("scalar", pe)
    p.op("scalar", lambda e: e.activation(out=gate[:], in_=ps_g[0:64, 0:16], func=AF.Sigmoid), inc=ac)
    p.wait("vector", ac)
    p.fop("vector", lambda e: e.tensor_scalar(acol[:], gate[:], -1.0, 1.0, ALU.mult, ALU.add), dv)
    p.wait("tensor", dv)
    p.op("tensor", lambda e: e.transpose(ps_g[0:16, 0:64], acol[:], idf[0:64, 0:64]), inc=pe)
    p.wait("vector", pe)
    p.op("vector", lambda e: e.tensor_copy(gcT[:], ps_g[0:16, 0:64]), inc=dv)
    dv_gcT = dv.v
    p.wait("tensor", dv, dv_gcT)
    pmark_g = 0
    for h in range(H0):
        p.wait("tensor", dv, pmark_g)
        p.op("tensor", lambda e, _h=h: e.matmul(ps_g[:, 0:64],
                                                bass.AP(id16, _h, [[16, 16], [0, 128]]),
                                                gcT[:], start=True, stop=True), inc=pe)
        p.wait("vector", pe)
        p.op("vector", lambda e, _h=h: e.tensor_copy(G2all[:, 64 * _h:64 * (_h + 1)], ps_g[:, 0:64]), inc=dv)
        pmark_g = dv.v
    p.op("vector", lambda e: e.memset(mem[:], 0.0))
    p.op("vector", lambda e: e.memset(zcs[:], 0.0), inc=dv)

    mk_sc = [(0, 0), (0, 0), (0, 0)]
    mk_attd = 0
    mk_num = 0
    pt_rd = [0, 0]
    hs_acPT = [0, 0]
    hs_acl = [0, 0]
    mk_mem = 0
    mk_tr = [(0, "dv"), (0, "dv")]
    tr_i = 0
    mk_wo = 0
    mk_g1 = 0
    GEMS = ((Wqs, qh), (Wks, kh), (Wvs, vh))
    xnt_free_prev = 0

    for n in range(nseg):
        if n == 2 and wc is not None:
            p.wait("sync", wc, 16)
            for k in range(16):
                if k >= 2:
                    p.wait("sync", ws, 16 * (k - 1))
                p.dma("sync", wbs[k % 2][:], io["scr_wout"][:, 128 * k:128 * (k + 1)], transpose=True, inc=wt)
                p.wait("gpsimd", wt, 16 * (k + 1))
                p.dma("gpsimd", io["scr_woutT"][128 * k:128 * (k + 1), :], wbs[k % 2][:], inc=ws)
        pe_wo_prev = pe.v
        p.wait("sync", pe, xnt_free_prev)
        p.dma("sync", xnTs[:], io["xnT"][:, 4096 * n:4096 * (n + 1)], inc=ld)
        ld_seg = ld.v
        p.wait("tensor", ld, ld_seg)
        for gi, (Wsb, dst) in enumerate(GEMS):
            for ft in range(8):
                bk = (8 * gi + ft) % 3
                p.wait("tensor", dv, mk_sc[bk][0])
                p.wait("tensor", ac, mk_sc[bk][1])
                for c in range(KC):
                    p.op("tensor", lambda e, _c=c, _f=ft, _W=Wsb, _n=n, _b=bk: e.matmul(
                        ps_sc[_b][:], _W[:, _c * 1024 + 128 * _f:_c * 1024 + 128 * (_f + 1)],
                        ap3(xnTs, 128 * _c, [[1024, 4], [1, 128]]),
                        start=(_c == 0), stop=(_c == KC - 1)), inc=pe if c == KC - 1 else None)
                pe_mm = pe.v
                p.wait("vector", pe, pe_mm)
                p.op("vector", lambda e, _f=ft, _d=dst, _b2=bk: e.tensor_copy(
                    bass.AP(_d, 2 * _f, [[H0 * SEG, 64], [SEG, 16], [16, 32]]),
                    bass.AP(ps_sc[_b2], 0, [[512, 64], [32, 16], [1, 32]])), inc=dv)
                p.wait("scalar", pe, pe_mm)
                p.op("scalar", lambda e, _f=ft, _d=dst, _b2=bk: e.activation(
                    out=bass.AP(_d, 2 * _f + 1, [[H0 * SEG, 64], [SEG, 16], [16, 32]]),
                    in_=bass.AP(ps_sc[_b2], 64 * 512, [[512, 64], [32, 16], [1, 32]]),
                    func=AF.Copy), inc=ac)
                mk_sc[bk] = (dv.v, ac.v)
        xnt_free_prev = pe.v
        dv_qkv = dv.v
        ac_qkv = ac.v
        if upto < 2:
            if "dbg_qh" in io:
                p.wait("gpsimd", dv, dv_qkv)
                p.wait("gpsimd", ac, ac_qkv)
                p.dma("gpsimd", io["dbg_qh"], qh[:], inc=st)
                p.dma("gpsimd", io["dbg_kh"], kh[:], inc=st)
                p.dma("gpsimd", io["dbg_vh"], vh[0:64, :], inc=st)
            continue

        p.wait("vector", dv, dv_qkv)
        p.wait("vector", ac, ac_qkv)
        p.fop("vector", lambda e: e.tensor_scalar_min(sqh[:], kh[:], 0.0), dv)
        p.wait("scalar", dv)
        p.op("scalar", lambda e: e.activation(out=sqh[:], in_=sqh[:], func=AF.Exp), inc=ac)
        p.wait("vector", ac)
        p.fop("vector", lambda e: e.scalar_tensor_tensor(ekh[:], kh[:], 0.0, sqh[:], ALU.max, ALU.add), dv)
        dv_ek = dv.v
        p.fop("vector", lambda e: e.tensor_reduce(zcseg[:], bass.AP(ekh, 0, [[H0 * SEG, 64], [SEG, H0], [1, SEG]]),
                                                  axis=AX.X, op=ALU.add), dv)
        p.fop("vector", lambda e: e.tensor_add(zcs[:], zcs[:], zcseg[:]), dv)
        if upto < 3:
            if "dbg_qh" in io:
                p.wait("gpsimd", dv, dv.v)
                p.dma("gpsimd", io["dbg_qh"], sqh[:], inc=st)
                p.dma("gpsimd", io["dbg_kh"], ekh[:], inc=st)
            continue

        p.wait("tensor", dv, dv_qkv)
        p.wait("tensor", ac, ac_qkv)
        for h in range(H0):
            for lt in range(4):
                p.wait("tensor", ac if mk_tr[tr_i][1] == "ac" else dv, mk_tr[tr_i][0])
                p.op("tensor", lambda e, _h=h, _l=lt, _t=tr_i: e.transpose(
                    ps_tr[_t][:, 0:64], vh[0:64, SEG * _h + 128 * _l:SEG * _h + 128 * (_l + 1)],
                    bass.AP(idb, 0, [[128, 64], [1, 64]])), inc=pe)
                p.wait("scalar", pe)
                p.op("scalar", lambda e, _h=h, _l=lt, _t=tr_i: e.activation(
                    out=vtok[:, (4 * _h + _l) * 64:(4 * _h + _l + 1) * 64], in_=ps_tr[_t][:, 0:64],
                    func=AF.Copy), inc=ac)
                mk_tr[tr_i] = (ac.v, "ac")
                tr_i ^= 1
        p.fop("vector", lambda e: e.tensor_scalar_min(ekh[:], qh[:], 0.0), dv)
        p.wait("scalar", dv)
        p.op("scalar", lambda e: e.activation(out=ekh[:], in_=ekh[:], func=AF.Exp), inc=ac)
        p.wait("vector", ac)
        p.fop("vector", lambda e: e.scalar_tensor_tensor(sqh[:], qh[:], 0.0, ekh[:], ALU.max, ALU.add), dv)
        dv_elu = dv.v
        p.fop("vector", lambda e: e.reciprocal(acol[:], zcs[:]), dv)
        p.fop("vector", lambda e: e.tensor_mul(acol[:], acol[:], gate[:]), dv)
        dv_acol = dv.v
        p.wait("tensor", dv, dv_acol)
        p.op("tensor", lambda e: e.transpose(ps_g[0:16, 0:64], acol[:], idf[0:64, 0:64]), inc=pe)
        p.wait("vector", pe)
        p.fop("vector", lambda e: e.tensor_copy(acolT[:], ps_g[0:16, 0:64]), dv)
        dv_acolT = dv.v
        p.wait("tensor", dv, dv_elu)
        for h in range(H0):
            for lt in range(4):
                p.wait("tensor", ac if mk_tr[tr_i][1] == "ac" else dv, mk_tr[tr_i][0])
                p.op("tensor", lambda e, _h=h, _l=lt, _t=tr_i: e.transpose(
                    ps_tr[_t][:, 0:64], sqh[0:64, SEG * _h + 128 * _l:SEG * _h + 128 * (_l + 1)],
                    bass.AP(idb, 0, [[128, 64], [1, 64]])), inc=pe)
                p.wait("scalar", pe)
                p.op("scalar", lambda e, _h=h, _l=lt, _t=tr_i: e.activation(
                    out=sqtok[:, (4 * _h + _l) * 64:(4 * _h + _l + 1) * 64], in_=ps_tr[_t][:, 0:64],
                    func=AF.Copy, accum_out=ssum[:, 4 * _h + _l:4 * _h + _l + 1]), inc=ac)
                mk_tr[tr_i] = (ac.v, "ac")
                tr_i ^= 1
        ac_tok = ac.v
        p.wait("vector", ac, ac_tok)
        p.fop("vector", lambda e: e.reciprocal(rcol[:], ssum[:]), dv)

        if upto < 4:
            if "dbg_qh" in io:
                p.wait("gpsimd", ac, ac.v)
                p.dma("gpsimd", io["dbg_tok1"], sqtok[:], inc=st)
                p.dma("gpsimd", io["dbg_tok2"], vtok[:], inc=st)
            continue
        for hh in range(H0 + 1):
            if hh < H0:
                h = hh
                r = h % 2
                for s4 in range(4):
                    bk = s4 % 3
                    p.wait("tensor", dv, mk_sc[bk][0])
                    p.wait("tensor", ac, mk_sc[bk][1])
                    p.op("tensor", lambda e, _h=h, _s=s4, _b=bk: e.matmul(
                        ps_sc[_b][:], qh[:, SEG * _h + 128 * _s:SEG * _h + 128 * (_s + 1)],
                        kh[:, SEG * _h:SEG * (_h + 1)], start=True, stop=True), inc=pe)
                    pe_s = pe.v
                    p.wait("vector", pe, pe_s)
                    p.fop("vector", lambda e, _s=s4, _b=bk: e.reduce_max(mstat[:, _s:_s + 1], ps_sc[_b][:], axis=AX.X), dv)
                    p.fop("vector", lambda e, _s=s4: e.tensor_scalar_mul(mstat[:, 8 + _s:9 + _s], mstat[:, _s:_s + 1], -0.125), dv)
                    dv_b = dv.v
                    p.wait("scalar", dv, dv_b)
                    lc = (4 if r == 0 else 12) + s4
                    p.op("scalar", lambda e, _s=s4, _b=bk, _lc=lc: e.activation(
                        out=sedump[:], in_=ps_sc[_b][:], func=AF.Exp, bias=mstat[:, 8 + _s:9 + _s], scale=0.125,
                        accum_out=mstat[:, _lc:_lc + 1]), inc=ac)
                    mk_sc[bk] = (dv.v, ac.v)
                hs_acl[r] = ac.v
            if hh > 0:
                hp = hh - 1
                rp = hp % 2
                sv_ac_PT = hs_acPT[rp]
                p.wait("tensor", ac, sv_ac_PT)
                p.wait("tensor", dv, mk_attd)
                for s4 in range(4):
                    for t4 in range(4):
                        p.op("tensor", lambda e, _h=hp, _s=s4, _t=t4, _r=rp: e.matmul(
                            ps_an[:, 64 * _s:64 * (_s + 1)],
                            PT[_r][:, 512 * _t + 128 * _s:512 * _t + 128 * (_s + 1)],
                            vtok[:, (4 * _h + _t) * 64:(4 * _h + _t + 1) * 64],
                            start=(_t == 0), stop=(_t == 3)), inc=pe if (s4 == 3 and t4 == 3) else None)
                pe_attd = pe.v
                pt_rd[rp] = pe_attd
                p.wait("tensor", dv, mk_mem)
                for lt in range(4):
                    p.op("tensor", lambda e, _h=hp, _l=lt: e.matmul(
                        ps_mem[:], sqtok[:, (4 * _h + _l) * 64:(4 * _h + _l + 1) * 64],
                        vtok[:, (4 * _h + _l) * 64:(4 * _h + _l + 1) * 64],
                        start=(_l == 0), stop=(_l == 3)), inc=pe if lt == 3 else None)
                pe_mem = pe.v
                p.wait("vector", pe, pe_mem)
                p.fop("vector", lambda e, _h=hp: e.tensor_add(mem[:, 64 * _h:64 * (_h + 1)],
                                                              mem[:, 64 * _h:64 * (_h + 1)], ps_mem[:]), dv)
                p.fop("vector", lambda e, _h=hp: e.tensor_copy(memb[:], mem[:, 64 * _h:64 * (_h + 1)]), dv)
                dv_memb = dv.v
                mk_mem = dv_memb
                p.wait("tensor", dv, max(dv_acolT, mk_g1))
                p.op("tensor", lambda e, _h=hp: e.matmul(ps_g[:, 0:64],
                                                         bass.AP(id16, _h, [[16, 16], [0, 128]]),
                                                         acolT[:], start=True, stop=True), inc=pe)
                pe_g1 = pe.v
                p.wait("vector", pe, pe_g1)
                p.fop("vector", lambda e: e.tensor_copy(G1sb[:], ps_g[:, 0:64]), dv)
                mk_g1 = dv.v
            if hh < H0:
                p.fop("vector", lambda e: e.tensor_scalar_mul(mstatb[:], mstat[:, 8:12], 8.0), dv)
                dv_nm = dv.v
                for s4 in range(4):
                    p.wait("tensor", dv, dv_nm)
                    p.wait("tensor", ac if mk_tr[tr_i][1] == "ac" else dv, mk_tr[tr_i][0])
                    p.op("tensor", lambda e, _s=s4, _t=tr_i: e.transpose(
                        ps_tr[_t][0:1, 0:128], mstatb[:, _s:_s + 1], idb[:]), inc=pe)
                    p.wait("vector", pe)
                    p.op("vector", lambda e, _s=s4, _t=tr_i: e.tensor_copy(
                        nrow[0:1, 128 * _s:128 * (_s + 1)], ps_tr[_t][0:1, 0:128]), inc=dv)
                    mk_tr[tr_i] = (dv.v, "dv")
                    tr_i ^= 1
                dv_nrow = dv.v
                for t4 in range(4):
                    bk = t4 % 3
                    p.wait("tensor", dv, mk_sc[bk][0])
                    p.wait("tensor", ac, mk_sc[bk][1])
                    p.wait("tensor", dv, dv_nrow)
                    p.op("tensor", lambda e, _h=h, _t=t4, _b=bk: e.matmul(
                        ps_sc[_b][:], kh[:, SEG * _h + 128 * _t:SEG * _h + 128 * (_t + 1)],
                        qh[:, SEG * _h:SEG * (_h + 1)], start=True, stop=False))
                    p.op("tensor", lambda e, _b=bk: e.matmul(
                        ps_sc[_b][:], onesb[:], nrow[:], start=False, stop=True), inc=pe)
                    pe_s2 = pe.v
                    p.wait("scalar", pe, pe_s2)
                    p.wait("scalar", pe, pt_rd[r])
                    p.op("scalar", lambda e, _t=t4, _b=bk, _r=r: e.activation(
                        out=PT[_r][:, 512 * _t:512 * (_t + 1)], in_=ps_sc[_b][:], func=AF.Exp, scale=0.125), inc=ac)
                    mk_sc[bk] = (dv.v, ac.v)
                hs_acPT[r] = ac.v
                if "dbg_PT" in io and n == 0 and h == 0:
                    p.wait("gpsimd", ac, ac.v)
                    p.dma("gpsimd", io["dbg_PT"], PT[0][:], inc=st)
                    p.dma("gpsimd", io["dbg_mst"], mstat[:], inc=st)
                    p.dma("gpsimd", io["dbg_nrow"], nrow[:], inc=st)
            if hh > 0:
                p.wait("tensor", dv, dv_memb)
                p.wait("tensor", dv, mk_num)
                for s4 in range(4):
                    p.op("tensor", lambda e, _h=hp, _s=s4: e.matmul(
                        ps_an[:, 256 + 64 * _s:256 + 64 * (_s + 1)],
                        sqh[:, SEG * _h + 128 * _s:SEG * _h + 128 * (_s + 1)],
                        memb[:], start=True, stop=True), inc=pe if s4 == 3 else None)
                pe_num = pe.v
                p.wait("vector", pe, pe_attd)
                p.wait("vector", pe, pe_num)
                p.wait("vector", ac, hs_acl[rp])
                lc0 = 4 if rp == 0 else 12
                p.fop("vector", lambda e, _l=lc0: e.reciprocal(linv[:], mstat[:, _l:_l + 4]), dv)
                for s4 in range(4):
                    p.fop("vector", lambda e, _h=hp, _s=s4: e.scalar_tensor_tensor(
                        t1s[:], ps_an[:, 256 + 64 * _s:256 + 64 * (_s + 1)], rcol[:, 4 * _h + _s:4 * _h + _s + 1],
                        G1sb[:], ALU.mult, ALU.mult), dv)
                    p.fop("vector", lambda e, _h=hp, _s=s4: e.scalar_tensor_tensor(
                        t2s[:], ps_an[:, 64 * _s:64 * (_s + 1)], linv[:, _s:_s + 1],
                        G2all[:, 64 * _h:64 * (_h + 1)], ALU.mult, ALU.mult), dv)
                    p.fop("vector", lambda e: e.tensor_add(atts[:], t1s[:], t2s[:]), dv)
                    dv_att = dv.v
                    if "dbg_PT" in io and n == 0 and hp == 0 and s4 == 0:
                        dbst1 = nc.alloc_sbuf_tensor("a0_dbst1", [128, 64], F32)
                        dbst2 = nc.alloc_sbuf_tensor("a0_dbst2", [128, 64], F32)
                        p.op("vector", lambda e: e.tensor_copy(dbst1[:], ps_an[:, 0:64]))
                        p.op("vector", lambda e: e.tensor_copy(dbst2[:], ps_an[:, 256:320]), inc=dv)
                        p.wait("gpsimd", dv)
                        p.dma("gpsimd", io["dbg_t1"], t1s[:], inc=st)
                        p.dma("gpsimd", io["dbg_t2"], t2s[:], inc=st)
                        p.dma("gpsimd", io["dbg_attd"], dbst1[:], inc=st)
                        p.dma("gpsimd", io["dbg_num"], dbst2[:], inc=st)
                        p.dma("gpsimd", io["dbg_G1"], G1sb[:], inc=st)
                        p.dma("gpsimd", io["dbg_G2"], G2all[:, 0:64], inc=st)
                        p.dma("gpsimd", io["dbg_rc"], rcol[:], inc=st)
                        p.dma("gpsimd", io["dbg_atts"], atts[:], inc=st)
                        p.wait("vector", st, st.v)
                    p.wait("tensor", dv, dv_att)
                    p.wait("tensor", ac if mk_tr[tr_i][1] == "ac" else dv, mk_tr[tr_i][0])
                    p.wait("tensor", pe, pe_wo_prev)
                    p.op("tensor", lambda e, _t=tr_i: e.transpose(ps_tr[_t][0:64, :], atts[:], idb[:]), inc=pe)
                    p.wait("scalar", pe)
                    p.op("scalar", lambda e, _h=hp, _s=s4, _t=tr_i: e.activation(
                        out=vh[0:64, SEG * _h + 128 * _s:SEG * _h + 128 * (_s + 1)], in_=ps_tr[_t][0:64, :],
                        func=AF.Copy))
                    if s4 == 0:
                        p.op("scalar", lambda e, _h=hp, _t=tr_i: e.activation(
                            out=vh[64:128, SEG * _h:SEG * _h + 127], in_=ps_tr[_t][0:64, 1:128],
                            func=AF.Copy), inc=ac)
                    else:
                        p.op("scalar", lambda e, _h=hp, _s=s4, _t=tr_i: e.activation(
                            out=vh[64:128, SEG * _h + 128 * _s - 1:SEG * _h + 128 * (_s + 1) - 1],
                            in_=ps_tr[_t][0:64, :], func=AF.Copy), inc=ac)
                    mk_tr[tr_i] = (ac.v, "ac")
                    tr_i ^= 1
                mk_attd = dv.v
                mk_num = dv.v
        dv_attT = dv.v
        ac_attT = ac.v
        if upto < 7:
            if "dbg_qh" in io:
                p.wait("gpsimd", dv, dv_attT)
                p.wait("gpsimd", ac, ac_attT)
                p.dma("gpsimd", io["dbg_qh"], vh[0:64, :], inc=st)
            continue

        st_x1_prev = st.v
        for pt in range(4):
            i_glob = 4 * n + pt
            p.dma("sync", xs[:], io["x"][128 * i_glob:128 * (i_glob + 1), :], inc=ld)
            p.dma("sync", g0s[:], io["g0"][128 * i_glob:128 * (i_glob + 1), :], inc=ld)
            ld_t = ld.v
            for nh in range(2):
                p.wait("tensor", dv, max(dv_attT, mk_wo, mk_sc[2][0]))
                p.wait("tensor", ac, max(ac_attT, mk_sc[2][1]))
                for rr in range(8):
                    p.op("tensor", lambda e, _p=pt, _n=nh, _r=rr: e.matmul(
                        ps_wo[:], bass.AP(vh, SEG * 4 * _p + 2 * _r, [[H0 * SEG, 128], [SEG, 4], [16, 32]]),
                        Wos[:, 1024 * _r + 512 * _n:1024 * _r + 512 * (_n + 1)],
                        start=(_r == 0), stop=(_r == 7)), inc=pe if rr == 7 else None)
                pe_wo = pe.v
                p.wait("vector", pe, pe_wo)
                p.wait("vector", ld, ld_t)
                p.wait("vector", st, st_x1_prev)
                p.fop("vector", lambda e, _n=nh: e.tensor_mul(tmwo[:], ps_wo[:], g0s[:, 512 * _n:512 * (_n + 1)]), dv)
                p.op("vector", lambda e, _n=nh: e.tensor_add(x1s[:, 512 * _n:512 * (_n + 1)],
                                                             xs[:, 512 * _n:512 * (_n + 1)], tmwo[:]), inc=dv)
                mk_wo = dv.v
            dv_x1 = dv.v
            p.wait("gpsimd", dv, dv_x1)
            p.dma("gpsimd", io["x1"][128 * i_glob:128 * (i_glob + 1), :], x1s[:], inc=st)
            st_x1_prev = st.v
            p.wait("sync", dv, dv_x1)
            mk_sc[2] = (mk_wo, mk_sc[2][1])

    p.wait("gpsimd", st)
    p.wait("gpsimd", ld)
    if wc is not None:
        p.wait("gpsimd", wc, 32)
        p.wait("gpsimd", ws, 16 * 16)



NH = 32
DST = 128
HD = 64
DIN = 2048
QQ = 128
CH = 512
NCH = T // CH
OPK = ["ps0", "ps1", "psA", "psB"]
LNEPS = 1.0e-5
RMSEPS = 1.0e-5
A_FLOOR = 1.0e-4
TWO_PI = float(2.0 * np.pi)
INV_2PI = float(1.0 / (2.0 * np.pi))
MAGIC = 12582912.0


def build_l1a(p, io, nch=NCH, upto=99):
    nc = p.nc
    idb_c = nc.inline_tensor(np.eye(128, dtype=BF), name="l1_idbc").ap()
    idf_c = nc.inline_tensor(np.eye(128, dtype=np.float32), name="l1_idfc").ap()
    sel32_c = nc.inline_tensor(np.eye(32, dtype=np.float32), name="l1_sel32c").ap()
    _t1 = np.where(np.tril(np.ones((128, 128)), -1) > 0, -1e30, 0.0).astype(BF)
    _t2 = np.where(np.tril(np.ones((128, 128)), 0) > 0, -1e30, 0.0).astype(BF)
    tri1_c = nc.inline_tensor(_t1, name="l1_tri1c").ap()
    tri2_c = nc.inline_tensor(_t2, name="l1_tri2c").ap()
    sel32b_c = nc.inline_tensor(np.eye(32, dtype=BF), name="l1_sel32bc").ap()
    onesf_c = nc.inline_tensor(np.ones((1, 128), dtype=np.float32), name="l1_onesfc").ap()

    A = lambda *a, **k: nc.alloc_sbuf_tensor(*a, **k)
    WinT = A("l1_WinT", [128, 8 * 4480], BF16)
    xnTs = A("l1_xnT", [128, 4 * 1024], BF16)
    zxTs = A("l1_zx", [128, 32 * 512], BF16)
    xDg = A("l1_xDg", [128, 16 * 512], BF16)
    Xtok = A("l1_Xtok", [128, 2 * 2048], BF16)
    wot = [A("l1_wot0", [128, 1024], BF16), A("l1_wot1", [128, 1024], BF16)]
    dts = A("l1_dts", [32, 512], F32)
    ars = A("l1_ars", [32, 512], F32)
    trs = A("l1_trs", [32, 512], F32)
    rps = A("l1_rps", [32, 512], F32)
    dtv = A("l1_dtv", [32, 512], F32)
    gam = A("l1_gam", [32, 512], F32)
    adt = A("l1_adt", [32, 512], F32)
    ssc = A("l1_ssc", [32, 512], F32)
    lna = A("l1_lna", [32, 512], F32)
    lng = A("l1_lng", [32, 520], F32)
    angs = A("l1_angs", [32, 512], F32)
    tmp32 = A("l1_t32a", [32, 512], F32)
    angc = A("l1_angc", [32, 1], F32)
    scl1 = A("l1_scl1", [32, 8], F32)
    bct = A("l1_bct", [128, 4 * 256], BF16)
    bcd = A("l1_bcd", [128, 2 * 512], BF16)
    cw = [A("l1_cw0", [128, 128], BF16), A("l1_cw1", [128, 128], BF16)]
    bv = [A("l1_bv0", [128, 128], BF16), A("l1_bv1", [128, 128], BF16)]
    msk1 = [A("l1_msk1a", [128, 128], BF16), A("l1_msk1b", [128, 128], BF16)]
    msk2 = [A("l1_msk2a", [128, 128], BF16), A("l1_msk2b", [128, 128], BF16)]
    cmask = [A("l1_cmaska", [128, 128], BF16), A("l1_cmaskb", [128, 128], BF16)]
    mskd = [A("l1_mskda", [128, 128], BF16), A("l1_mskdb", [128, 128], BF16)]
    ypre = A("l1_ypre", [128, 128], BF16)
    h_sb = A("l1_h", [128, NH * 64], F32)
    h_bb = [A("l1_hbba", [128, 64], BF16), A("l1_hbbb", [128, 64], BF16)]
    v1T = A("l1_v1T", [128, 32], F32)
    v2T = A("l1_v2T", [128, 32], F32)
    expSm = A("l1_expSm", [32, 128], F32)
    expST = A("l1_expST", [128, 32], F32)
    estot = A("l1_estot", [128, 32], F32)
    esrow = A("l1_esrow", [1, 32], F32)
    wT = A("l1_wT", [128, 32], F32)
    Dbc = A("l1_Dbc", [128, 32], F32)
    nwe = {nm: A("l1_nw" + nm, [128, 32], F32) for nm in ("Be", "Bo", "Ce", "Co")}
    rowb = A("l1_rowb", [1, 128], BF16)
    xprow = A("l1_xprow", [1, 2048], BF16)
    xpS = A("l1_xpS", [1, 2048], BF16)
    xpS2 = A("l1_xpS2", [1, 2048], BF16)
    srow = A("l1_srow", [1, 32], F32)
    srow2 = A("l1_srow2", [1, 32], F32)
    drow = A("l1_drow", [1, 32], F32)
    ctk = A("l1_ctk", [128, 4 * 32], F32)
    stk = A("l1_stk", [128, 4 * 32], F32)
    rc = {nm: A("l1_rc" + nm, [128, 32], F32)
          for nm in ("cEB", "sOB", "sEB", "cOB", "cEC", "sOC", "sEC", "cOC")}
    bcst = A("l1_bcst", [128, 8], F32)
    vcol = [A("l1_vcola", [128, 1], F32), A("l1_vcolb", [128, 1], F32)]
    epscol = A("l1_eps", [128, 1], F32)
    onecol = A("l1_one", [128, 1], F32)
    nwcol = A("l1_nwc", [128, 1], F32)
    nwcol2 = A("l1_nwc2", [128, 1], F32)
    idb = A("l1_idb", [128, 128], BF16)
    idf = A("l1_idf", [128, 128], F32)
    sel32 = A("l1_sel32", [32, 32], F32)
    tri1 = A("l1_tri1", [128, 128], BF16)
    tri2 = A("l1_tri2", [128, 128], BF16)
    sel32b = A("l1_sel32b", [32, 32], BF16)
    stil = A("l1_stil", [32, 128], F32)
    shib = A("l1_shib", [32, 128], BF16)
    slob = A("l1_slob", [32, 128], BF16)
    onesf = A("l1_onesf", [1, 128], F32)
    dtbc = A("l1_dtbc", [32, 1], F32)
    dsk = A("l1_dsk", [32, 1], F32)
    lnx = A("l1_lnx", [128, 1024], F32)
    lnst = A("l1_lnst", [128, 8], F32)
    bcf = lnx
    dmp = A("l1_dmp", [128, 512], F32)
    xnbh = A("l1_xnbh", [128, 512], BF16)
    dmpb = A("l1_dmpb", [128, 512], BF16)
    dmpb2 = A("l1_dmpb2", [128, 512], BF16)

    ps0 = nc.alloc_psum_tensor("l1_ps0", [128, 512], F32)
    ps1 = nc.alloc_psum_tensor("l1_ps1", [128, 512], F32)
    psA = nc.alloc_psum_tensor("l1_psA", [128, 512], F32)
    psB = nc.alloc_psum_tensor("l1_psB", [128, 512], F32)
    psS = nc.alloc_psum_tensor("l1_psS", [128, 128], F32)
    psY0 = nc.alloc_psum_tensor("l1_psY0", [128, 128], F32)
    psY1 = nc.alloc_psum_tensor("l1_psY1", [128, 128], F32)
    ps_trb = nc.alloc_psum_tensor("l1_pstrb", [128, 128], BF16)
    ps_gf = psA
    pss = {"ps0": ps0, "ps1": ps1, "psA": psA, "psB": psB,
           "psS": psS, "psY0": psY0, "psY1": psY1, "ps_trb": ps_trb}

    ld = p.sem("l1_ld")
    lw = p.sem("l1_lw")
    pe = p.sem("l1_pe")
    dv = p.sem("l1_dv")
    ac = p.sem("l1_ac")
    gp = p.sem("l1_gp")
    st = p.sem("l1_st")

    mk = {k: (0, dv) for k in pss}
    mk["ps_gf"] = (0, dv)

    def pwait(eng, key):
        v, s = mk[key]
        p.wait(eng, s, v)

    def pset(key, sem):
        mk[key] = (sem.v, sem)

    p.dma("sync", idb[:], idb_c, inc=ld)
    p.dma("sync", idf[:], idf_c, inc=ld)
    p.dma("sync", sel32[:], sel32_c, inc=ld)
    p.dma("sync", tri1[:], tri1_c, inc=ld)
    p.dma("sync", tri2[:], tri2_c, inc=ld)
    p.dma("sync", sel32b[:], sel32b_c, inc=ld)
    p.dma("sync", onesf[:], onesf_c, inc=ld)
    p.dma("sync", dtbc[:], io["dt_bias"][:, None], inc=ld)
    p.dma("sync", dsk[:], io["D_skip"][:, None], inc=ld)
    for c in range(8):
        p.dma("sync", WinT[:, 4480 * c:4480 * (c + 1)], io["scr_win"][:, 128 * c:128 * (c + 1)],
              transpose=True, inc=lw)

    if "dbg_o" in io:
        p.wait("gpsimd", st, st.v)
        p.dma("gpsimd", io["dbg_o"], io["scr_woutT"][0:128, :], inc=st)
    p.wait("tensor", ld, ld.v)
    p.op("tensor", lambda e: e.transpose(ps_gf[0:1, 0:32], dsk[:], idf[0:32, 0:32]), inc=pe)
    p.wait("vector", pe)
    p.fop("vector", lambda e: e.tensor_copy(drow[:], ps_gf[0:1, 0:32]), dv)
    p.wait("tensor", dv)
    p.op("tensor", lambda e: e.matmul(ps_gf[:, 0:32], onesf[:], drow[:], start=True, stop=True), inc=pe)
    p.wait("vector", pe)
    p.fop("vector", lambda e: e.tensor_copy(Dbc[:], ps_gf[:, 0:32]), dv)
    for nm, src_ap in (("Be", io["B_norm_w"][0:64:2][None, :]), ("Bo", io["B_norm_w"][1:64:2][None, :]),
                       ("Ce", io["C_norm_w"][0:64:2][None, :]), ("Co", io["C_norm_w"][1:64:2][None, :])):
        p.wait("sync", pe, pe.v)
        p.dma("sync", drow[:], src_ap, inc=ld, allow_slow_non_contiguous=True)
        p.wait("tensor", ld, ld.v)
        p.wait("tensor", dv, dv.v)
        p.op("tensor", lambda e: e.matmul(ps_gf[:, 0:32], onesf[:], drow[:], start=True, stop=True), inc=pe)
        p.wait("vector", pe)
        p.fop("vector", lambda e, _n=nm: e.tensor_copy(nwe[_n][:], ps_gf[:, 0:32]), dv)
    p.dma("sync", ssc[0:32, 0:128], io["B_bias"], inc=ld)
    p.dma("sync", lna[0:32, 0:128], io["C_bias"], inc=ld)
    p.dma("sync", nwcol[:], io["B_norm_w"][:, None], inc=ld)
    p.dma("sync", nwcol2[:], io["C_norm_w"][:, None], inc=ld)
    p.wait("vector", ld, ld.v)
    p.fop("vector", lambda e: e.tensor_mul(ssc[0:32, 0:128], ssc[0:32, 0:128], lna[0:32, 0:128]), dv)
    p.wait("tensor", dv)
    p.op("tensor", lambda e: e.transpose(ps_gf[:, 0:32], ssc[0:32, 0:128], idf[0:32, 0:32]), inc=pe)
    p.wait("vector", pe)
    p.fop("vector", lambda e: e.tensor_copy(wT[:], ps_gf[:, 0:32]), dv)
    p.fop("vector", lambda e: e.tensor_mul(nwcol[:], nwcol[:], nwcol2[:]), dv)
    p.op("vector", lambda e: e.memset(nwcol[0:64, :], 1.0), inc=dv)
    p.wait("vector", dv)
    p.fop("vector", lambda e: e.tensor_scalar_mul(wT[:], wT[:], nwcol[:]), dv)
    p.op("vector", lambda e: e.memset(epscol[:], RMSEPS), inc=dv)
    p.op("vector", lambda e: e.memset(onecol[:], 1.0), inc=dv)
    p.op("vector", lambda e: e.memset(h_sb[:], 0.0), inc=dv)
    p.op("vector", lambda e: e.memset(angc[:], 0.0), inc=dv)
    p.op("vector", lambda e: e.memset(xprow[:], 0.0), inc=dv)
    p.op("vector", lambda e: e.memset(rowb[:], 0.0), inc=dv)
    p.op("vector", lambda e: e.memset(lng[0:32, 512:520], -30.0), inc=dv)
    pset("psA", dv)

    M1, R1, SU, SS, SC, NG, EPC = 0, 1, 2, 3, 4, 5, 6
    p.op("vector", lambda e: e.memset(lnst[:, EPC:EPC + 1], LNEPS), inc=dv)
    st_ln = {0: 0, 1: 0}
    dv_lnx_prev = 0
    ac_lnx_prev = 0
    pe_xnbh_prev = 0
    for i in range(NT):
        p.wait("sync", dv, dv_lnx_prev)
        p.wait("sync", ac, ac_lnx_prev)
        p.dma("sync", lnx[:], io["x1"][128 * i:128 * (i + 1), :], inc=ld)
        ld_i = ld.v
        p.wait("vector", ld, ld_i)
        p.fop("vector", lambda e: e.reduce_sum(lnst[:, SU:SU + 1], lnx[:], axis=AX.X), dv)
        p.wait("scalar", ld, ld_i)
        p.wait("scalar", dv, dv.v)
        p.op("scalar", lambda e: e.activation(out=dmp[:], in_=lnx[:, 0:512], func=AF.Square,
                                              accum_out=lnst[:, SS:SS + 1]))
        p.op("scalar", lambda e: e.activation(out=dmp[:], in_=lnx[:, 512:1024], func=AF.Square,
                                              accum_out=lnst[:, 7:8]), inc=ac)
        ac_lnx_prev = ac.v
        p.wait("vector", ac)
        p.op("vector", lambda e: e.tensor_scalar_mul(lnst[:, M1:M1 + 1], lnst[:, SU:SU + 1], 1.0 / D))
        p.fop("vector", lambda e: e.tensor_add(lnst[:, SC:SC + 1], lnst[:, SS:SS + 1], lnst[:, 7:8]), dv)
        p.op("vector", lambda e: e.tensor_scalar_mul(lnst[:, SC:SC + 1], lnst[:, SC:SC + 1], 1.0 / D))
        p.fop("vector", lambda e: e.tensor_scalar_mul(lnst[:, NG:NG + 1], lnst[:, M1:M1 + 1], -1.0), dv)
        p.fop("vector", lambda e: e.scalar_tensor_tensor(lnst[:, SC:SC + 1], lnst[:, M1:M1 + 1],
                                                         lnst[:, NG:NG + 1], lnst[:, SC:SC + 1],
                                                         ALU.mult, ALU.add), dv)
        p.wait("scalar", dv)
        p.op("scalar", lambda e: e.activation(out=lnst[:, R1:R1 + 1], in_=lnst[:, SC:SC + 1], func=AF.Ln,
                                              bias=lnst[:, EPC:EPC + 1], scale=1.0), inc=ac)
        p.wait("scalar", ac)
        p.op("scalar", lambda e: e.activation(out=lnst[:, R1:R1 + 1], in_=lnst[:, R1:R1 + 1], func=AF.Exp,
                                              scale=-0.5), inc=ac)
        p.wait("vector", ac)
        for half in range(2):
            db = (dmpb, dmpb2)[half]
            p.wait("vector", pe, pe_xnbh_prev)
            p.fop("vector", lambda e, _h=half: e.tensor_scalar(
                xnbh[:], lnx[:, 512 * _h:512 * (_h + 1)], lnst[:, M1:M1 + 1], lnst[:, R1:R1 + 1],
                ALU.subtract, ALU.mult), dv)
            dv_xn = dv.v
            if half == 1:
                dv_lnx_prev = dv_xn
            p.wait("tensor", dv, dv_xn)
            for k in range(4):
                pwait("tensor", "ps_trb")
                p.op("tensor", lambda e, _k=k: e.transpose(ps_trb[:, 0:128], xnbh[:, 128 * _k:128 * (_k + 1)],
                                                            idb[:]), inc=pe)
                p.wait("vector", pe)
                p.wait("vector", st, st_ln[half])
                p.op("vector", lambda e, _k=k, _d=db: e.tensor_copy(_d[:, 128 * _k:128 * (_k + 1)], ps_trb[:, 0:128]),
                     inc=dv)
                pset("ps_trb", dv)
            pe_xnbh_prev = pe.v
            p.wait("gpsimd", dv, dv.v)
            p.dma("gpsimd", io["xn1T"][:, 1024 * i + 512 * half:1024 * i + 512 * (half + 1)], db[:], inc=st)
            st_ln[half] = st.v
    if upto < 1:
        p.wait("gpsimd", st)
        return

    lw_wt = lw.v
    p.wait("tensor", lw, lw_wt)
    dv_gated_prev = 0
    pe_op_prev = 0
    gp_xd_prev = 0
    pemark_wb = {0: 0, 1: 0}
    mark_cw = [0, 0]
    mark_msk = [0, 0]
    mark_cmask = [0, 0]
    mark_mskd = [0, 0]
    mark_hbb = [0, 0]
    mark_bv = [0, 0]
    mark_vcol = [0, 0]
    mark_ypre = 0
    mk_xtok = {0: (0, dv), 1: (0, dv)}
    xtok_ev = {0: (dv, 0), 1: (dv, 0)}
    ld_carry = ld.v
    for ci in range(nch):
        p.wait("sync", pe, pe.v)
        p.dma("sync", xnTs[:], io["xn1T"][:, 4096 * ci:4096 * (ci + 1)], inc=ld)
        ld_ch = ld.v

        p.wait("tensor", ld, ld_ch)
        for mt in range(32):
            key = "ps0" if mt % 2 == 0 else "ps1"
            pb = pss[key]
            pwait("tensor", key)
            for c in range(KC):
                p.op("tensor", lambda e, _c=c, _m=mt, _pb=pb: e.matmul(
                    _pb[:], WinT[:, 4480 * _c + 128 * _m:4480 * _c + 128 * (_m + 1)],
                    ap3(xnTs, 128 * _c, [[1024, 4], [1, 128]]),
                    start=(_c == 0), stop=(_c == KC - 1)), inc=pe if c == KC - 1 else None)
            pe_mm = pe.v
            p.wait("scalar", pe, pe_mm)
            p.wait("scalar", dv, dv_gated_prev)
            p.wait("scalar", gp, gp_xd_prev)
            p.op("scalar", lambda e, _m=mt, _pb=pb: e.activation(
                out=zxTs[:, 512 * _m:512 * (_m + 1)], in_=_pb[:],
                func=(AF.Silu if _m < 16 else AF.Copy)), inc=ac)
            mk[key] = (ac.v, ac)
        ac_zx = ac.v

        for j, dst in ((0, dts), (1, ars), (2, trs), (3, rps)):
            key = "psA" if j % 2 == 0 else "psB"
            pb = pss[key]
            pwait("tensor", key)
            for c in range(KC):
                p.op("tensor", lambda e, _c=c, _j=j, _pb=pb: e.matmul(
                    _pb[0:32, :], WinT[:, 4480 * _c + 4352 + 32 * _j:4480 * _c + 4352 + 32 * (_j + 1)],
                    ap3(xnTs, 128 * _c, [[1024, 4], [1, 128]]),
                    start=(_c == 0), stop=(_c == KC - 1)), inc=pe if c == KC - 1 else None)
            pe_mm = pe.v
            p.wait("vector", pe, pe_mm)
            p.fop("vector", lambda e, _d=dst, _pb=pb: e.tensor_copy(_d[:], _pb[0:32, :]), dv)
            mk[key] = (dv.v, dv)

        p.fop("vector", lambda e: e.tensor_scalar(tmp32[:], dts[:], dtbc[:], None, ALU.add), dv)
        p.fop("vector", lambda e: e.scalar_tensor_tensor(gam[:], tmp32[:], -1.0, tmp32[:], ALU.mult, ALU.max), dv)
        p.wait("scalar", dv)
        p.op("scalar", lambda e: e.activation(out=gam[:], in_=gam[:], func=AF.Exp, scale=-1.0), inc=ac)
        p.wait("scalar", ac)
        p.op("scalar", lambda e: e.activation(out=gam[:], in_=gam[:], func=AF.Ln,
                                              bias=onecol[0:32, 0:1], scale=1.0), inc=ac)
        p.wait("vector", ac)
        p.fop("vector", lambda e: e.scalar_tensor_tensor(dtv[:], tmp32[:], 0.0, gam[:], ALU.max, ALU.add), dv)
        p.fop("vector", lambda e: e.scalar_tensor_tensor(gam[:], ars[:], -1.0, ars[:], ALU.mult, ALU.max), dv)
        p.wait("scalar", dv)
        p.op("scalar", lambda e: e.activation(out=gam[:], in_=gam[:], func=AF.Exp, scale=-1.0), inc=ac)
        p.wait("scalar", ac)
        p.op("scalar", lambda e: e.activation(out=gam[:], in_=gam[:], func=AF.Ln,
                                              bias=onecol[0:32, 0:1], scale=1.0), inc=ac)
        p.wait("vector", ac)
        p.fop("vector", lambda e: e.scalar_tensor_tensor(gam[:], ars[:], 0.0, gam[:], ALU.max, ALU.add), dv)
        p.fop("vector", lambda e: e.tensor_scalar_add(gam[:], gam[:], A_FLOOR), dv)
        p.fop("vector", lambda e: e.tensor_mul(adt[:], dtv[:], gam[:]), dv)
        p.fop("vector", lambda e: e.tensor_scalar_mul(adt[:], adt[:], -1.0), dv)
        p.op("vector", lambda e: e.memset(tmp32[:], 1.0), inc=dv)
        p.wait("vector", dv)
        p.fop("vector", lambda e: e.tensor_tensor_scan(ssc[:], tmp32[:], adt[:], 0.0, ALU.mult, ALU.add), dv)
        p.wait("scalar", dv)
        p.op("scalar", lambda e: e.activation(out=trs[:], in_=trs[:], func=AF.Sigmoid), inc=ac)
        p.wait("vector", ac)
        p.fop("vector", lambda e: e.tensor_scalar(tmp32[:], trs[:], -0.5, 1.0, ALU.mult, ALU.add), dv)
        p.fop("vector", lambda e: e.tensor_mul(tmp32[:], tmp32[:], dtv[:]), dv)
        p.wait("scalar", dv)
        p.op("scalar", lambda e: e.activation(out=lna[:, 0:512], in_=tmp32[:], func=AF.Ln), inc=ac)
        p.wait("vector", ac)
        p.fop("vector", lambda e: e.tensor_mul(gam[:], trs[:], dtv[:]), dv)
        p.fop("vector", lambda e: e.tensor_scalar_mul(gam[:], gam[:], 0.5), dv)
        p.wait("scalar", dv)
        p.op("scalar", lambda e: e.activation(out=lng[:, 0:512], in_=gam[:], func=AF.Ln), inc=ac)
        p.wait("vector", ac)
        p.fop("vector", lambda e: e.tensor_sub(lna[:, 0:512], lna[:, 0:512], ssc[:]), dv)
        p.fop("vector", lambda e: e.tensor_sub(lng[:, 0:512], lng[:, 0:512], ssc[:]), dv)
        p.op("vector", lambda e: e.memset(tmp32[:], 1.0), inc=dv)
        p.wait("vector", dv)
        p.fop("vector", lambda e: e.tensor_tensor_scan(angs[:], tmp32[:], rps[:], angc[:], ALU.mult, ALU.add), dv)
        p.fop("vector", lambda e: e.tensor_copy(angc[:], angs[:, 511:512]), dv)
        p.fop("vector", lambda e: e.tensor_scalar_mul(tmp32[:], angs[:], INV_2PI), dv)
        p.fop("vector", lambda e: e.tensor_scalar(dtv[:], tmp32[:], MAGIC, MAGIC, ALU.add, ALU.subtract), dv)
        p.fop("vector", lambda e: e.tensor_sub(dtv[:], tmp32[:], dtv[:]), dv)
        p.fop("vector", lambda e: e.tensor_scalar_mul(dtv[:], dtv[:], TWO_PI), dv)
        p.wait("scalar", dv)
        p.op("scalar", lambda e: e.activation(out=dts[:], in_=dtv[:], func=AF.Sin), inc=ac)
        p.fop("vector", lambda e: e.tensor_scalar_add(tmp32[:], tmp32[:], 0.25), dv)
        p.fop("vector", lambda e: e.tensor_scalar(dtv[:], tmp32[:], MAGIC, MAGIC, ALU.add, ALU.subtract), dv)
        p.fop("vector", lambda e: e.tensor_sub(dtv[:], tmp32[:], dtv[:]), dv)
        p.fop("vector", lambda e: e.tensor_scalar_mul(dtv[:], dtv[:], TWO_PI), dv)
        p.wait("scalar", dv)
        p.op("scalar", lambda e: e.activation(out=ars[:], in_=dtv[:], func=AF.Sin), inc=ac)
        ac_cs = ac.v

        p.wait("tensor", ac, ac_cs)
        p.wait("vector", ld, ld_carry)
        for tt in range(4):
            for dst_, src_ in ((ctk, ars), (stk, dts)):
                pwait("tensor", "psA")
                p.op("tensor", lambda e, _t=tt, _s=src_: e.transpose(
                    ps_gf[:, 0:32], _s[:, 128 * _t:128 * (_t + 1)], idf[0:32, 0:32]), inc=pe)
                p.wait("vector", pe)
                p.fop("vector", lambda e, _t=tt, _d=dst_: e.tensor_copy(_d[:, 32 * _t:32 * (_t + 1)],
                                                                        ps_gf[:, 0:32]), dv)
                pset("psA", dv)
            key = "psA" if tt % 2 == 0 else "psB"
            pb = pss[key]
            pwait("tensor", key)
            for c in range(KC):
                p.op("tensor", lambda e, _c=c, _t=tt, _pb=pb: e.matmul(
                    _pb[:, 0:256], xnTs[:, 1024 * _t + 128 * _c:1024 * _t + 128 * (_c + 1)],
                    WinT[:, 4480 * _c + 4096:4480 * _c + 4352],
                    start=(_c == 0), stop=(_c == KC - 1)), inc=pe if c == KC - 1 else None)
            pe_bc = pe.v
            p.wait("scalar", pe, pe_bc)
            p.op("scalar", lambda e, _pb=pb: e.activation(out=dmp[:, 0:128], in_=_pb[:, 0:128],
                                                          func=AF.Square, accum_out=bcst[:, 0:1]))
            p.op("scalar", lambda e, _pb=pb: e.activation(out=dmp[:, 128:256], in_=_pb[:, 128:256],
                                                          func=AF.Square, accum_out=bcst[:, 1:2]), inc=ac)
            p.wait("vector", ac)
            p.fop("vector", lambda e: e.tensor_scalar_mul(bcst[:, 2:4], bcst[:, 0:2], 1.0 / 128.0), dv)
            p.wait("scalar", dv)
            p.op("scalar", lambda e: e.activation(out=bcst[:, 2:4], in_=bcst[:, 2:4], func=AF.Ln,
                                                  bias=epscol[:, 0:1], scale=1.0), inc=ac)
            p.wait("scalar", ac)
            p.op("scalar", lambda e: e.activation(out=bcst[:, 2:4], in_=bcst[:, 2:4], func=AF.Exp,
                                                  scale=-0.5), inc=ac)
            p.wait("vector", ac)
            p.fop("vector", lambda e, _pb=pb: e.tensor_scalar_mul(bcf[:, 0:128], _pb[:, 0:128],
                                                                  bcst[:, 2:3]), dv)
            p.fop("vector", lambda e, _pb=pb: e.tensor_scalar_mul(bcf[:, 128:256], _pb[:, 128:256],
                                                                  bcst[:, 3:4]), dv)
            mk[key] = (dv.v, dv)
            for dstc, a_, b_ in (("cEB", ctk, "Be"), ("sOB", stk, "Bo"), ("sEB", stk, "Be"), ("cOB", ctk, "Bo"),
                                 ("cEC", ctk, "Ce"), ("sOC", stk, "Co"), ("sEC", stk, "Ce"), ("cOC", ctk, "Co")):
                p.op("vector", lambda e, _d=dstc, _a=a_, _b=b_, _t=tt: e.tensor_mul(
                    rc[_d][:], _a[:, 32 * _t:32 * (_t + 1)], nwe[_b][:]))
            p.fop("vector", lambda e: e.engine_nop(), dv)
            for half, (ce, so, se, co) in ((0, ("cEB", "sOB", "sEB", "cOB")), (1, ("cEC", "sOC", "sEC", "cOC"))):
                base = 128 * half
                ev = bass.AP(bcf, base, [[1024, 128], [2, 32]])
                od = bass.AP(bcf, base + 1, [[1024, 128], [2, 32]])
                t_ev = bass.AP(bct, 256 * tt + base, [[4 * 256, 128], [2, 32]])
                t_od = bass.AP(bct, 256 * tt + base + 1, [[4 * 256, 128], [2, 32]])
                p.fop("vector", lambda e, _e=ev, _c=ce: e.tensor_mul(dmp[:, 0:32], _e, rc[_c][:]), dv)
                p.fop("vector", lambda e, _o=od, _s=so: e.tensor_mul(dmp[:, 32:64], _o, rc[_s][:]), dv)
                p.fop("vector", lambda e, _t=t_ev: e.tensor_sub(_t, dmp[:, 0:32], dmp[:, 32:64]), dv)
                p.fop("vector", lambda e, _e=ev, _s=se: e.tensor_mul(dmp[:, 0:32], _e, rc[_s][:]), dv)
                p.fop("vector", lambda e, _o=od, _c=co: e.tensor_mul(dmp[:, 32:64], _o, rc[_c][:]), dv)
                p.fop("vector", lambda e, _t=t_od: e.tensor_add(_t, dmp[:, 0:32], dmp[:, 32:64]), dv)
                p.fop("vector", lambda e, _t=tt, _b=base: e.tensor_copy(
                    bct[:, 256 * _t + _b + 64:256 * _t + _b + 128], bcf[:, _b + 64:_b + 128]), dv)
            dv_bct = dv.v
            for half in range(2):
                pwait("tensor", "ps_trb")
                p.wait("tensor", dv, dv_bct)
                p.op("tensor", lambda e, _t=tt, _h=half: e.transpose(
                    ps_trb[:, 0:128], bct[:, 256 * _t + 128 * _h:256 * _t + 128 * (_h + 1)], idb[:]), inc=pe)
                p.wait("vector", pe)
                p.op("vector", lambda e, _t=tt, _h=half: e.tensor_copy(
                    bcd[:, 512 * _h + 128 * _t:512 * _h + 128 * (_t + 1)], ps_trb[:, 0:128]), inc=dv)
                pset("ps_trb", dv)
            ht = tt // 2
            if tt % 2 == 0:
                v0h, s0h = mk_xtok[ht]
                p.wait("tensor", s0h, v0h)
            for it in range(8 * (tt % 2), 8 * (tt % 2) + 8):
                pwait("tensor", "ps_trb")
                p.op("tensor", lambda e, _t=ht, _i=it: e.transpose(
                    ps_trb[:, 0:128], zxTs[:, 512 * (16 + _i) + 128 * _t:512 * (16 + _i) + 128 * (_t + 1)],
                    idb[:]), inc=pe)
                p.wait("scalar", pe)
                p.op("scalar", lambda e, _s=ht, _i=it: e.activation(
                    out=Xtok[:, 2048 * _s + 128 * _i:2048 * _s + 128 * (_i + 1)], in_=ps_trb[:, 0:128],
                    func=AF.Copy), inc=ac)
                pset("ps_trb", ac)
            xtok_ev[ht] = (ac, ac.v)
        dv_bcd = dv.v

        p.wait("gpsimd", pe, pe_op_prev)
        p.wait("gpsimd", ac, ac_zx)
        for it in range(16):
            p.op("gpsimd", lambda e, _i=it: e.tensor_scalar_mul(
                xDg[0:64, 512 * _i:512 * (_i + 1)], zxTs[0:64, 512 * (16 + _i):512 * (17 + _i)],
                Dbc[0:64, 2 * _i:2 * _i + 1]))
            p.op("gpsimd", lambda e, _i=it: e.tensor_scalar_mul(
                xDg[64:128, 512 * _i:512 * (_i + 1)], zxTs[64:128, 512 * (16 + _i):512 * (17 + _i)],
                Dbc[64:128, 2 * _i + 1:2 * _i + 2]), inc=gp)
        gp_xd = gp.v
        gp_xd_prev = gp_xd

        if upto < 5:
            if "dbg_a" in io and ci == 0:
                p.wait("gpsimd", dv, dv.v)
                p.wait("gpsimd", ac, ac.v)
                p.dma("gpsimd", io["dbg_a"], ssc[:], inc=st)
                p.dma("gpsimd", io["dbg_b"], lna[:, 0:512], inc=st)
                p.dma("gpsimd", io["dbg_c"], lng[:, 0:512], inc=st)
                p.dma("gpsimd", io["dbg_d"], angs[:], inc=st)
                p.dma("gpsimd", io["dbg_e"], bct[:], inc=st)
                p.dma("gpsimd", io["dbg_f"], Xtok[:, 0:2048], inc=st)
                p.dma("gpsimd", io["dbg_g"], zxTs[:, 0:512], inc=st)
                p.dma("gpsimd", io["dbg_h"], bcd[:], inc=st)
                p.dma("gpsimd", io["dbg_i"], dts[:], inc=st)
                p.dma("gpsimd", io["dbg_j"], ars[:], inc=st)
                p.dma("gpsimd", io["dbg_k"], ctk[:], inc=st)
                p.dma("gpsimd", io["dbg_l"], stk[:], inc=st)
                p.dma("gpsimd", io["dbg_m"], rc["cEB"][:], inc=st)
                p.dma("gpsimd", io["dbg_n"], bcf[:, 0:256], inc=st)
            continue

        ld_xprow_hist = {}
        for tt in range(4):
            sl = tt % 2
            if tt >= 2:
                p.wait("tensor", ac, ac_zx)
                p.wait("tensor", ld, ld_carry)
                v0, s0_ = mk_xtok[sl]
                p.wait("tensor", s0_, v0)
                for it in range(16):
                    pwait("tensor", "ps_trb")
                    p.op("tensor", lambda e, _t=tt, _i=it: e.transpose(
                        ps_trb[:, 0:128], zxTs[:, 512 * (16 + _i) + 128 * _t:512 * (16 + _i) + 128 * (_t + 1)],
                        idb[:]), inc=pe)
                    p.wait("scalar", pe)
                    p.op("scalar", lambda e, _s=sl, _i=it: e.activation(
                        out=Xtok[:, 2048 * _s + 128 * _i:2048 * _s + 128 * (_i + 1)], in_=ps_trb[:, 0:128],
                        func=AF.Copy), inc=ac)
                    pset("ps_trb", ac)
                xtok_ev[sl] = (ac, ac.v)
            xe_sem, xe_val = xtok_ev[sl]
            if tt == 0:
                p.fop("vector", lambda e, _t=tt: e.tensor_copy(stil[:], ssc[:, 128 * _t:128 * (_t + 1)]), dv)
            else:
                p.fop("vector", lambda e, _t=tt: e.tensor_scalar(
                    stil[:], ssc[:, 128 * _t:128 * (_t + 1)], ssc[:, 128 * _t - 1:128 * _t], None,
                    ALU.subtract), dv)
            p.fop("vector", lambda e: e.tensor_copy(shib[:], stil[:]), dv)
            p.fop("vector", lambda e: e.tensor_sub(slob[:], stil[:], shib[:]), dv)
            p.wait("scalar", dv)
            p.op("scalar", lambda e: e.activation(out=expSm[:], in_=stil[:], func=AF.Exp), inc=ac)
            p.wait("tensor", ac)
            pwait("tensor", "psA")
            p.op("tensor", lambda e: e.transpose(ps_gf[:, 0:32], expSm[:], idf[0:32, 0:32]), inc=pe)
            p.wait("vector", pe)
            p.fop("vector", lambda e: e.tensor_copy(expST[:], ps_gf[:, 0:32]), dv)
            pset("psA", dv)
            p.fop("vector", lambda e: e.tensor_copy(scl1[0:32, 0:1], expSm[:, 127:128]), dv)
            p.wait("tensor", dv)
            pwait("tensor", "psA")
            p.op("tensor", lambda e: e.transpose(ps_gf[0:1, 0:32], scl1[0:32, 0:1], idf[0:32, 0:32]), inc=pe)
            p.wait("vector", pe)
            p.fop("vector", lambda e: e.tensor_copy(esrow[:], ps_gf[0:1, 0:32]), dv)
            p.wait("tensor", dv)
            p.op("tensor", lambda e: e.matmul(ps_gf[:, 0:32], onesf[:], esrow[:], start=True, stop=True), inc=pe)
            p.wait("vector", pe)
            p.fop("vector", lambda e: e.tensor_copy(estot[:], ps_gf[:, 0:32]), dv)
            pset("psA", dv)
            p.wait("vector", ac, ac.v)
            for dst_, src_t, off in ((v1T, lna, 0), (v2T, lng, 1)):
                if tt == 0:
                    p.fop("vector", lambda e, _s=src_t, _o=off, _t=tt: e.tensor_copy(
                        stil[:], _s[:, 128 * _t + _o:128 * _t + _o + 128]), dv)
                else:
                    p.fop("vector", lambda e, _s=src_t, _o=off, _t=tt: e.tensor_scalar(
                        stil[:], _s[:, 128 * _t + _o:128 * _t + _o + 128],
                        ssc[:, 128 * _t - 1:128 * _t], None, ALU.add), dv)
                pwait("tensor", "psA")
                p.wait("tensor", dv, dv.v)
                p.op("tensor", lambda e: e.transpose(ps_gf[:, 0:32], stil[:], idf[0:32, 0:32]), inc=pe)
                p.wait("vector", pe)
                p.fop("vector", lambda e, _d=dst_: e.tensor_copy(_d[:], ps_gf[:, 0:32]), dv)
                pset("psA", dv)
            if tt > 0:
                p.wait("scalar", dv, dv_bcd)
                p.wait("scalar", pe, pe.v)
                p.dma("scalar", rowb[:], bct[127:128, 256 * (tt - 1):256 * (tt - 1) + 128], inc=ld)
                pxe_sem, pxe_val = xtok_ev[(tt - 1) % 2]
                p.wait("scalar", pxe_sem, pxe_val)
                p.dma("scalar", xprow[:], Xtok[127:128, 2048 * ((tt - 1) % 2):2048 * ((tt - 1) % 2) + 2048], inc=ld)
            ld_rows = ld.v
            p.fop("vector", lambda e, _t=tt: e.tensor_copy(scl1[0:32, 1:2], adt[:, 128 * _t:128 * _t + 1]), dv)
            p.wait("scalar", dv)
            p.op("scalar", lambda e: e.activation(out=scl1[0:32, 2:3], in_=scl1[0:32, 1:2], func=AF.Exp,
                                                  scale=-1.0), inc=ac)
            p.wait("vector", ac)
            p.fop("vector", lambda e, _t=tt: e.tensor_mul(scl1[0:32, 3:4], gam[:, 128 * _t:128 * _t + 1],
                                                          scl1[0:32, 2:3]), dv)
            p.fop("vector", lambda e: e.tensor_mul(scl1[0:32, 4:5], scl1[0:32, 3:4], expSm[:, 127:128]), dv)
            dv_scol = dv.v
            for cidx, dstrow in ((3, srow), (4, srow2)):
                p.wait("tensor", dv, dv_scol)
                pwait("tensor", "psA")
                p.op("tensor", lambda e, _c=cidx: e.transpose(ps_gf[0:1, 0:32], scl1[0:32, _c:_c + 1],
                                                              idf[0:32, 0:32]), inc=pe)
                p.wait("vector", pe)
                p.fop("vector", lambda e, _d=dstrow: e.tensor_copy(_d[:], ps_gf[0:1, 0:32]), dv)
                pset("psA", dv)
            p.wait("vector", ld, ld_rows)
            p.wait("vector", pe, pe.v)
            p.fop("vector", lambda e: e.tensor_mul(xpS2[:], xprow[:],
                                                   bass.AP(srow, 0, [[32, 1], [1, 32], [0, 64]])), dv)
            p.fop("vector", lambda e: e.tensor_mul(xpS[:], xprow[:],
                                                   bass.AP(srow2, 0, [[32, 1], [1, 32], [0, 64]])), dv)
            dv_rows = dv.v

            hsM = {}
            at = {}
            def emit_a_tail():
                p.wait("vector", pe, at["pe_sp"])
                p.wait("vector", gp, at["gp_cm"])
                p.wait("vector", pe, mark_mskd[at["r"]])
                p.op("vector", lambda e, _r=at["r"], _ps=at["psp"]: e.tensor_mul(
                    mskd[_r][:], _ps[:, 128:256] if _r == 1 else _ps[:, 0:128],
                    cmask[_r][:]), inc=dv)
                mk[at["key_sp"]] = (dv.v, dv)
                mark_cmask[at["r"]] = dv.v
                dv_mskd = dv.v
                p.wait("vector", pe, at["pe_cor"])
                p.wait("vector", pe, mark_hbb[at["r"]])
                p.op("vector", lambda e, _h=at["h"], _r=at["r"], _pY=at["psY"]: e.tensor_add(
                    h_bb[_r][:], _pY[:, 64:128], h_sb[:, 64 * _h:64 * (_h + 1)]), inc=dv)
                dv_hbb = dv.v
                p.wait("vector", pe, mark_bv[at["r"]])
                p.wait("vector", ac, at["ac_vcol"])
                p.op("vector", lambda e, _t=tt, _r=at["r"]: e.tensor_scalar_mul(
                    bv[_r][:], bct[:, 256 * _t:256 * _t + 128], vcol[_r][:]), inc=dv)
                dv_bv = dv.v
                mark_vcol[at["r"]] = dv_bv
                hsM[at["h"]] = dict(dv_mskd=dv_mskd, dv_hbb=dv_hbb, dv_bv=dv_bv, r=at["r"],
                                    key_pY=at["key_pY"], psY=at["psY"], key_hu=at["key_hu"])
                at["done"] = True
            for hh in range(NH + 1):
                if hh < NH:
                    h = hh
                    r = h % 2
                    key_pA = "psA" if r == 0 else "psB"
                    pA = pss[key_pA]
                    psY = psY0 if r == 0 else psY1
                    key_pY = "psY0" if r == 0 else "psY1"
                    key_hu = "ps0" if r == 0 else "ps1"
                    p.wait("scalar", pe, mark_cw[r])
                    p.wait("scalar", dv, dv_bcd)
                    p.op("scalar", lambda e, _h=h, _t=tt, _r=r: e.activation(
                        out=cw[_r][:], in_=bcd[:, 512 + 128 * _t:512 + 128 * (_t + 1)], func=AF.Copy,
                        scale=wT[:, _h:_h + 1]), inc=ac)
                    ac_cw = ac.v
                    pwait("tensor", key_pA)
                    p.op("tensor", lambda e, _h=h, _pb=pA: e.matmul(
                        _pb[:, 0:128], bass.AP(sel32b, _h, [[32, 32], [0, 128]]),
                        shib[:], start=True, stop=False))
                    p.op("tensor", lambda e, _h=h, _pb=pA: e.matmul(
                        _pb[:, 0:128], bass.AP(sel32b, _h, [[32, 32], [0, 128]]),
                        slob[:], start=False, stop=False))
                    p.op("tensor", lambda e, _pb=pA: e.matmul(
                        _pb[:, 0:128], idb[:], tri1[:], start=False, stop=True))
                    p.op("tensor", lambda e, _h=h, _pb=pA: e.matmul(
                        _pb[:, 128:256], bass.AP(sel32b, _h, [[32, 32], [0, 128]]),
                        shib[:], start=True, stop=False))
                    p.op("tensor", lambda e, _h=h, _pb=pA: e.matmul(
                        _pb[:, 128:256], bass.AP(sel32b, _h, [[32, 32], [0, 128]]),
                        slob[:], start=False, stop=False))
                    p.op("tensor", lambda e, _pb=pA: e.matmul(
                        _pb[:, 128:256], idb[:], tri2[:], start=False, stop=True), inc=pe)
                    pe_arg = pe.v
                    p.wait("scalar", pe, pe_arg)
                    p.wait("scalar", gp, mark_msk[r])
                    p.wait("scalar", dv, dv_rows)
                    p.op("scalar", lambda e, _h=h, _r=r, _pb=pA: e.activation(
                        out=msk1[_r][:], in_=_pb[:, 0:128], func=AF.Exp, bias=v1T[:, _h:_h + 1], scale=1.0))
                    p.op("scalar", lambda e, _h=h, _r=r, _pb=pA: e.activation(
                        out=msk2[_r][:], in_=_pb[:, 128:256], func=AF.Exp, bias=v2T[:, _h:_h + 1], scale=1.0), inc=ac)
                    ac_msk = ac.v
                    mk[key_pA] = (ac.v, ac)
                    p.wait("gpsimd", ac, ac_msk)
                    p.wait("gpsimd", dv, mark_cmask[r])
                    p.op("gpsimd", lambda e, _r=r: e.tensor_add(cmask[_r][:], msk1[_r][:], msk2[_r][:]), inc=gp)
                    gp_cm = gp.v
                    mark_msk[r] = gp_cm
                    p.wait("scalar", gp, gp_cm)
                    p.wait("scalar", dv, mark_vcol[r])
                    p.op("scalar", lambda e, _r=r: e.activation(out=vcol[_r][:], in_=cmask[_r][:, 127:128],
                                                                func=AF.Copy), inc=ac)
                    ac_vcol = ac.v
                    key_sp = "psS" if r == 0 else "ps1"
                    psp = psS if r == 0 else ps1
                    pwait("tensor", key_sp)
                    p.wait("tensor", ac, ac_cw)
                    p.op("tensor", lambda e, _t=tt, _r=r, _ps=psp: e.matmul(
                        _ps[:, 128:256] if _r == 1 else _ps[:, 0:128],
                        bcd[:, 128 * _t:128 * (_t + 1)], cw[_r][:], start=True, stop=True), inc=pe)
                    pe_sp = pe.v
                    pwait("tensor", key_pY)
                    p.wait("tensor", dv, dv_rows)
                    p.op("tensor", lambda e, _h=h, _pY=psY: e.matmul(_pY[:, 64:128], rowb[:],
                                                                     xpS2[0:1, 64 * _h:64 * (_h + 1)],
                                                                     start=True, stop=True), inc=pe)
                    pe_cor = pe.v
                    at.update(h=h, r=r, psp=psp, key_sp=key_sp, psY=psY, key_pY=key_pY,
                              key_hu=key_hu, pe_sp=pe_sp, pe_cor=pe_cor, gp_cm=gp_cm,
                              ac_vcol=ac_vcol, done=False)
                if hh >= 1:
                    h = hh - 1
                    S = hsM[h]
                    r = S["r"]
                    psY = S["psY"]
                    phu = pss[S["key_hu"]]
                    p.wait("tensor", dv, S["dv_mskd"])
                    p.wait("tensor", dv, S["dv_hbb"])
                    p.op("tensor", lambda e, _h=h, _t=tt % 2, _r=r, _pY=psY: e.matmul(
                        _pY[:, 0:64], mskd[_r][:], Xtok[:, 2048 * _t + 64 * _h:2048 * _t + 64 * (_h + 1)],
                        start=True, stop=True))
                    p.op("tensor", lambda e, _r=r, _pY=psY: e.matmul(_pY[:, 64:128], cw[_r][:], h_bb[_r][:],
                                                                     start=True, stop=True), inc=pe)
                    pe_y = pe.v
                    mark_mskd[r] = pe_y
                    mark_hbb[r] = pe_y
                    mark_cw[r] = pe_y
                    pwait("tensor", S["key_hu"])
                    p.wait("tensor", dv, S["dv_bv"])
                    p.op("tensor", lambda e, _h=h, _t=tt % 2, _r=r, _pb=phu: e.matmul(
                        _pb[:, 0:64], bv[_r][:], Xtok[:, 2048 * _t + 64 * _h:2048 * _t + 64 * (_h + 1)],
                        start=True, stop=False))
                    p.op("tensor", lambda e, _h=h, _pb=phu: e.matmul(
                        _pb[:, 0:64], rowb[:], xpS[0:1, 64 * _h:64 * (_h + 1)], start=False, stop=True), inc=pe)
                    pe_hu = pe.v
                    mark_bv[r] = pe_hu
                    p.wait("vector", pe, pe_y)
                    p.wait("vector", pe, mark_ypre)
                    p.fop("vector", lambda e, _h=h, _pY=psY: e.tensor_copy(
                        ypre[:, 64 * (_h % 2):64 * (_h % 2) + 64], _pY[:, 0:64]), dv)
                    p.op("vector", lambda e, _h=h, _pY=psY: e.scalar_tensor_tensor(
                        ypre[:, 64 * (_h % 2):64 * (_h % 2) + 64], _pY[:, 64:128], expST[:, _h:_h + 1],
                        ypre[:, 64 * (_h % 2):64 * (_h % 2) + 64], ALU.mult, ALU.add), inc=dv)
                    dv_ypre = dv.v
                    mk[S["key_pY"]] = (dv.v, dv)
                    if hh < NH:
                        emit_a_tail()
                    p.wait("vector", pe, pe_hu)
                    p.op("vector", lambda e, _h=h, _pb=phu: e.scalar_tensor_tensor(
                        h_sb[:, 64 * _h:64 * (_h + 1)], h_sb[:, 64 * _h:64 * (_h + 1)], estot[:, _h:_h + 1],
                        _pb[:, 0:64], ALU.mult, ALU.add), inc=dv)
                    mk[S["key_hu"]] = (dv.v, dv)
                    if h % 2 == 1:
                        pr = h // 2
                        pwait("tensor", "ps_trb")
                        p.wait("tensor", dv, dv_ypre)
                        p.op("tensor", lambda e: e.transpose(ps_trb[:, 0:128], ypre[:], idb[:]), inc=pe)
                        pe_yt = pe.v
                        mark_ypre = pe_yt
                        p.wait("vector", pe, pe_yt)
                        p.wait("vector", gp, gp_xd)
                        p.fop("vector", lambda e, _p=pr, _t=tt: e.tensor_add(
                            ypre[:], ps_trb[:, 0:128], xDg[:, 512 * _p + 128 * _t:512 * _p + 128 * (_t + 1)]), dv)
                        p.op("vector", lambda e, _p=pr, _t=tt: e.tensor_mul(
                            xDg[:, 512 * _p + 128 * _t:512 * _p + 128 * (_t + 1)], ypre[:],
                            zxTs[:, 512 * _p + 128 * _t:512 * _p + 128 * (_t + 1)]), inc=dv)
                        pset("ps_trb", dv)
                if hh < NH and not at["done"]:
                    emit_a_tail()
            mk_xtok[sl] = (pe.v, pe)
            if tt == 3:
                p.wait("scalar", dv, dv.v)
                cxe_sem, cxe_val = xtok_ev[1]
                p.wait("scalar", cxe_sem, cxe_val)
                p.dma("scalar", rowb[:], bct[127:128, 256 * 3:256 * 3 + 128], inc=ld)
                p.dma("scalar", xprow[:], Xtok[127:128, 2048 * 1:2048 * 2], inc=ld)
                ld_carry = ld.v
        dv_gated = dv.v
        dv_gated_prev = dv_gated

        if upto < 7:
            if "dbg_a" in io and ci == 0:
                p.wait("gpsimd", dv, dv.v)
                p.dma("gpsimd", io["dbg_f"], xDg[:, 0:2048], inc=st)
                p.dma("gpsimd", io["dbg_h"], h_sb[:, 0:1024], inc=st)
                p.dma("gpsimd", io["dbg_g"], xDg[:, 2048:2560], inc=st)
            continue

        for g in range(2):
            for k in range(16):
                wb = wot[k % 2]
                p.wait("sync", pe, pemark_wb[k % 2])
                p.dma("sync", wb[:], io["scr_woutT"][128 * k:128 * (k + 1), :], inc=lw)
                lw_k = lw.v
                p.wait("tensor", lw, lw_k)
                p.wait("tensor", dv, dv_gated)
                for q in range(4):
                    t_loc = 2 * g + q // 2
                    dmh = q % 2
                    key = OPK[q]
                    if k == 0:
                        pwait("tensor", key)
                    p.op("tensor", lambda e, _k=k, _q=q, _t=t_loc, _d=dmh: e.matmul(
                        pss[OPK[_q]][:], xDg[:, 512 * _k + 128 * _t:512 * _k + 128 * (_t + 1)],
                        wot[_k % 2][:, 512 * _d:512 * (_d + 1)],
                        start=(_k == 0), stop=(_k == 15)), inc=pe if q == 3 else None)
                pemark_wb[k % 2] = pe.v
            pe_op = pe.v
            pe_op_prev = pe_op
            p.wait("vector", pe, pe_op)
            for q in range(4):
                t_loc = 2 * g + q // 2
                dmh = q % 2
                p.fop("vector", lambda e, _q=q: e.tensor_copy(dmp[:], pss[OPK[_q]][:]), dv)
                p.wait("gpsimd", dv, dv.v)
                p.dma("gpsimd", io["s1"][128 * (4 * ci + t_loc):128 * (4 * ci + t_loc + 1),
                                         512 * dmh:512 * (dmh + 1)], dmp[:], inc=st)
                pset(OPK[q], dv)
                p.wait("vector", st, st.v)

    p.wait("gpsimd", st)
    p.wait("gpsimd", ld)
    p.wait("gpsimd", lw)


def build_l1f(p, io):
    nc = p.nc
    onesb_c = nc.inline_tensor(np.ones((1, 128), dtype=BF), name="f5_onesbc").ap()
    onesf_c = nc.inline_tensor(np.ones((1, 128), dtype=np.float32), name="f5_onesfc").ap()
    idf_c = nc.inline_tensor(np.eye(128, dtype=np.float32), name="f5_idf").ap()
    A = lambda *a, **k: nc.alloc_sbuf_tensor(*a, **k)
    kst = A("f5_kst", [128, 8 * 1024], F32)
    wqs = A("f5_wqs", [128, 8 * 1024], BF16)
    asb = A("f5_asb", [128, 8 * 16], BF16)
    bsb = A("f5_bsb", [16, 1024], BF16)
    brow = A("f5_brow", [1, 1024], BF16)
    onesb = A("f5_onesb", [1, 128], BF16)
    onesf = A("f5_onesf", [1, 128], F32)
    xnt = A("f5_xnt", [128, 1024], BF16)
    xnt2 = A("f5_xnt2", [128, 1024], BF16)
    xaTs = A("f5_xaT", [16, 128], BF16)
    g1s = A("f5_g1s", [128, 1024], BF16)
    x1s = A("f5_x1s", [128, 1024], F32)
    s1s = A("f5_s1s", [128, 1024], F32)
    x2s = A("f5_x2s", [128, 1024], F32)
    tmps = A("f5_tmp", [128, 512], F32)
    mcol = A("f5_mcol", [128, 4], F32)
    mrow = A("f5_mrow", [1, 128], F32)
    idf128 = A("f5_idf128", [128, 128], F32)

    ps_y = nc.alloc_psum_tensor("f5_psy", [128, 512], F32)
    ps_y2 = nc.alloc_psum_tensor("f5_psy2", [128, 512], F32)
    ps_xa = nc.alloc_psum_tensor("f5_psxa", [16, 128], F32)
    ps_t = nc.alloc_psum_tensor("f5_pst", [128, 128], F32)

    ld = p.sem("f5_ld")
    pe = p.sem("f5_pe")
    dv = p.sem("f5_dv")
    ac = p.sem("f5_ac")
    st = p.sem("f5_st")

    p.dma("sync", kst[:].rearrange("p (c f) -> p c f", c=8),
          io["kernel"].rearrange("(c p) f -> p c f", p=128), inc=ld)
    p.dma("gpsimd", asb[:].rearrange("p (c r) -> p c r", c=8),
          io["a"].rearrange("(c p) r -> p c r", p=128), inc=ld)
    p.dma("gpsimd", bsb[:], io["b"], inc=ld)
    p.dma("gpsimd", brow[:], io["bias"][None, :], inc=ld)
    p.dma("sync", onesb[:], onesb_c, inc=ld)
    p.dma("sync", onesf[:], onesf_c, inc=ld)
    p.dma("sync", idf128[:], idf_c, inc=ld)
    ld0 = ld.v

    p.wait("vector", ld, ld0)
    p.fop("vector", lambda e: e.reduce_max(mcol[:, 0:1], kst[:], axis=AX.X, apply_absolute_value=True), dv)
    p.wait("tensor", dv)
    p.op("tensor", lambda e: e.transpose(ps_t[0:1, 0:128], mcol[:, 0:1], idf128[:]), inc=pe)
    p.wait("vector", pe)
    p.fop("vector", lambda e: e.tensor_copy(mrow[:], ps_t[0:1, 0:128]), dv)
    p.fop("vector", lambda e: e.reduce_max(mrow[0:1, 0:1], mrow[:], axis=AX.X), dv)
    p.wait("tensor", dv)
    p.op("tensor", lambda e: e.matmul(ps_t[:, 0:1], onesf[:], mrow[0:1, 0:1], start=True, stop=True), inc=pe)
    p.wait("vector", pe)
    p.fop("vector", lambda e: e.tensor_copy(mcol[:, 1:2], ps_t[:, 0:1]), dv)
    p.fop("vector", lambda e: e.tensor_scalar_mul(mcol[:, 2:3], mcol[:, 1:2], 1.0 / 7.0), dv)
    p.fop("vector", lambda e: e.reciprocal(mcol[:, 3:4], mcol[:, 2:3]), dv)
    for cc in range(4):
        sl = slice(2048 * cc, 2048 * (cc + 1))
        p.fop("vector", lambda e, _s=sl: e.tensor_scalar_mul(kst[:, _s], kst[:, _s], mcol[:, 3:4]), dv)
        p.fop("vector", lambda e, _s=sl: e.tensor_scalar(kst[:, _s], kst[:, _s], 12582912.0, 12582912.0,
                                                         ALU.add, ALU.subtract), dv)
        p.fop("vector", lambda e, _s=sl: e.tensor_scalar(kst[:, _s], kst[:, _s], -8.0, 7.0,
                                                         ALU.max, ALU.min), dv)
        p.fop("vector", lambda e, _s=sl: e.tensor_scalar_mul(wqs[:, _s], kst[:, _s], mcol[:, 2:3]), dv)
    dv_wq = dv.v

    st_prev = 0
    dvmark = {"y": 0, "y2": 0, "xa": 0}
    for i in range(NT):
        xt = xnt if i % 2 == 0 else xnt2
        p.wait("sync", pe, pe.v if i >= 2 else 0)
        p.wait("sync", dv, dv.v if i >= 1 else 0)
        p.dma("sync", xt[:], io["xn1T"][:, 1024 * i:1024 * (i + 1)], inc=ld)
        p.dma("sync", x1s[:], io["x1"][128 * i:128 * (i + 1), :], inc=ld)
        p.dma("sync", s1s[:], io["s1"][128 * i:128 * (i + 1), :], inc=ld)
        ld_i = ld.v
        p.wait("tensor", ld, ld_i)
        p.wait("tensor", dv, dv_wq)
        p.wait("tensor", dv, dvmark["xa"])
        for c in range(KC):
            p.op("tensor", lambda e, _c=c, _x=xt: e.matmul(
                ps_xa[:], asb[:, 16 * _c:16 * (_c + 1)], _x[:, 128 * _c:128 * (_c + 1)],
                start=(_c == 0), stop=(_c == KC - 1)), inc=pe if c == KC - 1 else None)
        pe_xa = pe.v
        p.wait("vector", pe, pe_xa)
        p.fop("vector", lambda e: e.tensor_copy(xaTs[:], ps_xa[:]), dv)
        dvmark["xa"] = dv.v
        dv_xa = dv.v
        for nh in range(2):
            pb = ps_y if nh == 0 else ps_y2
            kk = "y" if nh == 0 else "y2"
            p.wait("tensor", dv, dvmark[kk])
            p.wait("tensor", dv, dv_xa)
            for c in range(KC):
                p.op("tensor", lambda e, _c=c, _n=nh, _x=xt, _pb=pb: e.matmul(
                    _pb[:], _x[:, 128 * _c:128 * (_c + 1)], wqs[:, 1024 * _c + 512 * _n:1024 * _c + 512 * (_n + 1)],
                    start=(_c == 0), stop=False))
            p.op("tensor", lambda e, _n=nh, _pb=pb: e.matmul(
                _pb[:], onesb[:], brow[0:1, 512 * _n:512 * (_n + 1)], start=False, stop=False))
            p.op("tensor", lambda e, _n=nh, _pb=pb: e.matmul(
                _pb[:], xaTs[:], bsb[:, 512 * _n:512 * (_n + 1)], start=False, stop=True), inc=pe)
            pe_y = pe.v
            p.wait("scalar", pe, pe_y)
            p.op("scalar", lambda e, _n=nh, _pb=pb: e.activation(
                out=g1s[:, 512 * _n:512 * (_n + 1)], in_=_pb[:], func=AF.Silu), inc=ac)
            dvmark[kk] = ac.v
            mkeng = ac
        ac_g1 = ac.v
        p.wait("vector", ac, ac_g1)
        p.wait("vector", ld, ld_i)
        p.wait("vector", st, st_prev)
        for nh in range(2):
            p.fop("vector", lambda e, _n=nh: e.tensor_mul(tmps[:], s1s[:, 512 * _n:512 * (_n + 1)],
                                                          g1s[:, 512 * _n:512 * (_n + 1)]), dv)
            p.op("vector", lambda e, _n=nh: e.tensor_add(x2s[:, 512 * _n:512 * (_n + 1)],
                                                         x1s[:, 512 * _n:512 * (_n + 1)], tmps[:]), inc=dv)
        dv_x2 = dv.v
        p.wait("gpsimd", dv, dv_x2)
        p.dma("gpsimd", io["out"][128 * i:128 * (i + 1), :], x2s[:], inc=st)
        st_prev = st.v

    p.wait("gpsimd", st)
    p.wait("gpsimd", ld)

CORES = list(range(8))


def _mk_io(nc, specs):
    io = {}
    for name, shape, dt_, kind in specs:
        io[name] = nc.dram_tensor(name, shape, dt_, kind=kind).ap()
    return io


def _barrier(p, bar, scr):
    sv, sp, fs = scr
    tgt = bar.v + 5
    bar.v = tgt
    p.q["vector"].append(lambda e, _b=bar: e.memset(sv[:], 0.0).then_inc(_b.h, 1))
    p.q["scalar"].append(lambda e, _b=bar: e.memzero(fs[:]).then_inc(_b.h, 1))
    p.q["gpsimd"].append(lambda e, _b=bar: e.memset(sv[:, 1:2], 0.0).then_inc(_b.h, 1))
    p.q["tensor"].append(lambda e, _b=bar: e.matmul(sp[:], sv[0:1, 0:1], sv[0:1, 0:4],
                                                    start=True, stop=True).then_inc(_b.h, 1))
    p.q["sync"].append(lambda e, _b=bar: e.nop().then_inc(_b.h, 1))
    for eng in ENGINES:
        p.q[eng].append(lambda e, _b=bar, _t=tgt: e.wait_ge(_b.h, _t))


def kernel(**inputs):
    inp = inputs
    B = 8
    x_full = np.ascontiguousarray(inp["L0_attn_x"], dtype=np.float32)

    nc = bass.Bass("TRN2", target_bir_lowering=False)
    io = _mk_io(nc, [
        ("x", [T, D], F32, "ExternalInput"),
        ("ln_g", [D], F32, "ExternalInput"), ("ln_b", [D], F32, "ExternalInput"),
        ("rW", [D, 8], F32, "ExternalInput"), ("rb", [8], F32, "ExternalInput"),
        ("W1", [8 * 1024, 2048], F32, "ExternalInput"), ("b1", [8, 2048], F32, "ExternalInput"),
        ("W2", [8 * 2048, 1024], F32, "ExternalInput"), ("b2", [8, 1024], F32, "ExternalInput"),
        ("Wq", [D, D], F32, "ExternalInput"), ("Wk", [D, D], F32, "ExternalInput"),
        ("Wv", [D, D], F32, "ExternalInput"), ("Wo", [D, D], F32, "ExternalInput"),
        ("betas", [16, 64], F32, "ExternalInput"),
        ("W_in", [4480, 1024], F32, "ExternalInput"), ("W_out", [1024, 2048], F32, "ExternalInput"),
        ("dt_bias", [32], F32, "ExternalInput"),
        ("B_bias", [32, 128], F32, "ExternalInput"), ("C_bias", [32, 128], F32, "ExternalInput"),
        ("B_norm_w", [128], F32, "ExternalInput"), ("C_norm_w", [128], F32, "ExternalInput"),
        ("D_skip", [32], F32, "ExternalInput"),
        ("kernel", [D, D], F32, "ExternalInput"), ("bias", [D], F32, "ExternalInput"),
        ("a", [D, 16], F32, "ExternalInput"), ("b", [16, D], F32, "ExternalInput"),
        ("xn", [T, D], F32, "Internal"), ("xnT", [128, NT * 1024], BF16, "Internal"),
        ("zT", [128, NT * 1024], BF16, "Internal"), ("wr", [T, 8], F32, "Internal"),
        ("wrT", [8, T], F32, "Internal"), ("g0", [T, D], BF16, "Internal"),
        ("x1", [T, D], F32, "Internal"),
        ("scr_win", [4480, 1024], BF16, "Internal"), ("scr_wout", [1024, 2048], BF16, "Internal"),
        ("scr_woutT", [2048, 1024], BF16, "Internal"),
        ("xn1T", [128, NT * 1024], BF16, "Internal"), ("s1", [T, D], F32, "Internal"),
        ("out", [T, D], F32, "ExternalOutput")])

    p = Prog(nc)
    bar = p.sem("stage_bar")
    scr_v = nc.alloc_sbuf_tensor("bar_scr", [1, 4], F32)
    scr_f = nc.alloc_sbuf_tensor("bar_scr2", [1, 4], F32)
    stage_no = [0]

    def staged(build_fn, sub):
        sb, pb = nc.sbuf_base, nc.psum_base
        build_fn(p, {k: io[k] for k in sub})
        nc.sbuf_base, nc.psum_base = sb, pb
        scr_p = nc.alloc_psum_tensor("bar_psum%d" % stage_no[0], [1, 4], F32)
        stage_no[0] += 1
        nc.psum_base = pb
        _barrier(p, bar, (scr_v, scr_p, scr_f))

    staged(build_prep, ["x", "ln_g", "ln_b", "rW", "rb", "xn", "xnT", "zT", "wr", "wrT"])
    staged(build_moe, ["zT", "xn", "wr", "wrT", "W1", "b1", "W2", "b2", "g0"])
    staged(build_attn0, ["xnT", "Wq", "Wk", "Wv", "Wo", "betas", "g0", "x", "x1",
                         "W_in", "W_out", "scr_win", "scr_wout", "scr_woutT"])
    staged(build_l1a, ["x1", "W_in", "W_out", "dt_bias", "B_bias", "C_bias",
                       "B_norm_w", "C_norm_w", "D_skip", "scr_win", "scr_wout", "scr_woutT",
                       "xn1T", "s1"])
    staged(build_l1f, ["x1", "s1", "xn1T", "kernel", "bias", "a", "b", "out"])
    p.build_block()

    W1f = np.ascontiguousarray(inp["L0_ffn_W1"]).reshape(8 * 1024, 2048)
    W2f = np.ascontiguousarray(inp["L0_ffn_W2"]).reshape(8 * 2048, 1024)
    betas2 = np.ascontiguousarray(inp["L0_attn_betas"]).reshape(16, 64)
    shared = {"ln_g": inp["L0_ffn_ln_g"], "ln_b": inp["L0_ffn_ln_b"], "rW": inp["L0_ffn_rW"],
              "rb": inp["L0_ffn_rb"], "W1": W1f, "b1": inp["L0_ffn_b1"], "W2": W2f,
              "b2": inp["L0_ffn_b2"], "Wq": inp["L0_attn_Wq"], "Wk": inp["L0_attn_Wk"],
              "Wv": inp["L0_attn_Wv"], "Wo": inp["L0_attn_Wo"], "betas": betas2,
              "W_in": inp["L1_attn_W_in"], "W_out": inp["L1_attn_W_out"],
              "dt_bias": inp["L1_attn_dt_bias"], "B_bias": inp["L1_attn_B_bias"],
              "C_bias": inp["L1_attn_C_bias"], "B_norm_w": inp["L1_attn_B_norm_w"],
              "C_norm_w": inp["L1_attn_C_norm_w"], "D_skip": inp["L1_attn_D_skip"],
              "kernel": inp["L1_ffn_kernel"], "bias": inp["L1_ffn_bias"],
              "a": inp["L1_ffn_a"], "b": inp["L1_ffn_b"]}
    maps = [dict(shared, x=x_full[bb]) for bb in range(B)]
    res = run_bass_kernel_spmd(nc, maps, core_ids=CORES).results
    return np.stack([np.asarray(res[bb]["out"], dtype=np.float32) for bb in range(B)], axis=0)


# version 26
# speedup vs baseline: 1.2119x; 1.0014x over previous
import numpy as np
import ml_dtypes
import concourse.bass as bass
import concourse.mybir as mybir
from concourse.bass_utils import run_bass_kernel_spmd

F32 = mybir.dt.float32
BF16 = mybir.dt.bfloat16
AF = mybir.ActivationFunctionType
ALU = mybir.AluOpType
AX = mybir.AxisListType
BF = ml_dtypes.bfloat16

ENGINES = ("sync", "tensor", "vector", "scalar", "gpsimd")

D = 1024
T = 2048
NT = T // 128
KC = D // 128
EPS = 1.0e-5


class Sem:
    __slots__ = ("h", "v", "name")

    def __init__(self, h, name):
        self.h = h
        self.v = 0
        self.name = name


class Prog:
    def __init__(self, nc):
        self.nc = nc
        self.q = {n: [] for n in ENGINES}

    def sem(self, name):
        return Sem(self.nc.alloc_semaphore(name), name)

    def wait(self, eng, sem, val=None):
        v = sem.v if val is None else val
        self.q[eng].append(lambda e, _s=sem, _v=v: e.wait_ge(_s.h, _v))

    def op(self, eng, builder, inc=None):
        if inc is not None:
            inc.v += 1

        def f(e, _b=builder, _i=inc):
            ins = _b(e)
            if _i is not None:
                ins.then_inc(_i.h, 1)
        self.q[eng].append(f)

    def fop(self, eng, builder, sem):
        self.op(eng, builder, inc=sem)
        self.wait(eng, sem)

    def dma(self, eng, out, in_, inc=None, **kw):
        if inc is not None:
            inc.v += 16

        def f(e, _o=out, _in=in_, _i=inc, _kw=kw):
            ins = e.dma_start(out=_o, in_=_in, **_kw)
            if _i is not None:
                ins.then_inc(_i.h, 16)
        self.q[eng].append(f)

    def build_block(self):
        nc = self.nc
        with nc.Block() as block:
            @block.sync
            def _(e):
                for f in self.q["sync"]:
                    f(e)

            @block.tensor
            def _(e):
                for f in self.q["tensor"]:
                    f(e)

            @block.vector
            def _(e):
                for f in self.q["vector"]:
                    f(e)

            @block.scalar
            def _(e):
                for f in self.q["scalar"]:
                    f(e)

            @block.gpsimd
            def _(e):
                for f in self.q["gpsimd"]:
                    f(e)


def build_prep(p, io):
    nc = p.nc
    t_x, t_g, t_b, t_rW, t_rb = io["x"], io["ln_g"], io["ln_b"], io["rW"], io["rb"]
    t_xn, t_xnT, t_zT, t_wr, t_wrT = io["xn"], io["xnT"], io["zT"], io["wr"], io["wrT"]

    identb_c = nc.inline_tensor(np.eye(128, dtype=BF), name="prep_idb").ap()
    identf_c = nc.inline_tensor(np.eye(128, dtype=np.float32), name="prep_idf").ap()
    onesf_c = nc.inline_tensor(np.ones((1, 128), dtype=np.float32), name="prep_onesf").ap()

    xt = [nc.alloc_sbuf_tensor("pp_xt0", [128, D], F32),
          nc.alloc_sbuf_tensor("pp_xt1", [128, D], F32)]
    xnf2 = [nc.alloc_sbuf_tensor("pp_xnf0", [128, D], F32),
            nc.alloc_sbuf_tensor("pp_xnf1", [128, D], F32)]
    xnb = nc.alloc_sbuf_tensor("pp_xnb", [128, D], BF16)
    zf = nc.alloc_sbuf_tensor("pp_zf", [128, D], F32)
    zb = nc.alloc_sbuf_tensor("pp_zb", [128, D], BF16)
    ztf = nc.alloc_sbuf_tensor("pp_ztf", [128, D], F32)
    sqd = nc.alloc_sbuf_tensor("pp_sqd", [128, D], F32)
    stat = nc.alloc_sbuf_tensor("pp_stat", [128, 16], F32)
    gb = nc.alloc_sbuf_tensor("pp_gb", [128, D], F32)
    bb = nc.alloc_sbuf_tensor("pp_bb", [128, D], F32)
    grow = nc.alloc_sbuf_tensor("pp_grow", [1, D], F32)
    brow = nc.alloc_sbuf_tensor("pp_brow", [1, D], F32)
    rbrow = nc.alloc_sbuf_tensor("pp_rbrow", [1, 8], F32)
    rw = nc.alloc_sbuf_tensor("pp_rw", [128, KC * 8], F32)
    onesf = nc.alloc_sbuf_tensor("pp_onesf", [1, 128], F32)
    idb = nc.alloc_sbuf_tensor("pp_idb", [128, 128], BF16)
    idf = nc.alloc_sbuf_tensor("pp_idf", [128, 128], F32)
    rt = nc.alloc_sbuf_tensor("pp_rt", [128, 8], F32)
    rt2 = nc.alloc_sbuf_tensor("pp_rt2", [128, 8], F32)
    wv = nc.alloc_sbuf_tensor("pp_wv", [128, 8], F32)
    wvT = nc.alloc_sbuf_tensor("pp_wvT", [8, 128], F32)
    stgx = nc.alloc_sbuf_tensor("pp_stgx", [128, D], BF16)
    stgz = nc.alloc_sbuf_tensor("pp_stgz", [128, D], BF16)

    ps_tb = nc.alloc_psum_tensor("pp_ps_tb", [128, D], BF16)
    ps_tz = nc.alloc_psum_tensor("pp_ps_tz", [128, D], BF16)
    ps_tf = nc.alloc_psum_tensor("pp_ps_tf", [128, D], F32)
    ps_r = nc.alloc_psum_tensor("pp_ps_r", [128, 8], F32)
    ps_wT = nc.alloc_psum_tensor("pp_ps_wT", [8, 128], F32)

    ld = p.sem("pp_ld")
    pe = p.sem("pp_pe")
    dv = p.sem("pp_dv")
    ac = p.sem("pp_ac")
    st = p.sem("pp_st")

    p.dma("sync", grow[:], t_g[None, :], inc=ld)
    p.dma("sync", brow[:], t_b[None, :], inc=ld)
    p.dma("sync", rbrow[:], t_rb[None, :], inc=ld)
    p.dma("sync", rw[:].rearrange("p (k e) -> p k e", k=KC),
          t_rW.rearrange("(k p) e -> p k e", p=128), inc=ld)
    p.dma("sync", idb[:], identb_c, inc=ld)
    p.dma("sync", idf[:], identf_c, inc=ld)
    p.dma("sync", onesf[:], onesf_c, inc=ld)
    ld_pre = ld.v

    p.wait("vector", ld, ld_pre)
    p.op("vector", lambda e: e.memset(stat[:, 15:16], EPS), inc=dv)
    p.wait("tensor", ld, ld_pre)
    for h in range(2):
        p.op("tensor", lambda e, _h=h: e.matmul(ps_tf[:, _h * 512:(_h + 1) * 512], onesf[:],
                                                grow[0:1, _h * 512:(_h + 1) * 512], start=True, stop=True),
             inc=pe if h == 1 else None)
    p.wait("vector", pe)
    p.op("vector", lambda e: e.tensor_copy(gb[:], ps_tf[:]), inc=dv)
    dv_gb = dv.v
    p.wait("tensor", dv, dv_gb)
    for h in range(2):
        p.op("tensor", lambda e, _h=h: e.matmul(ps_tf[:, _h * 512:(_h + 1) * 512], onesf[:],
                                                brow[0:1, _h * 512:(_h + 1) * 512], start=True, stop=True),
             inc=pe if h == 1 else None)
    p.wait("vector", pe)
    p.op("vector", lambda e: e.tensor_copy(bb[:], ps_tf[:]), inc=dv)

    M1, R1, M2, R2, SU, SS, SC, NG = range(8)
    dv_xn_hist = {}
    dv_evx_prev = 0
    dv_evz_prev = 0
    dv_ztf_prev = 0
    st_xn_prev = {0: 0, 1: 0}
    st_xnT_prev = 0
    st_zT_prev = 0
    st_wr_prev = 0
    ac_xnb_prev = {0: 0, 1: 0}
    ac_sq2_prev = {0: 0, 1: 0}
    ac_zb_prev = 0
    ac_rexp_prev = 0
    pe_tx_prev = 0
    pe_tz_prev = 0
    pe_ztf_prev = 0

    for i in range(NT):
        xti = xt[i % 2]
        if i >= 2:
            p.wait("sync", dv, dv_xn_hist[i - 2])
        p.dma("sync", xti[:], t_x[128 * i:128 * (i + 1), :], inc=ld)
        ld_i = ld.v

        p.wait("vector", ld, ld_i)
        p.fop("vector", lambda e, _x=xti: e.reduce_sum(stat[:, SU:SU + 1], _x[:], axis=AX.X), dv)
        p.wait("scalar", ld, ld_i)
        p.op("scalar", lambda e, _x=xti: e.activation(out=sqd[:], in_=_x[:], func=AF.Square,
                                                      accum_out=stat[:, SS:SS + 1]), inc=ac)
        p.wait("vector", ac)
        p.wait("vector", ac, ac_rexp_prev)
        p.op("vector", lambda e: e.tensor_scalar_mul(stat[:, M1:M1 + 1], stat[:, SU:SU + 1], 1.0 / D))
        p.op("vector", lambda e: e.tensor_scalar_mul(stat[:, SC:SC + 1], stat[:, SS:SS + 1], 1.0 / D))
        p.fop("vector", lambda e: e.tensor_scalar_mul(stat[:, NG:NG + 1], stat[:, M1:M1 + 1], -1.0), dv)
        p.op("vector", lambda e: e.scalar_tensor_tensor(stat[:, SC:SC + 1], stat[:, M1:M1 + 1],
                                                        stat[:, NG:NG + 1], stat[:, SC:SC + 1],
                                                        ALU.mult, ALU.add), inc=dv)
        p.wait("scalar", dv)
        p.op("scalar", lambda e: e.activation(out=stat[:, R1:R1 + 1], in_=stat[:, SC:SC + 1], func=AF.Ln,
                                              bias=stat[:, 15:16], scale=1.0), inc=ac)
        p.wait("scalar", ac)
        p.op("scalar", lambda e: e.activation(out=stat[:, R1:R1 + 1], in_=stat[:, R1:R1 + 1], func=AF.Exp,
                                              scale=-0.5), inc=ac)
        par = i % 2
        xnf = xnf2[par]
        p.wait("vector", ac)
        p.wait("vector", st, st_xn_prev[par])
        p.wait("vector", ac, max(ac_xnb_prev[par], ac_sq2_prev[par]))
        p.op("vector", lambda e, _x=xti, _f=xnf: e.tensor_scalar(_f[:], _x[:], stat[:, M1:M1 + 1], stat[:, R1:R1 + 1],
                                                                 ALU.subtract, ALU.mult), inc=dv)
        dv_xn = dv.v
        dv_xn_hist[i] = dv_xn
        p.wait("gpsimd", dv, dv_xn)
        p.dma("gpsimd", t_xn[128 * i:128 * (i + 1), :], xnf[:], inc=st)
        st_xn_prev[par] = st.v
        p.wait("scalar", dv, dv_xn)
        p.wait("scalar", pe, pe_tx_prev)
        p.op("scalar", lambda e, _f=xnf: e.activation(out=xnb[:], in_=_f[:], func=AF.Copy), inc=ac)
        ac_xnb = ac.v
        ac_xnb_prev[par] = ac_xnb

        p.fop("vector", lambda e, _f=xnf: e.reduce_sum(stat[:, SU:SU + 1], _f[:], axis=AX.X), dv)
        p.op("vector", lambda e: e.tensor_scalar_mul(stat[:, M2:M2 + 1], stat[:, SU:SU + 1], 1.0 / D), inc=dv)
        p.wait("scalar", dv)
        p.op("scalar", lambda e, _f=xnf: e.activation(out=sqd[:], in_=_f[:], func=AF.Square,
                                              accum_out=stat[:, SS:SS + 1]), inc=ac)
        ac_sq2_prev[par] = ac.v
        p.wait("vector", ac)
        p.op("vector", lambda e: e.tensor_scalar_mul(stat[:, SC:SC + 1], stat[:, SS:SS + 1], 1.0 / D))
        p.fop("vector", lambda e: e.tensor_scalar_mul(stat[:, NG:NG + 1], stat[:, M2:M2 + 1], -1.0), dv)
        p.op("vector", lambda e: e.scalar_tensor_tensor(stat[:, SC:SC + 1], stat[:, M2:M2 + 1],
                                                        stat[:, NG:NG + 1], stat[:, SC:SC + 1],
                                                        ALU.mult, ALU.add), inc=dv)
        p.wait("scalar", dv)
        p.op("scalar", lambda e: e.activation(out=stat[:, R2:R2 + 1], in_=stat[:, SC:SC + 1], func=AF.Ln,
                                              bias=stat[:, 15:16], scale=1.0), inc=ac)
        p.wait("scalar", ac)
        p.op("scalar", lambda e: e.activation(out=stat[:, R2:R2 + 1], in_=stat[:, R2:R2 + 1], func=AF.Exp,
                                              scale=-0.5), inc=ac)
        p.wait("vector", ac)
        p.wait("vector", ac, ac_zb_prev)
        p.wait("vector", pe, pe_ztf_prev)
        p.op("vector", lambda e, _f=xnf: e.tensor_scalar(zf[:], _f[:], stat[:, M2:M2 + 1], stat[:, R2:R2 + 1],
                                                 ALU.subtract, ALU.mult))
        p.op("vector", lambda e: e.tensor_mul(zf[:], zf[:], gb[:]))
        p.op("vector", lambda e: e.tensor_add(zf[:], zf[:], bb[:]), inc=dv)
        dv_z = dv.v
        p.wait("scalar", dv, dv_z)
        p.wait("scalar", pe, pe_tz_prev)
        p.op("scalar", lambda e: e.activation(out=zb[:], in_=zf[:], func=AF.Copy), inc=ac)
        ac_zb = ac.v
        ac_zb_prev = ac_zb

        p.wait("tensor", ac, ac_xnb)
        p.wait("tensor", dv, dv_evx_prev)
        for k in range(KC):
            p.op("tensor", lambda e, _k=k: e.transpose(ps_tb[:, 128 * _k:128 * (_k + 1)],
                                                       xnb[:, 128 * _k:128 * (_k + 1)], idb[:]),
                 inc=pe if k == KC - 1 else None)
        pe_tx_prev = pe.v
        p.wait("vector", pe)
        p.wait("vector", st, st_xnT_prev)
        p.op("vector", lambda e: e.tensor_copy(stgx[:], ps_tb[:]), inc=dv)
        dv_evx_prev = dv.v
        p.wait("gpsimd", dv, dv_evx_prev)
        p.dma("gpsimd", t_xnT[:, 1024 * i:1024 * (i + 1)], stgx[:], inc=st)
        st_xnT_prev = st.v

        p.wait("tensor", ac, ac_zb)
        p.wait("tensor", dv, dv_evz_prev)
        for k in range(KC):
            p.op("tensor", lambda e, _k=k: e.transpose(ps_tz[:, 128 * _k:128 * (_k + 1)],
                                                       zb[:, 128 * _k:128 * (_k + 1)], idb[:]),
                 inc=pe if k == KC - 1 else None)
        pe_tz_prev = pe.v
        p.wait("vector", pe)
        p.wait("vector", st, st_zT_prev)
        p.op("vector", lambda e: e.tensor_copy(stgz[:], ps_tz[:]), inc=dv)
        dv_evz_prev = dv.v
        p.wait("gpsimd", dv, dv_evz_prev)
        p.dma("gpsimd", t_zT[:, 1024 * i:1024 * (i + 1)], stgz[:], inc=st)
        st_zT_prev = st.v

        p.wait("tensor", dv, max(dv_z, dv_ztf_prev))
        for k in range(KC):
            p.op("tensor", lambda e, _k=k: e.transpose(ps_tf[:, 128 * _k:128 * (_k + 1)],
                                                       zf[:, 128 * _k:128 * (_k + 1)], idf[:]),
                 inc=pe if k == KC - 1 else None)
        pe_ztf_prev = pe.v
        p.wait("vector", pe)
        p.op("vector", lambda e: e.tensor_copy(ztf[:], ps_tf[:]), inc=dv)
        dv_ztf = dv.v
        dv_ztf_prev = dv.v

        p.wait("tensor", dv, dv_ztf)
        p.wait("tensor", ac, ac_rexp_prev)
        for k in range(KC):
            p.op("tensor", lambda e, _k=k: e.matmul(ps_r[:], ztf[:, 128 * _k:128 * (_k + 1)],
                                                    rw[:, 8 * _k:8 * (_k + 1)], start=(_k == 0), stop=False))
        p.op("tensor", lambda e: e.matmul(ps_r[:], onesf[:, 0:128], rbrow[:], start=False, stop=True), inc=pe)
        pe_r = pe.v
        p.wait("vector", pe, pe_r)
        p.fop("vector", lambda e: e.reduce_max(stat[:, SC:SC + 1], ps_r[:], axis=AX.X), dv)
        p.op("vector", lambda e: e.tensor_scalar_mul(stat[:, SC:SC + 1], stat[:, SC:SC + 1], -1.0), inc=dv)
        p.wait("scalar", dv)
        p.op("scalar", lambda e: e.activation(out=rt[:], in_=ps_r[:], func=AF.Exp,
                                              bias=stat[:, SC:SC + 1], scale=1.0), inc=ac)
        ac_rexp_prev = ac.v
        p.wait("vector", ac)
        p.fop("vector", lambda e: e.reduce_max(stat[:, SC:SC + 1], rt[:], axis=AX.X), dv)
        p.fop("vector", lambda e: e.tensor_scalar(rt2[:], rt[:], stat[:, SC:SC + 1], None, ALU.is_equal), dv)
        p.fop("vector", lambda e: e.scalar_tensor_tensor(rt2[:], rt2[:], -1e30, rt[:], ALU.mult, ALU.add), dv)
        p.fop("vector", lambda e: e.reduce_max(stat[:, SC:SC + 1], rt2[:], axis=AX.X), dv)
        p.fop("vector", lambda e: e.tensor_scalar(rt2[:], rt[:], stat[:, SC:SC + 1], None, ALU.is_ge), dv)
        p.wait("vector", st, st_wr_prev)
        p.fop("vector", lambda e: e.tensor_mul(wv[:], rt[:], rt2[:]), dv)
        p.fop("vector", lambda e: e.reduce_sum(stat[:, SC:SC + 1], wv[:], axis=AX.X), dv)
        p.fop("vector", lambda e: e.reciprocal(stat[:, SC:SC + 1], stat[:, SC:SC + 1]), dv)
        p.op("vector", lambda e: e.tensor_scalar_mul(wv[:], wv[:], stat[:, SC:SC + 1]), inc=dv)
        dv_w = dv.v
        p.wait("tensor", dv, dv_w)
        p.op("tensor", lambda e: e.transpose(ps_wT[:], wv[:], idf[:]), inc=pe)
        p.wait("vector", pe)
        p.op("vector", lambda e: e.tensor_copy(wvT[:], ps_wT[:]), inc=dv)
        dv_wT = dv.v
        p.wait("gpsimd", dv, dv_wT)
        p.dma("gpsimd", t_wr[128 * i:128 * (i + 1), :], wv[:], inc=st)
        p.dma("gpsimd", t_wrT[:, 128 * i:128 * (i + 1)], wvT[:], inc=st)
        st_wr_prev = st.v
        if "dbg" in io:
            p.op("vector", lambda e: e.tensor_copy(stat[:, 8:15], rt[:, 0:7]), inc=dv)
            p.wait("gpsimd", dv)
            p.dma("gpsimd", io["dbg"][i], stat[:], inc=st)

    p.wait("gpsimd", st)
    p.wait("gpsimd", ld)
    return st


def ap3(t, off, dims):
    row = t.shape[1]
    return bass.AP(t, off, [[row, t.shape[0]]] + dims)


def build_moe(p, io):
    nc = p.nc
    identf_c = nc.inline_tensor(np.eye(128, dtype=np.float32), name="moe_idf").ap()

    zTs = nc.alloc_sbuf_tensor("mo_zT", [128, NT * 1024], BF16)
    mixed = nc.alloc_sbuf_tensor("mo_mx", [128, NT * 1024], F32)
    wrs = nc.alloc_sbuf_tensor("mo_wr", [128, NT * 8], F32)
    wrTs = nc.alloc_sbuf_tensor("mo_wrT", [8, T], F32)
    b2s = nc.alloc_sbuf_tensor("mo_b2", [8, 1024], F32)
    b1s = nc.alloc_sbuf_tensor("mo_b1", [8, 2048], F32)
    b1T = nc.alloc_sbuf_tensor("mo_b1T", [128, 16 * 8], F32)
    idf = nc.alloc_sbuf_tensor("mo_idf", [128, 128], F32)
    W1s = nc.alloc_sbuf_tensor("mo_W1", [128, 8 * 2048], BF16)
    W2s = nc.alloc_sbuf_tensor("mo_W2", [128, 16 * 1024], BF16)
    hid = nc.alloc_sbuf_tensor("mo_hid", [128, 16 * 512], BF16)
    g0s = nc.alloc_sbuf_tensor("mo_g0", [128, 1024], BF16)

    ps_g1 = [nc.alloc_psum_tensor("mo_ps1a", [128, 512], F32),
             nc.alloc_psum_tensor("mo_ps1b", [128, 512], F32)]
    ps_g2 = [nc.alloc_psum_tensor("mo_ps2a", [128, 512], F32),
             nc.alloc_psum_tensor("mo_ps2b", [128, 512], F32)]
    ps_b2 = nc.alloc_psum_tensor("mo_psb2", [128, 512], F32)
    ps_t8 = nc.alloc_psum_tensor("mo_pst8", [128, 8], F32)

    ld = p.sem("mo_ld")
    lw = p.sem("mo_lw")
    pe = p.sem("mo_pe")
    dv = p.sem("mo_dv")
    ac = p.sem("mo_ac")
    st = p.sem("mo_st")

    p.dma("sync", zTs[:], io["zT"], inc=ld)
    p.dma("sync", mixed[:].rearrange("p (i c) -> p i c", i=NT),
          io["xn"].rearrange("(i p) c -> p i c", p=128), inc=ld)
    p.dma("sync", wrs[:].rearrange("p (i e) -> p i e", i=NT),
          io["wr"].rearrange("(i p) e -> p i e", p=128), inc=ld)
    p.dma("sync", wrTs[:], io["wrT"], inc=ld)
    p.dma("sync", b2s[:], io["b2"], inc=ld)
    p.dma("sync", b1s[:], io["b1"], inc=ld)
    p.dma("sync", idf[:], identf_c, inc=ld)
    ld0 = ld.v

    p.wait("tensor", ld, ld0)
    evprev = 0
    for t_ in range(16):
        p.wait("tensor", dv, evprev)
        p.op("tensor", lambda e, _t=t_: e.transpose(ps_t8[:, 0:8], b1s[:, 128 * _t:128 * (_t + 1)],
                                                    idf[0:8, 0:8]), inc=pe)
        p.wait("vector", pe)
        p.op("vector", lambda e, _t=t_: e.tensor_copy(b1T[:, 8 * _t:8 * (_t + 1)], ps_t8[:, 0:8]), inc=dv)
        evprev = dv.v

    pe_g1_done = 0
    pe_g2_done = 0
    acmark1 = {0: 0, 1: 0}
    dvmark2 = {0: 0, 1: 0}
    for e_ in range(8):
        p.wait("gpsimd", pe, pe_g1_done)
        p.dma("gpsimd", W1s[:].rearrange("p (c f) -> p c f", c=8),
              io["W1"][1024 * e_:1024 * (e_ + 1), :].rearrange("(c p) f -> p c f", p=128), inc=lw)
        lw_w1 = lw.v
        p.wait("gpsimd", pe, pe_g2_done)
        p.dma("gpsimd", W2s[:].rearrange("p (c f) -> p c f", c=16),
              io["W2"][2048 * e_:2048 * (e_ + 1), :].rearrange("(c p) f -> p c f", p=128), inc=lw)
        lw_e = lw.v
        p.wait("tensor", lw, lw_w1)
        p.wait("tensor", ld, ld0)
        for T2 in range(4):
            pe_gemm2_prev = pe_g2_done
            for hft in range(16):
                bk = hft % 2
                pb = ps_g1[bk]
                p.wait("tensor", ac, acmark1[bk])
                for c in range(KC):
                    p.op("tensor", lambda e, _c=c, _h=hft, _pb=pb, _T=T2: e.matmul(
                        _pb[:], W1s[:, _c * 2048 + 128 * _h:_c * 2048 + 128 * (_h + 1)],
                        ap3(zTs, 4096 * _T + 128 * _c, [[1024, 4], [1, 128]]),
                        start=(_c == 0), stop=(_c == KC - 1)), inc=pe if c == KC - 1 else None)
                pe_mm = pe.v
                p.wait("scalar", pe, pe_mm)
                p.wait("scalar", pe, pe_gemm2_prev)
                p.op("scalar", lambda e, _h=hft, _e=e_, _pb=pb: e.activation(
                    out=hid[:, 512 * _h:512 * (_h + 1)], in_=_pb[:], func=AF.Gelu,
                    bias=b1T[:, 8 * _h + _e:8 * _h + _e + 1], scale=1.0), inc=ac)
                acmark1[bk] = ac.v
            ac_hid_done = ac.v
            pe_g1_done = pe.v
            p.wait("tensor", ac, ac_hid_done)
            if T2 == 0:
                p.wait("tensor", lw, lw_e)
            for tt in range(4):
                i_glob = 4 * T2 + tt
                for nh in range(2):
                    bk = (2 * tt + nh) % 2
                    pb = ps_g2[bk]
                    p.wait("tensor", dv, dvmark2[bk])
                    for hk in range(16):
                        p.op("tensor", lambda e, _k=hk, _t=tt, _n=nh, _pb=pb: e.matmul(
                            _pb[:], hid[:, 512 * _k + 128 * _t:512 * _k + 128 * (_t + 1)],
                            W2s[:, 1024 * _k + 512 * _n:1024 * _k + 512 * (_n + 1)],
                            start=(_k == 0), stop=(_k == 15)), inc=pe if hk == 15 else None)
                    pe_mm2 = pe.v
                    p.wait("vector", pe, pe_mm2)
                    p.op("vector", lambda e, _i=i_glob, _n=nh, _e=e_, _pb=pb: e.scalar_tensor_tensor(
                        mixed[:, 1024 * _i + 512 * _n:1024 * _i + 512 * (_n + 1)], _pb[:],
                        wrs[:, 8 * _i + _e:8 * _i + _e + 1],
                        mixed[:, 1024 * _i + 512 * _n:1024 * _i + 512 * (_n + 1)],
                        ALU.mult, ALU.add), inc=dv)
                    dvmark2[bk] = dv.v
            pe_g2_done = pe.v

    dv_mx_done = dv.v
    st_prev = 0
    dvmark_b2 = 0
    p.wait("tensor", dv, dv_mx_done)
    for i in range(NT):
        for nh in range(2):
            p.wait("tensor", dv, dvmark_b2)
            p.op("tensor", lambda e, _i=i, _n=nh: e.matmul(
                ps_b2[:], wrTs[:, 128 * _i:128 * (_i + 1)], b2s[:, 512 * _n:512 * (_n + 1)],
                start=True, stop=True), inc=pe)
            pe_b2 = pe.v
            p.wait("vector", pe, pe_b2)
            p.op("vector", lambda e, _i=i, _n=nh: e.tensor_add(
                mixed[:, 1024 * _i + 512 * _n:1024 * _i + 512 * (_n + 1)],
                mixed[:, 1024 * _i + 512 * _n:1024 * _i + 512 * (_n + 1)], ps_b2[:]), inc=dv)
            dvmark_b2 = dv.v
        dv_i = dv.v
        p.wait("scalar", dv, dv_i)
        p.wait("scalar", st, st_prev)
        p.op("scalar", lambda e, _i=i: e.activation(out=g0s[:], in_=mixed[:, 1024 * _i:1024 * (_i + 1)],
                                                    func=AF.Silu), inc=ac)
        ac_i = ac.v
        p.wait("gpsimd", ac, ac_i)
        p.dma("gpsimd", io["g0"][128 * i:128 * (i + 1), :], g0s[:], inc=st)
        st_prev = st.v

    p.wait("gpsimd", st)
    p.wait("gpsimd", ld)
    p.wait("gpsimd", lw)


H0 = 16
SEG = 512
NSEG = 4


def build_attn0(p, io, nseg=NSEG, upto=99):
    nc = p.nc
    identb_c = nc.inline_tensor(np.eye(128, dtype=BF), name="a0_idbc").ap()
    id16_c = nc.inline_tensor(np.eye(16, dtype=np.float32), name="a0_id16c").ap()
    idf_c = nc.inline_tensor(np.eye(128, dtype=np.float32), name="a0_idfc").ap()
    onesb_c = nc.inline_tensor(np.ones((1, 128), dtype=BF), name="a0_onesbc").ap()

    xnTs = nc.alloc_sbuf_tensor("a0_xnT", [128, 4 * 1024], BF16)
    Wqs = nc.alloc_sbuf_tensor("a0_Wq", [128, 8 * 1024], BF16)
    Wks = nc.alloc_sbuf_tensor("a0_Wk", [128, 8 * 1024], BF16)
    Wvs = nc.alloc_sbuf_tensor("a0_Wv", [128, 8 * 1024], BF16)
    Wos = nc.alloc_sbuf_tensor("a0_Wo", [128, 8 * 1024], BF16)
    qh = nc.alloc_sbuf_tensor("a0_qh", [64, H0 * SEG], BF16)
    kh = nc.alloc_sbuf_tensor("a0_kh", [64, H0 * SEG], BF16)
    vh = nc.alloc_sbuf_tensor("a0_vh", [128, H0 * SEG], BF16)
    sqh = nc.alloc_sbuf_tensor("a0_sqh", [64, H0 * SEG], BF16)
    ekh = nc.alloc_sbuf_tensor("a0_ekh", [64, H0 * SEG], BF16)
    vtok = nc.alloc_sbuf_tensor("a0_vtok", [128, H0 * 4 * 64], BF16)
    sqtok = nc.alloc_sbuf_tensor("a0_sqtok", [128, H0 * 4 * 64], BF16)
    PT = [nc.alloc_sbuf_tensor("a0_PT0", [128, 4 * SEG], BF16),
          nc.alloc_sbuf_tensor("a0_PT1", [128, 4 * SEG], BF16)]
    mem = nc.alloc_sbuf_tensor("a0_mem", [64, H0 * 64], F32)
    memb = nc.alloc_sbuf_tensor("a0_memb", [64, 64], BF16)
    zcs = nc.alloc_sbuf_tensor("a0_zc", [64, H0], F32)
    zcseg = nc.alloc_sbuf_tensor("a0_zcseg", [64, H0], F32)
    gate = nc.alloc_sbuf_tensor("a0_gate", [64, H0], F32)
    acol = nc.alloc_sbuf_tensor("a0_acol", [64, H0], F32)
    acolT = nc.alloc_sbuf_tensor("a0_acolT", [16, 64], F32)
    gcT = nc.alloc_sbuf_tensor("a0_gcT", [16, 64], F32)
    G2all = nc.alloc_sbuf_tensor("a0_G2all", [128, H0 * 64], F32)
    G1sb = nc.alloc_sbuf_tensor("a0_G1sb", [128, 64], F32)
    ssum = nc.alloc_sbuf_tensor("a0_ssum", [128, H0 * 4], F32)
    rcol = nc.alloc_sbuf_tensor("a0_rcol", [128, H0 * 4], F32)
    mstat = nc.alloc_sbuf_tensor("a0_mst", [128, 16], F32)
    linv = nc.alloc_sbuf_tensor("a0_linv", [128, 4], F32)
    mstatb = nc.alloc_sbuf_tensor("a0_mstb", [128, 4], BF16)
    nrow = nc.alloc_sbuf_tensor("a0_nrow", [1, SEG], BF16)
    sedump = nc.alloc_sbuf_tensor("a0_sed", [128, 512], BF16)
    onesb = nc.alloc_sbuf_tensor("a0_onesb", [1, 128], BF16)
    idb = nc.alloc_sbuf_tensor("a0_idb", [128, 128], BF16)
    id16 = nc.alloc_sbuf_tensor("a0_id16", [16, 16], F32)
    idf = nc.alloc_sbuf_tensor("a0_idf", [128, 128], F32)
    bet = nc.alloc_sbuf_tensor("a0_bet", [16, 64], F32)
    t1s = nc.alloc_sbuf_tensor("a0_t1s", [128, 64], F32)
    t2s = nc.alloc_sbuf_tensor("a0_t2s", [128, 64], F32)
    atts = nc.alloc_sbuf_tensor("a0_atts", [128, 64], BF16)
    xs = nc.alloc_sbuf_tensor("a0_xs", [128, 1024], F32)
    g0s = nc.alloc_sbuf_tensor("a0_g0s", [128, 1024], BF16)
    x1s = nc.alloc_sbuf_tensor("a0_x1s", [128, 1024], F32)
    tmwo = nc.alloc_sbuf_tensor("a0_tmwo", [128, 512], F32)

    ps_sc = [nc.alloc_psum_tensor("a0_psA", [128, 512], F32),
             nc.alloc_psum_tensor("a0_psB", [128, 512], F32)]
    ps_an = nc.alloc_psum_tensor("a0_psAN", [128, 512], F32)
    ps_mem = nc.alloc_psum_tensor("a0_psM", [64, 64], F32)
    ps_tr = [nc.alloc_psum_tensor("a0_psT0", [128, 128], BF16),
             nc.alloc_psum_tensor("a0_psT1", [128, 128], BF16)]
    ps_wo = nc.alloc_psum_tensor("a0_psW", [128, 512], F32)
    ps_sc.append(ps_wo)
    ps_g = nc.alloc_psum_tensor("a0_psG", [128, 128], F32)

    ld = p.sem("a0_ld")
    pe = p.sem("a0_pe")
    dv = p.sem("a0_dv")
    ac = p.sem("a0_ac")
    gp = p.sem("a0_gp")
    st = p.sem("a0_st")

    for Wsb, key in ((Wqs, "Wq"), (Wks, "Wk"), (Wvs, "Wv"), (Wos, "Wo")):
        p.dma("gpsimd", Wsb[:].rearrange("p (c f) -> p c f", c=8),
              io[key].rearrange("(c p) f -> p c f", p=128), inc=ld)
    p.dma("sync", idb[:], identb_c, inc=ld)
    p.dma("sync", id16[:], id16_c, inc=ld)
    p.dma("sync", idf[:], idf_c, inc=ld)
    p.dma("sync", onesb[:], onesb_c, inc=ld)
    p.dma("sync", bet[:], io["betas"], inc=ld)
    ld0 = ld.v
    wc = None
    if "W_out" in io:
        wbs = [nc.alloc_sbuf_tensor("a0_wb%d" % k, [128, 1024], BF16) for k in range(2)]
        wc = p.sem("a0_wc")
        wt = p.sem("a0_wt")
        ws = p.sem("a0_ws")
        p.dma("gpsimd", io["scr_wout"], io["W_out"], inc=wc)
        p.dma("gpsimd", io["scr_win"], io["W_in"], inc=wc)

    p.wait("tensor", ld, ld0)
    p.op("tensor", lambda e: e.transpose(ps_g[0:64, 0:16], bet[:], id16[:]), inc=pe)
    p.wait("scalar", pe)
    p.op("scalar", lambda e: e.activation(out=gate[:], in_=ps_g[0:64, 0:16], func=AF.Sigmoid), inc=ac)
    p.wait("vector", ac)
    p.fop("vector", lambda e: e.tensor_scalar(acol[:], gate[:], -1.0, 1.0, ALU.mult, ALU.add), dv)
    p.wait("tensor", dv)
    p.op("tensor", lambda e: e.transpose(ps_g[0:16, 0:64], acol[:], idf[0:64, 0:64]), inc=pe)
    p.wait("vector", pe)
    p.op("vector", lambda e: e.tensor_copy(gcT[:], ps_g[0:16, 0:64]), inc=dv)
    dv_gcT = dv.v
    p.wait("tensor", dv, dv_gcT)
    pmark_g = 0
    for h in range(H0):
        p.wait("tensor", dv, pmark_g)
        p.op("tensor", lambda e, _h=h: e.matmul(ps_g[:, 0:64],
                                                bass.AP(id16, _h, [[16, 16], [0, 128]]),
                                                gcT[:], start=True, stop=True), inc=pe)
        p.wait("vector", pe)
        p.op("vector", lambda e, _h=h: e.tensor_copy(G2all[:, 64 * _h:64 * (_h + 1)], ps_g[:, 0:64]), inc=dv)
        pmark_g = dv.v
    p.op("vector", lambda e: e.memset(mem[:], 0.0))
    p.op("vector", lambda e: e.memset(zcs[:], 0.0), inc=dv)

    mk_sc = [(0, 0), (0, 0), (0, 0)]
    mk_attd = 0
    mk_num = 0
    pt_rd = [0, 0]
    hs_acPT = [0, 0]
    hs_acl = [0, 0]
    mk_mem = 0
    mk_tr = [(0, "dv"), (0, "dv")]
    tr_i = 0
    mk_wo = 0
    mk_g1 = 0
    GEMS = ((Wqs, qh), (Wks, kh), (Wvs, vh))
    xnt_free_prev = 0

    for n in range(nseg):
        if n == 2 and wc is not None:
            p.wait("sync", wc, 16)
            for k in range(16):
                if k >= 2:
                    p.wait("sync", ws, 16 * (k - 1))
                p.dma("sync", wbs[k % 2][:], io["scr_wout"][:, 128 * k:128 * (k + 1)], transpose=True, inc=wt)
                p.wait("gpsimd", wt, 16 * (k + 1))
                p.dma("gpsimd", io["scr_woutT"][128 * k:128 * (k + 1), :], wbs[k % 2][:], inc=ws)
        pe_wo_prev = pe.v
        p.wait("sync", pe, xnt_free_prev)
        p.dma("sync", xnTs[:], io["xnT"][:, 4096 * n:4096 * (n + 1)], inc=ld)
        ld_seg = ld.v
        p.wait("tensor", ld, ld_seg)
        for gi, (Wsb, dst) in enumerate(GEMS):
            for ft in range(8):
                bk = (8 * gi + ft) % 3
                p.wait("tensor", dv, mk_sc[bk][0])
                p.wait("tensor", ac, mk_sc[bk][1])
                for c in range(KC):
                    p.op("tensor", lambda e, _c=c, _f=ft, _W=Wsb, _n=n, _b=bk: e.matmul(
                        ps_sc[_b][:], _W[:, _c * 1024 + 128 * _f:_c * 1024 + 128 * (_f + 1)],
                        ap3(xnTs, 128 * _c, [[1024, 4], [1, 128]]),
                        start=(_c == 0), stop=(_c == KC - 1)), inc=pe if c == KC - 1 else None)
                pe_mm = pe.v
                p.wait("vector", pe, pe_mm)
                p.op("vector", lambda e, _f=ft, _d=dst, _b2=bk: e.tensor_copy(
                    bass.AP(_d, 2 * _f, [[H0 * SEG, 64], [SEG, 16], [16, 32]]),
                    bass.AP(ps_sc[_b2], 0, [[512, 64], [32, 16], [1, 32]])), inc=dv)
                p.wait("scalar", pe, pe_mm)
                p.op("scalar", lambda e, _f=ft, _d=dst, _b2=bk: e.activation(
                    out=bass.AP(_d, 2 * _f + 1, [[H0 * SEG, 64], [SEG, 16], [16, 32]]),
                    in_=bass.AP(ps_sc[_b2], 64 * 512, [[512, 64], [32, 16], [1, 32]]),
                    func=AF.Copy), inc=ac)
                mk_sc[bk] = (dv.v, ac.v)
        xnt_free_prev = pe.v
        dv_qkv = dv.v
        ac_qkv = ac.v
        if upto < 2:
            if "dbg_qh" in io:
                p.wait("gpsimd", dv, dv_qkv)
                p.wait("gpsimd", ac, ac_qkv)
                p.dma("gpsimd", io["dbg_qh"], qh[:], inc=st)
                p.dma("gpsimd", io["dbg_kh"], kh[:], inc=st)
                p.dma("gpsimd", io["dbg_vh"], vh[0:64, :], inc=st)
            continue

        p.wait("vector", dv, dv_qkv)
        p.wait("vector", ac, ac_qkv)
        p.fop("vector", lambda e: e.tensor_scalar_min(sqh[:], kh[:], 0.0), dv)
        p.wait("scalar", dv)
        p.op("scalar", lambda e: e.activation(out=sqh[:], in_=sqh[:], func=AF.Exp), inc=ac)
        p.wait("vector", ac)
        p.fop("vector", lambda e: e.scalar_tensor_tensor(ekh[:], kh[:], 0.0, sqh[:], ALU.max, ALU.add), dv)
        dv_ek = dv.v
        p.fop("vector", lambda e: e.tensor_reduce(zcseg[:], bass.AP(ekh, 0, [[H0 * SEG, 64], [SEG, H0], [1, SEG]]),
                                                  axis=AX.X, op=ALU.add), dv)
        p.fop("vector", lambda e: e.tensor_add(zcs[:], zcs[:], zcseg[:]), dv)
        if upto < 3:
            if "dbg_qh" in io:
                p.wait("gpsimd", dv, dv.v)
                p.dma("gpsimd", io["dbg_qh"], sqh[:], inc=st)
                p.dma("gpsimd", io["dbg_kh"], ekh[:], inc=st)
            continue

        p.wait("tensor", dv, dv_qkv)
        p.wait("tensor", ac, ac_qkv)
        for h in range(H0):
            for lt in range(4):
                p.wait("tensor", ac if mk_tr[tr_i][1] == "ac" else dv, mk_tr[tr_i][0])
                p.op("tensor", lambda e, _h=h, _l=lt, _t=tr_i: e.transpose(
                    ps_tr[_t][:, 0:64], vh[0:64, SEG * _h + 128 * _l:SEG * _h + 128 * (_l + 1)],
                    bass.AP(idb, 0, [[128, 64], [1, 64]])), inc=pe)
                p.wait("scalar", pe)
                p.op("scalar", lambda e, _h=h, _l=lt, _t=tr_i: e.activation(
                    out=vtok[:, (4 * _h + _l) * 64:(4 * _h + _l + 1) * 64], in_=ps_tr[_t][:, 0:64],
                    func=AF.Copy), inc=ac)
                mk_tr[tr_i] = (ac.v, "ac")
                tr_i ^= 1
        p.fop("vector", lambda e: e.tensor_scalar_min(ekh[:], qh[:], 0.0), dv)
        p.wait("scalar", dv)
        p.op("scalar", lambda e: e.activation(out=ekh[:], in_=ekh[:], func=AF.Exp), inc=ac)
        p.wait("vector", ac)
        p.fop("vector", lambda e: e.scalar_tensor_tensor(sqh[:], qh[:], 0.0, ekh[:], ALU.max, ALU.add), dv)
        dv_elu = dv.v
        p.fop("vector", lambda e: e.reciprocal(acol[:], zcs[:]), dv)
        p.fop("vector", lambda e: e.tensor_mul(acol[:], acol[:], gate[:]), dv)
        dv_acol = dv.v
        p.wait("tensor", dv, dv_acol)
        p.op("tensor", lambda e: e.transpose(ps_g[0:16, 0:64], acol[:], idf[0:64, 0:64]), inc=pe)
        p.wait("vector", pe)
        p.fop("vector", lambda e: e.tensor_copy(acolT[:], ps_g[0:16, 0:64]), dv)
        dv_acolT = dv.v
        p.wait("tensor", dv, dv_elu)
        for h in range(H0):
            for lt in range(4):
                p.wait("tensor", ac if mk_tr[tr_i][1] == "ac" else dv, mk_tr[tr_i][0])
                p.op("tensor", lambda e, _h=h, _l=lt, _t=tr_i: e.transpose(
                    ps_tr[_t][:, 0:64], sqh[0:64, SEG * _h + 128 * _l:SEG * _h + 128 * (_l + 1)],
                    bass.AP(idb, 0, [[128, 64], [1, 64]])), inc=pe)
                p.wait("scalar", pe)
                p.op("scalar", lambda e, _h=h, _l=lt, _t=tr_i: e.activation(
                    out=sqtok[:, (4 * _h + _l) * 64:(4 * _h + _l + 1) * 64], in_=ps_tr[_t][:, 0:64],
                    func=AF.Copy, accum_out=ssum[:, 4 * _h + _l:4 * _h + _l + 1]), inc=ac)
                mk_tr[tr_i] = (ac.v, "ac")
                tr_i ^= 1
        ac_tok = ac.v
        p.wait("vector", ac, ac_tok)
        p.fop("vector", lambda e: e.reciprocal(rcol[:], ssum[:]), dv)

        if upto < 4:
            if "dbg_qh" in io:
                p.wait("gpsimd", ac, ac.v)
                p.dma("gpsimd", io["dbg_tok1"], sqtok[:], inc=st)
                p.dma("gpsimd", io["dbg_tok2"], vtok[:], inc=st)
            continue
        for hh in range(H0 + 1):
            if hh < H0:
                h = hh
                r = h % 2
                for s4 in range(4):
                    bk = s4 % 3
                    p.wait("tensor", dv, mk_sc[bk][0])
                    p.wait("tensor", ac, mk_sc[bk][1])
                    p.op("tensor", lambda e, _h=h, _s=s4, _b=bk: e.matmul(
                        ps_sc[_b][:], qh[:, SEG * _h + 128 * _s:SEG * _h + 128 * (_s + 1)],
                        kh[:, SEG * _h:SEG * (_h + 1)], start=True, stop=True), inc=pe)
                    pe_s = pe.v
                    p.wait("vector", pe, pe_s)
                    p.fop("vector", lambda e, _s=s4, _b=bk: e.reduce_max(mstat[:, _s:_s + 1], ps_sc[_b][:], axis=AX.X), dv)
                    p.fop("vector", lambda e, _s=s4: e.tensor_scalar_mul(mstat[:, 8 + _s:9 + _s], mstat[:, _s:_s + 1], -0.125), dv)
                    dv_b = dv.v
                    p.wait("scalar", dv, dv_b)
                    lc = (4 if r == 0 else 12) + s4
                    p.op("scalar", lambda e, _s=s4, _b=bk, _lc=lc: e.activation(
                        out=sedump[:], in_=ps_sc[_b][:], func=AF.Exp, bias=mstat[:, 8 + _s:9 + _s], scale=0.125,
                        accum_out=mstat[:, _lc:_lc + 1]), inc=ac)
                    mk_sc[bk] = (dv.v, ac.v)
                hs_acl[r] = ac.v
            if hh > 0:
                hp = hh - 1
                rp = hp % 2
                sv_ac_PT = hs_acPT[rp]
                p.wait("tensor", ac, sv_ac_PT)
                p.wait("tensor", dv, mk_attd)
                for s4 in range(4):
                    for t4 in range(4):
                        p.op("tensor", lambda e, _h=hp, _s=s4, _t=t4, _r=rp: e.matmul(
                            ps_an[:, 64 * _s:64 * (_s + 1)],
                            PT[_r][:, 512 * _t + 128 * _s:512 * _t + 128 * (_s + 1)],
                            vtok[:, (4 * _h + _t) * 64:(4 * _h + _t + 1) * 64],
                            start=(_t == 0), stop=(_t == 3)), inc=pe if (s4 == 3 and t4 == 3) else None)
                pe_attd = pe.v
                pt_rd[rp] = pe_attd
                p.wait("tensor", dv, mk_mem)
                for lt in range(4):
                    p.op("tensor", lambda e, _h=hp, _l=lt: e.matmul(
                        ps_mem[:], sqtok[:, (4 * _h + _l) * 64:(4 * _h + _l + 1) * 64],
                        vtok[:, (4 * _h + _l) * 64:(4 * _h + _l + 1) * 64],
                        start=(_l == 0), stop=(_l == 3)), inc=pe if lt == 3 else None)
                pe_mem = pe.v
                p.wait("vector", pe, pe_mem)
                p.fop("vector", lambda e, _h=hp: e.tensor_add(mem[:, 64 * _h:64 * (_h + 1)],
                                                              mem[:, 64 * _h:64 * (_h + 1)], ps_mem[:]), dv)
                p.fop("vector", lambda e, _h=hp: e.tensor_copy(memb[:], mem[:, 64 * _h:64 * (_h + 1)]), dv)
                dv_memb = dv.v
                mk_mem = dv_memb
                p.wait("tensor", dv, max(dv_acolT, mk_g1))
                p.op("tensor", lambda e, _h=hp: e.matmul(ps_g[:, 0:64],
                                                         bass.AP(id16, _h, [[16, 16], [0, 128]]),
                                                         acolT[:], start=True, stop=True), inc=pe)
                pe_g1 = pe.v
                p.wait("vector", pe, pe_g1)
                p.fop("vector", lambda e: e.tensor_copy(G1sb[:], ps_g[:, 0:64]), dv)
                mk_g1 = dv.v
            if hh < H0:
                p.fop("vector", lambda e: e.tensor_scalar_mul(mstatb[:], mstat[:, 8:12], 8.0), dv)
                dv_nm = dv.v
                for s4 in range(4):
                    p.wait("tensor", dv, dv_nm)
                    p.wait("tensor", ac if mk_tr[tr_i][1] == "ac" else dv, mk_tr[tr_i][0])
                    p.op("tensor", lambda e, _s=s4, _t=tr_i: e.transpose(
                        ps_tr[_t][0:1, 0:128], mstatb[:, _s:_s + 1], idb[:]), inc=pe)
                    p.wait("vector", pe)
                    p.op("vector", lambda e, _s=s4, _t=tr_i: e.tensor_copy(
                        nrow[0:1, 128 * _s:128 * (_s + 1)], ps_tr[_t][0:1, 0:128]), inc=dv)
                    mk_tr[tr_i] = (dv.v, "dv")
                    tr_i ^= 1
                dv_nrow = dv.v
                for t4 in range(4):
                    bk = t4 % 3
                    p.wait("tensor", dv, mk_sc[bk][0])
                    p.wait("tensor", ac, mk_sc[bk][1])
                    p.wait("tensor", dv, dv_nrow)
                    p.op("tensor", lambda e, _h=h, _t=t4, _b=bk: e.matmul(
                        ps_sc[_b][:], kh[:, SEG * _h + 128 * _t:SEG * _h + 128 * (_t + 1)],
                        qh[:, SEG * _h:SEG * (_h + 1)], start=True, stop=False))
                    p.op("tensor", lambda e, _b=bk: e.matmul(
                        ps_sc[_b][:], onesb[:], nrow[:], start=False, stop=True), inc=pe)
                    pe_s2 = pe.v
                    p.wait("scalar", pe, pe_s2)
                    p.wait("scalar", pe, pt_rd[r])
                    p.op("scalar", lambda e, _t=t4, _b=bk, _r=r: e.activation(
                        out=PT[_r][:, 512 * _t:512 * (_t + 1)], in_=ps_sc[_b][:], func=AF.Exp, scale=0.125), inc=ac)
                    mk_sc[bk] = (dv.v, ac.v)
                hs_acPT[r] = ac.v
                if "dbg_PT" in io and n == 0 and h == 0:
                    p.wait("gpsimd", ac, ac.v)
                    p.dma("gpsimd", io["dbg_PT"], PT[0][:], inc=st)
                    p.dma("gpsimd", io["dbg_mst"], mstat[:], inc=st)
                    p.dma("gpsimd", io["dbg_nrow"], nrow[:], inc=st)
            if hh > 0:
                p.wait("tensor", dv, dv_memb)
                p.wait("tensor", dv, mk_num)
                for s4 in range(4):
                    p.op("tensor", lambda e, _h=hp, _s=s4: e.matmul(
                        ps_an[:, 256 + 64 * _s:256 + 64 * (_s + 1)],
                        sqh[:, SEG * _h + 128 * _s:SEG * _h + 128 * (_s + 1)],
                        memb[:], start=True, stop=True), inc=pe if s4 == 3 else None)
                pe_num = pe.v
                p.wait("vector", pe, pe_attd)
                p.wait("vector", pe, pe_num)
                p.wait("vector", ac, hs_acl[rp])
                lc0 = 4 if rp == 0 else 12
                p.fop("vector", lambda e, _l=lc0: e.reciprocal(linv[:], mstat[:, _l:_l + 4]), dv)
                for s4 in range(4):
                    p.fop("vector", lambda e, _h=hp, _s=s4: e.scalar_tensor_tensor(
                        t1s[:], ps_an[:, 256 + 64 * _s:256 + 64 * (_s + 1)], rcol[:, 4 * _h + _s:4 * _h + _s + 1],
                        G1sb[:], ALU.mult, ALU.mult), dv)
                    p.fop("vector", lambda e, _h=hp, _s=s4: e.scalar_tensor_tensor(
                        t2s[:], ps_an[:, 64 * _s:64 * (_s + 1)], linv[:, _s:_s + 1],
                        G2all[:, 64 * _h:64 * (_h + 1)], ALU.mult, ALU.mult), dv)
                    p.fop("vector", lambda e: e.tensor_add(atts[:], t1s[:], t2s[:]), dv)
                    dv_att = dv.v
                    if "dbg_PT" in io and n == 0 and hp == 0 and s4 == 0:
                        dbst1 = nc.alloc_sbuf_tensor("a0_dbst1", [128, 64], F32)
                        dbst2 = nc.alloc_sbuf_tensor("a0_dbst2", [128, 64], F32)
                        p.op("vector", lambda e: e.tensor_copy(dbst1[:], ps_an[:, 0:64]))
                        p.op("vector", lambda e: e.tensor_copy(dbst2[:], ps_an[:, 256:320]), inc=dv)
                        p.wait("gpsimd", dv)
                        p.dma("gpsimd", io["dbg_t1"], t1s[:], inc=st)
                        p.dma("gpsimd", io["dbg_t2"], t2s[:], inc=st)
                        p.dma("gpsimd", io["dbg_attd"], dbst1[:], inc=st)
                        p.dma("gpsimd", io["dbg_num"], dbst2[:], inc=st)
                        p.dma("gpsimd", io["dbg_G1"], G1sb[:], inc=st)
                        p.dma("gpsimd", io["dbg_G2"], G2all[:, 0:64], inc=st)
                        p.dma("gpsimd", io["dbg_rc"], rcol[:], inc=st)
                        p.dma("gpsimd", io["dbg_atts"], atts[:], inc=st)
                        p.wait("vector", st, st.v)
                    p.wait("tensor", dv, dv_att)
                    p.wait("tensor", ac if mk_tr[tr_i][1] == "ac" else dv, mk_tr[tr_i][0])
                    p.wait("tensor", pe, pe_wo_prev)
                    p.op("tensor", lambda e, _t=tr_i: e.transpose(ps_tr[_t][0:64, :], atts[:], idb[:]), inc=pe)
                    p.wait("scalar", pe)
                    p.op("scalar", lambda e, _h=hp, _s=s4, _t=tr_i: e.activation(
                        out=vh[0:64, SEG * _h + 128 * _s:SEG * _h + 128 * (_s + 1)], in_=ps_tr[_t][0:64, :],
                        func=AF.Copy))
                    if s4 == 0:
                        p.op("scalar", lambda e, _h=hp, _t=tr_i: e.activation(
                            out=vh[64:128, SEG * _h:SEG * _h + 127], in_=ps_tr[_t][0:64, 1:128],
                            func=AF.Copy), inc=ac)
                    else:
                        p.op("scalar", lambda e, _h=hp, _s=s4, _t=tr_i: e.activation(
                            out=vh[64:128, SEG * _h + 128 * _s - 1:SEG * _h + 128 * (_s + 1) - 1],
                            in_=ps_tr[_t][0:64, :], func=AF.Copy), inc=ac)
                    mk_tr[tr_i] = (ac.v, "ac")
                    tr_i ^= 1
                mk_attd = dv.v
                mk_num = dv.v
        dv_attT = dv.v
        ac_attT = ac.v
        if upto < 7:
            if "dbg_qh" in io:
                p.wait("gpsimd", dv, dv_attT)
                p.wait("gpsimd", ac, ac_attT)
                p.dma("gpsimd", io["dbg_qh"], vh[0:64, :], inc=st)
            continue

        st_x1_prev = st.v
        for pt in range(4):
            i_glob = 4 * n + pt
            p.dma("sync", xs[:], io["x"][128 * i_glob:128 * (i_glob + 1), :], inc=ld)
            p.dma("sync", g0s[:], io["g0"][128 * i_glob:128 * (i_glob + 1), :], inc=ld)
            ld_t = ld.v
            for nh in range(2):
                p.wait("tensor", dv, max(dv_attT, mk_wo, mk_sc[2][0]))
                p.wait("tensor", ac, max(ac_attT, mk_sc[2][1]))
                for rr in range(8):
                    p.op("tensor", lambda e, _p=pt, _n=nh, _r=rr: e.matmul(
                        ps_wo[:], bass.AP(vh, SEG * 4 * _p + 2 * _r, [[H0 * SEG, 128], [SEG, 4], [16, 32]]),
                        Wos[:, 1024 * _r + 512 * _n:1024 * _r + 512 * (_n + 1)],
                        start=(_r == 0), stop=(_r == 7)), inc=pe if rr == 7 else None)
                pe_wo = pe.v
                p.wait("vector", pe, pe_wo)
                p.wait("vector", ld, ld_t)
                p.wait("vector", st, st_x1_prev)
                p.fop("vector", lambda e, _n=nh: e.tensor_mul(tmwo[:], ps_wo[:], g0s[:, 512 * _n:512 * (_n + 1)]), dv)
                p.op("vector", lambda e, _n=nh: e.tensor_add(x1s[:, 512 * _n:512 * (_n + 1)],
                                                             xs[:, 512 * _n:512 * (_n + 1)], tmwo[:]), inc=dv)
                mk_wo = dv.v
            dv_x1 = dv.v
            p.wait("gpsimd", dv, dv_x1)
            p.dma("gpsimd", io["x1"][128 * i_glob:128 * (i_glob + 1), :], x1s[:], inc=st)
            st_x1_prev = st.v
            p.wait("sync", dv, dv_x1)
            mk_sc[2] = (mk_wo, mk_sc[2][1])

    p.wait("gpsimd", st)
    p.wait("gpsimd", ld)
    if wc is not None:
        p.wait("gpsimd", wc, 32)
        p.wait("gpsimd", ws, 16 * 16)



NH = 32
DST = 128
HD = 64
DIN = 2048
QQ = 128
CH = 512
NCH = T // CH
OPK = ["ps0", "ps1", "psA", "psB"]
LNEPS = 1.0e-5
RMSEPS = 1.0e-5
A_FLOOR = 1.0e-4
TWO_PI = float(2.0 * np.pi)
INV_2PI = float(1.0 / (2.0 * np.pi))
MAGIC = 12582912.0


def build_l1a(p, io, nch=NCH, upto=99):
    nc = p.nc
    idb_c = nc.inline_tensor(np.eye(128, dtype=BF), name="l1_idbc").ap()
    idf_c = nc.inline_tensor(np.eye(128, dtype=np.float32), name="l1_idfc").ap()
    sel32_c = nc.inline_tensor(np.eye(32, dtype=np.float32), name="l1_sel32c").ap()
    _t1 = np.where(np.tril(np.ones((128, 128)), -1) > 0, -1e30, 0.0).astype(BF)
    _t2 = np.where(np.tril(np.ones((128, 128)), 0) > 0, -1e30, 0.0).astype(BF)
    tri1_c = nc.inline_tensor(_t1, name="l1_tri1c").ap()
    tri2_c = nc.inline_tensor(_t2, name="l1_tri2c").ap()
    sel32b_c = nc.inline_tensor(np.eye(32, dtype=BF), name="l1_sel32bc").ap()
    onesf_c = nc.inline_tensor(np.ones((1, 128), dtype=np.float32), name="l1_onesfc").ap()

    A = lambda *a, **k: nc.alloc_sbuf_tensor(*a, **k)
    WinT = A("l1_WinT", [128, 8 * 4480], BF16)
    xnTs = A("l1_xnT", [128, 4 * 1024], BF16)
    zxTs = A("l1_zx", [128, 32 * 512], BF16)
    xDg = A("l1_xDg", [128, 16 * 512], BF16)
    Xtok = A("l1_Xtok", [128, 2 * 2048], BF16)
    wot = [A("l1_wot0", [128, 1024], BF16), A("l1_wot1", [128, 1024], BF16)]
    dts = A("l1_dts", [32, 512], F32)
    ars = A("l1_ars", [32, 512], F32)
    trs = A("l1_trs", [32, 512], F32)
    rps = A("l1_rps", [32, 512], F32)
    dtv = A("l1_dtv", [32, 512], F32)
    gam = A("l1_gam", [32, 512], F32)
    adt = A("l1_adt", [32, 512], F32)
    ssc = A("l1_ssc", [32, 512], F32)
    lna = A("l1_lna", [32, 512], F32)
    lng = A("l1_lng", [32, 520], F32)
    angs = A("l1_angs", [32, 512], F32)
    tmp32 = A("l1_t32a", [32, 512], F32)
    angc = A("l1_angc", [32, 1], F32)
    scl1 = A("l1_scl1", [32, 8], F32)
    bct = A("l1_bct", [128, 4 * 256], BF16)
    bcd = A("l1_bcd", [128, 2 * 512], BF16)
    cw = [A("l1_cw0", [128, 128], BF16), A("l1_cw1", [128, 128], BF16)]
    bv = [A("l1_bv0", [128, 128], BF16), A("l1_bv1", [128, 128], BF16)]
    msk1 = [A("l1_msk1a", [128, 128], BF16), A("l1_msk1b", [128, 128], BF16)]
    msk2 = [A("l1_msk2a", [128, 128], BF16), A("l1_msk2b", [128, 128], BF16)]
    cmask = [A("l1_cmaska", [128, 128], BF16), A("l1_cmaskb", [128, 128], BF16)]
    mskd = [A("l1_mskda", [128, 128], BF16), A("l1_mskdb", [128, 128], BF16)]
    ypre = A("l1_ypre", [128, 128], BF16)
    h_sb = A("l1_h", [128, NH * 64], F32)
    h_bb = [A("l1_hbba", [128, 64], BF16), A("l1_hbbb", [128, 64], BF16)]
    v1T = A("l1_v1T", [128, 32], F32)
    v2T = A("l1_v2T", [128, 32], F32)
    expSm = A("l1_expSm", [32, 128], F32)
    expST = A("l1_expST", [128, 32], F32)
    estot = A("l1_estot", [128, 32], F32)
    esrow = A("l1_esrow", [1, 32], F32)
    wT = A("l1_wT", [128, 32], F32)
    Dbc = A("l1_Dbc", [128, 32], F32)
    nwe = {nm: A("l1_nw" + nm, [128, 32], F32) for nm in ("Be", "Bo", "Ce", "Co")}
    rowb = A("l1_rowb", [1, 128], BF16)
    xprow = A("l1_xprow", [1, 2048], BF16)
    xpS = A("l1_xpS", [1, 2048], BF16)
    xpS2 = A("l1_xpS2", [1, 2048], BF16)
    srow = A("l1_srow", [1, 32], F32)
    srow2 = A("l1_srow2", [1, 32], F32)
    drow = A("l1_drow", [1, 32], F32)
    ctk = A("l1_ctk", [128, 4 * 32], F32)
    stk = A("l1_stk", [128, 4 * 32], F32)
    rc = {nm: A("l1_rc" + nm, [128, 32], F32)
          for nm in ("cEB", "sOB", "sEB", "cOB", "cEC", "sOC", "sEC", "cOC")}
    bcst = A("l1_bcst", [128, 8], F32)
    vcol = [A("l1_vcola", [128, 1], F32), A("l1_vcolb", [128, 1], F32)]
    epscol = A("l1_eps", [128, 1], F32)
    onecol = A("l1_one", [128, 1], F32)
    nwcol = A("l1_nwc", [128, 1], F32)
    nwcol2 = A("l1_nwc2", [128, 1], F32)
    idb = A("l1_idb", [128, 128], BF16)
    idf = A("l1_idf", [128, 128], F32)
    sel32 = A("l1_sel32", [32, 32], F32)
    tri1 = A("l1_tri1", [128, 128], BF16)
    tri2 = A("l1_tri2", [128, 128], BF16)
    sel32b = A("l1_sel32b", [32, 32], BF16)
    stil = A("l1_stil", [32, 128], F32)
    shib = A("l1_shib", [32, 128], BF16)
    slob = A("l1_slob", [32, 128], BF16)
    onesf = A("l1_onesf", [1, 128], F32)
    dtbc = A("l1_dtbc", [32, 1], F32)
    dsk = A("l1_dsk", [32, 1], F32)
    lnx = A("l1_lnx", [128, 1024], F32)
    lnst = A("l1_lnst", [128, 8], F32)
    bcf = lnx
    dmp = A("l1_dmp", [128, 512], F32)
    xnbh = A("l1_xnbh", [128, 512], BF16)
    dmpb = A("l1_dmpb", [128, 512], BF16)
    dmpb2 = A("l1_dmpb2", [128, 512], BF16)

    ps0 = nc.alloc_psum_tensor("l1_ps0", [128, 512], F32)
    ps1 = nc.alloc_psum_tensor("l1_ps1", [128, 512], F32)
    psA = nc.alloc_psum_tensor("l1_psA", [128, 512], F32)
    psB = nc.alloc_psum_tensor("l1_psB", [128, 512], F32)
    psS = nc.alloc_psum_tensor("l1_psS", [128, 128], F32)
    psY0 = nc.alloc_psum_tensor("l1_psY0", [128, 128], F32)
    psY1 = nc.alloc_psum_tensor("l1_psY1", [128, 128], F32)
    ps_trb = nc.alloc_psum_tensor("l1_pstrb", [128, 128], BF16)
    ps_gf = psA
    pss = {"ps0": ps0, "ps1": ps1, "psA": psA, "psB": psB,
           "psS": psS, "psY0": psY0, "psY1": psY1, "ps_trb": ps_trb}

    ld = p.sem("l1_ld")
    lw = p.sem("l1_lw")
    pe = p.sem("l1_pe")
    dv = p.sem("l1_dv")
    ac = p.sem("l1_ac")
    gp = p.sem("l1_gp")
    st = p.sem("l1_st")

    mk = {k: (0, dv) for k in pss}
    mk["ps_gf"] = (0, dv)

    def pwait(eng, key):
        v, s = mk[key]
        p.wait(eng, s, v)

    def pset(key, sem):
        mk[key] = (sem.v, sem)

    p.dma("sync", idb[:], idb_c, inc=ld)
    p.dma("sync", idf[:], idf_c, inc=ld)
    p.dma("sync", sel32[:], sel32_c, inc=ld)
    p.dma("sync", tri1[:], tri1_c, inc=ld)
    p.dma("sync", tri2[:], tri2_c, inc=ld)
    p.dma("sync", sel32b[:], sel32b_c, inc=ld)
    p.dma("sync", onesf[:], onesf_c, inc=ld)
    p.dma("sync", dtbc[:], io["dt_bias"][:, None], inc=ld)
    p.dma("sync", dsk[:], io["D_skip"][:, None], inc=ld)
    for c in range(8):
        p.dma("sync", WinT[:, 4480 * c:4480 * (c + 1)], io["scr_win"][:, 128 * c:128 * (c + 1)],
              transpose=True, inc=lw)

    if "dbg_o" in io:
        p.wait("gpsimd", st, st.v)
        p.dma("gpsimd", io["dbg_o"], io["scr_woutT"][0:128, :], inc=st)
    p.wait("tensor", ld, ld.v)
    p.op("tensor", lambda e: e.transpose(ps_gf[0:1, 0:32], dsk[:], idf[0:32, 0:32]), inc=pe)
    p.wait("vector", pe)
    p.fop("vector", lambda e: e.tensor_copy(drow[:], ps_gf[0:1, 0:32]), dv)
    p.wait("tensor", dv)
    p.op("tensor", lambda e: e.matmul(ps_gf[:, 0:32], onesf[:], drow[:], start=True, stop=True), inc=pe)
    p.wait("vector", pe)
    p.fop("vector", lambda e: e.tensor_copy(Dbc[:], ps_gf[:, 0:32]), dv)
    for nm, src_ap in (("Be", io["B_norm_w"][0:64:2][None, :]), ("Bo", io["B_norm_w"][1:64:2][None, :]),
                       ("Ce", io["C_norm_w"][0:64:2][None, :]), ("Co", io["C_norm_w"][1:64:2][None, :])):
        p.wait("sync", pe, pe.v)
        p.dma("sync", drow[:], src_ap, inc=ld, allow_slow_non_contiguous=True)
        p.wait("tensor", ld, ld.v)
        p.wait("tensor", dv, dv.v)
        p.op("tensor", lambda e: e.matmul(ps_gf[:, 0:32], onesf[:], drow[:], start=True, stop=True), inc=pe)
        p.wait("vector", pe)
        p.fop("vector", lambda e, _n=nm: e.tensor_copy(nwe[_n][:], ps_gf[:, 0:32]), dv)
    p.dma("sync", ssc[0:32, 0:128], io["B_bias"], inc=ld)
    p.dma("sync", lna[0:32, 0:128], io["C_bias"], inc=ld)
    p.dma("sync", nwcol[:], io["B_norm_w"][:, None], inc=ld)
    p.dma("sync", nwcol2[:], io["C_norm_w"][:, None], inc=ld)
    p.wait("vector", ld, ld.v)
    p.fop("vector", lambda e: e.tensor_mul(ssc[0:32, 0:128], ssc[0:32, 0:128], lna[0:32, 0:128]), dv)
    p.wait("tensor", dv)
    p.op("tensor", lambda e: e.transpose(ps_gf[:, 0:32], ssc[0:32, 0:128], idf[0:32, 0:32]), inc=pe)
    p.wait("vector", pe)
    p.fop("vector", lambda e: e.tensor_copy(wT[:], ps_gf[:, 0:32]), dv)
    p.fop("vector", lambda e: e.tensor_mul(nwcol[:], nwcol[:], nwcol2[:]), dv)
    p.op("vector", lambda e: e.memset(nwcol[0:64, :], 1.0), inc=dv)
    p.wait("vector", dv)
    p.fop("vector", lambda e: e.tensor_scalar_mul(wT[:], wT[:], nwcol[:]), dv)
    p.op("vector", lambda e: e.memset(epscol[:], RMSEPS), inc=dv)
    p.op("vector", lambda e: e.memset(onecol[:], 1.0), inc=dv)
    p.op("vector", lambda e: e.memset(h_sb[:], 0.0), inc=dv)
    p.op("vector", lambda e: e.memset(angc[:], 0.0), inc=dv)
    p.op("vector", lambda e: e.memset(xprow[:], 0.0), inc=dv)
    p.op("vector", lambda e: e.memset(rowb[:], 0.0), inc=dv)
    p.op("vector", lambda e: e.memset(lng[0:32, 512:520], -30.0), inc=dv)
    pset("psA", dv)

    M1, R1, SU, SS, SC, NG, EPC = 0, 1, 2, 3, 4, 5, 6
    p.op("vector", lambda e: e.memset(lnst[:, EPC:EPC + 1], LNEPS), inc=dv)
    st_ln = {0: 0, 1: 0}
    dv_lnx_prev = 0
    ac_lnx_prev = 0
    pe_xnbh_prev = 0
    for i in range(NT):
        p.wait("sync", dv, dv_lnx_prev)
        p.wait("sync", ac, ac_lnx_prev)
        p.dma("sync", lnx[:], io["x1"][128 * i:128 * (i + 1), :], inc=ld)
        ld_i = ld.v
        p.wait("vector", ld, ld_i)
        p.fop("vector", lambda e: e.reduce_sum(lnst[:, SU:SU + 1], lnx[:], axis=AX.X), dv)
        p.wait("scalar", ld, ld_i)
        p.wait("scalar", dv, dv.v)
        p.op("scalar", lambda e: e.activation(out=dmp[:], in_=lnx[:, 0:512], func=AF.Square,
                                              accum_out=lnst[:, SS:SS + 1]))
        p.op("scalar", lambda e: e.activation(out=dmp[:], in_=lnx[:, 512:1024], func=AF.Square,
                                              accum_out=lnst[:, 7:8]), inc=ac)
        ac_lnx_prev = ac.v
        p.wait("vector", ac)
        p.op("vector", lambda e: e.tensor_scalar_mul(lnst[:, M1:M1 + 1], lnst[:, SU:SU + 1], 1.0 / D))
        p.fop("vector", lambda e: e.tensor_add(lnst[:, SC:SC + 1], lnst[:, SS:SS + 1], lnst[:, 7:8]), dv)
        p.op("vector", lambda e: e.tensor_scalar_mul(lnst[:, SC:SC + 1], lnst[:, SC:SC + 1], 1.0 / D))
        p.fop("vector", lambda e: e.tensor_scalar_mul(lnst[:, NG:NG + 1], lnst[:, M1:M1 + 1], -1.0), dv)
        p.fop("vector", lambda e: e.scalar_tensor_tensor(lnst[:, SC:SC + 1], lnst[:, M1:M1 + 1],
                                                         lnst[:, NG:NG + 1], lnst[:, SC:SC + 1],
                                                         ALU.mult, ALU.add), dv)
        p.wait("scalar", dv)
        p.op("scalar", lambda e: e.activation(out=lnst[:, R1:R1 + 1], in_=lnst[:, SC:SC + 1], func=AF.Ln,
                                              bias=lnst[:, EPC:EPC + 1], scale=1.0), inc=ac)
        p.wait("scalar", ac)
        p.op("scalar", lambda e: e.activation(out=lnst[:, R1:R1 + 1], in_=lnst[:, R1:R1 + 1], func=AF.Exp,
                                              scale=-0.5), inc=ac)
        p.wait("vector", ac)
        for half in range(2):
            db = (dmpb, dmpb2)[half]
            p.wait("vector", pe, pe_xnbh_prev)
            p.fop("vector", lambda e, _h=half: e.tensor_scalar(
                xnbh[:], lnx[:, 512 * _h:512 * (_h + 1)], lnst[:, M1:M1 + 1], lnst[:, R1:R1 + 1],
                ALU.subtract, ALU.mult), dv)
            dv_xn = dv.v
            if half == 1:
                dv_lnx_prev = dv_xn
            p.wait("tensor", dv, dv_xn)
            for k in range(4):
                pwait("tensor", "ps_trb")
                p.op("tensor", lambda e, _k=k: e.transpose(ps_trb[:, 0:128], xnbh[:, 128 * _k:128 * (_k + 1)],
                                                            idb[:]), inc=pe)
                p.wait("vector", pe)
                p.wait("vector", st, st_ln[half])
                p.op("vector", lambda e, _k=k, _d=db: e.tensor_copy(_d[:, 128 * _k:128 * (_k + 1)], ps_trb[:, 0:128]),
                     inc=dv)
                pset("ps_trb", dv)
            pe_xnbh_prev = pe.v
            p.wait("gpsimd", dv, dv.v)
            p.dma("gpsimd", io["xn1T"][:, 1024 * i + 512 * half:1024 * i + 512 * (half + 1)], db[:], inc=st)
            st_ln[half] = st.v
    if upto < 1:
        p.wait("gpsimd", st)
        return

    lw_wt = lw.v
    p.wait("tensor", lw, lw_wt)
    dv_gated_prev = 0
    pe_op_prev = 0
    gp_xd_prev = 0
    pemark_wb = {0: 0, 1: 0}
    mark_cw = [0, 0]
    mark_msk = [0, 0]
    mark_cmask = [0, 0]
    mark_mskd = [0, 0]
    mark_hbb = [0, 0]
    mark_bv = [0, 0]
    mark_vcol = [0, 0]
    mark_ypre = 0
    mk_xtok = {0: (0, dv), 1: (0, dv)}
    xtok_ev = {0: (dv, 0), 1: (dv, 0)}
    ld_carry = ld.v
    for ci in range(nch):
        p.wait("sync", pe, pe.v)
        p.dma("sync", xnTs[:], io["xn1T"][:, 4096 * ci:4096 * (ci + 1)], inc=ld)
        ld_ch = ld.v

        p.wait("tensor", ld, ld_ch)
        for mt in range(32):
            key = "ps0" if mt % 2 == 0 else "ps1"
            pb = pss[key]
            pwait("tensor", key)
            for c in range(KC):
                p.op("tensor", lambda e, _c=c, _m=mt, _pb=pb: e.matmul(
                    _pb[:], WinT[:, 4480 * _c + 128 * _m:4480 * _c + 128 * (_m + 1)],
                    ap3(xnTs, 128 * _c, [[1024, 4], [1, 128]]),
                    start=(_c == 0), stop=(_c == KC - 1)), inc=pe if c == KC - 1 else None)
            pe_mm = pe.v
            p.wait("scalar", pe, pe_mm)
            p.wait("scalar", dv, dv_gated_prev)
            p.wait("scalar", gp, gp_xd_prev)
            p.op("scalar", lambda e, _m=mt, _pb=pb: e.activation(
                out=zxTs[:, 512 * _m:512 * (_m + 1)], in_=_pb[:],
                func=(AF.Silu if _m < 16 else AF.Copy)), inc=ac)
            mk[key] = (ac.v, ac)
        ac_zx = ac.v

        for j, dst in ((0, dts), (1, ars), (2, trs), (3, rps)):
            key = "psA" if j % 2 == 0 else "psB"
            pb = pss[key]
            pwait("tensor", key)
            for c in range(KC):
                p.op("tensor", lambda e, _c=c, _j=j, _pb=pb: e.matmul(
                    _pb[0:32, :], WinT[:, 4480 * _c + 4352 + 32 * _j:4480 * _c + 4352 + 32 * (_j + 1)],
                    ap3(xnTs, 128 * _c, [[1024, 4], [1, 128]]),
                    start=(_c == 0), stop=(_c == KC - 1)), inc=pe if c == KC - 1 else None)
            pe_mm = pe.v
            p.wait("vector", pe, pe_mm)
            p.fop("vector", lambda e, _d=dst, _pb=pb: e.tensor_copy(_d[:], _pb[0:32, :]), dv)
            mk[key] = (dv.v, dv)

        p.fop("vector", lambda e: e.tensor_scalar(tmp32[:], dts[:], dtbc[:], None, ALU.add), dv)
        p.fop("vector", lambda e: e.scalar_tensor_tensor(gam[:], tmp32[:], -1.0, tmp32[:], ALU.mult, ALU.max), dv)
        p.wait("scalar", dv)
        p.op("scalar", lambda e: e.activation(out=gam[:], in_=gam[:], func=AF.Exp, scale=-1.0), inc=ac)
        p.wait("scalar", ac)
        p.op("scalar", lambda e: e.activation(out=gam[:], in_=gam[:], func=AF.Ln,
                                              bias=onecol[0:32, 0:1], scale=1.0), inc=ac)
        p.wait("vector", ac)
        p.fop("vector", lambda e: e.scalar_tensor_tensor(dtv[:], tmp32[:], 0.0, gam[:], ALU.max, ALU.add), dv)
        p.fop("vector", lambda e: e.scalar_tensor_tensor(gam[:], ars[:], -1.0, ars[:], ALU.mult, ALU.max), dv)
        p.wait("scalar", dv)
        p.op("scalar", lambda e: e.activation(out=gam[:], in_=gam[:], func=AF.Exp, scale=-1.0), inc=ac)
        p.wait("scalar", ac)
        p.op("scalar", lambda e: e.activation(out=gam[:], in_=gam[:], func=AF.Ln,
                                              bias=onecol[0:32, 0:1], scale=1.0), inc=ac)
        p.wait("vector", ac)
        p.fop("vector", lambda e: e.scalar_tensor_tensor(gam[:], ars[:], 0.0, gam[:], ALU.max, ALU.add), dv)
        p.fop("vector", lambda e: e.tensor_scalar_add(gam[:], gam[:], A_FLOOR), dv)
        p.fop("vector", lambda e: e.tensor_mul(adt[:], dtv[:], gam[:]), dv)
        p.fop("vector", lambda e: e.tensor_scalar_mul(adt[:], adt[:], -1.0), dv)
        p.op("vector", lambda e: e.memset(tmp32[:], 1.0), inc=dv)
        p.wait("vector", dv)
        p.fop("vector", lambda e: e.tensor_tensor_scan(ssc[:], tmp32[:], adt[:], 0.0, ALU.mult, ALU.add), dv)
        p.wait("scalar", dv)
        p.op("scalar", lambda e: e.activation(out=trs[:], in_=trs[:], func=AF.Sigmoid), inc=ac)
        p.wait("vector", ac)
        p.fop("vector", lambda e: e.tensor_scalar(tmp32[:], trs[:], -0.5, 1.0, ALU.mult, ALU.add), dv)
        p.fop("vector", lambda e: e.tensor_mul(tmp32[:], tmp32[:], dtv[:]), dv)
        p.wait("scalar", dv)
        p.op("scalar", lambda e: e.activation(out=lna[:, 0:512], in_=tmp32[:], func=AF.Ln), inc=ac)
        p.wait("vector", ac)
        p.fop("vector", lambda e: e.tensor_mul(gam[:], trs[:], dtv[:]), dv)
        p.fop("vector", lambda e: e.tensor_scalar_mul(gam[:], gam[:], 0.5), dv)
        p.wait("scalar", dv)
        p.op("scalar", lambda e: e.activation(out=lng[:, 0:512], in_=gam[:], func=AF.Ln), inc=ac)
        p.wait("vector", ac)
        p.fop("vector", lambda e: e.tensor_sub(lna[:, 0:512], lna[:, 0:512], ssc[:]), dv)
        p.fop("vector", lambda e: e.tensor_sub(lng[:, 0:512], lng[:, 0:512], ssc[:]), dv)
        p.op("vector", lambda e: e.memset(tmp32[:], 1.0), inc=dv)
        p.wait("vector", dv)
        p.fop("vector", lambda e: e.tensor_tensor_scan(angs[:], tmp32[:], rps[:], angc[:], ALU.mult, ALU.add), dv)
        p.fop("vector", lambda e: e.tensor_copy(angc[:], angs[:, 511:512]), dv)
        p.fop("vector", lambda e: e.tensor_scalar_mul(tmp32[:], angs[:], INV_2PI), dv)
        p.fop("vector", lambda e: e.tensor_scalar(dtv[:], tmp32[:], MAGIC, MAGIC, ALU.add, ALU.subtract), dv)
        p.fop("vector", lambda e: e.tensor_sub(dtv[:], tmp32[:], dtv[:]), dv)
        p.fop("vector", lambda e: e.tensor_scalar_mul(dtv[:], dtv[:], TWO_PI), dv)
        p.wait("scalar", dv)
        p.op("scalar", lambda e: e.activation(out=dts[:], in_=dtv[:], func=AF.Sin), inc=ac)
        p.fop("vector", lambda e: e.tensor_scalar_add(tmp32[:], tmp32[:], 0.25), dv)
        p.fop("vector", lambda e: e.tensor_scalar(dtv[:], tmp32[:], MAGIC, MAGIC, ALU.add, ALU.subtract), dv)
        p.fop("vector", lambda e: e.tensor_sub(dtv[:], tmp32[:], dtv[:]), dv)
        p.fop("vector", lambda e: e.tensor_scalar_mul(dtv[:], dtv[:], TWO_PI), dv)
        p.wait("scalar", dv)
        p.op("scalar", lambda e: e.activation(out=ars[:], in_=dtv[:], func=AF.Sin), inc=ac)
        ac_cs = ac.v

        p.wait("tensor", ac, ac_cs)
        p.wait("vector", ld, ld_carry)
        for tt in range(4):
            for dst_, src_ in ((ctk, ars), (stk, dts)):
                pwait("tensor", "psA")
                p.op("tensor", lambda e, _t=tt, _s=src_: e.transpose(
                    ps_gf[:, 0:32], _s[:, 128 * _t:128 * (_t + 1)], idf[0:32, 0:32]), inc=pe)
                p.wait("vector", pe)
                p.fop("vector", lambda e, _t=tt, _d=dst_: e.tensor_copy(_d[:, 32 * _t:32 * (_t + 1)],
                                                                        ps_gf[:, 0:32]), dv)
                pset("psA", dv)
            key = "psA" if tt % 2 == 0 else "psB"
            pb = pss[key]
            pwait("tensor", key)
            for c in range(KC):
                p.op("tensor", lambda e, _c=c, _t=tt, _pb=pb: e.matmul(
                    _pb[:, 0:256], xnTs[:, 1024 * _t + 128 * _c:1024 * _t + 128 * (_c + 1)],
                    WinT[:, 4480 * _c + 4096:4480 * _c + 4352],
                    start=(_c == 0), stop=(_c == KC - 1)), inc=pe if c == KC - 1 else None)
            pe_bc = pe.v
            p.wait("scalar", pe, pe_bc)
            p.op("scalar", lambda e, _pb=pb: e.activation(out=dmp[:, 0:128], in_=_pb[:, 0:128],
                                                          func=AF.Square, accum_out=bcst[:, 0:1]))
            p.op("scalar", lambda e, _pb=pb: e.activation(out=dmp[:, 128:256], in_=_pb[:, 128:256],
                                                          func=AF.Square, accum_out=bcst[:, 1:2]), inc=ac)
            p.wait("vector", ac)
            p.fop("vector", lambda e: e.tensor_scalar_mul(bcst[:, 2:4], bcst[:, 0:2], 1.0 / 128.0), dv)
            p.wait("scalar", dv)
            p.op("scalar", lambda e: e.activation(out=bcst[:, 2:4], in_=bcst[:, 2:4], func=AF.Ln,
                                                  bias=epscol[:, 0:1], scale=1.0), inc=ac)
            p.wait("scalar", ac)
            p.op("scalar", lambda e: e.activation(out=bcst[:, 2:4], in_=bcst[:, 2:4], func=AF.Exp,
                                                  scale=-0.5), inc=ac)
            p.wait("vector", ac)
            p.fop("vector", lambda e, _pb=pb: e.tensor_scalar_mul(bcf[:, 0:128], _pb[:, 0:128],
                                                                  bcst[:, 2:3]), dv)
            p.fop("vector", lambda e, _pb=pb: e.tensor_scalar_mul(bcf[:, 128:256], _pb[:, 128:256],
                                                                  bcst[:, 3:4]), dv)
            mk[key] = (dv.v, dv)
            for dstc, a_, b_ in (("cEB", ctk, "Be"), ("sOB", stk, "Bo"), ("sEB", stk, "Be"), ("cOB", ctk, "Bo"),
                                 ("cEC", ctk, "Ce"), ("sOC", stk, "Co"), ("sEC", stk, "Ce"), ("cOC", ctk, "Co")):
                p.op("vector", lambda e, _d=dstc, _a=a_, _b=b_, _t=tt: e.tensor_mul(
                    rc[_d][:], _a[:, 32 * _t:32 * (_t + 1)], nwe[_b][:]))
            p.fop("vector", lambda e: e.engine_nop(), dv)
            for half, (ce, so, se, co) in ((0, ("cEB", "sOB", "sEB", "cOB")), (1, ("cEC", "sOC", "sEC", "cOC"))):
                base = 128 * half
                ev = bass.AP(bcf, base, [[1024, 128], [2, 32]])
                od = bass.AP(bcf, base + 1, [[1024, 128], [2, 32]])
                t_ev = bass.AP(bct, 256 * tt + base, [[4 * 256, 128], [2, 32]])
                t_od = bass.AP(bct, 256 * tt + base + 1, [[4 * 256, 128], [2, 32]])
                p.fop("vector", lambda e, _e=ev, _c=ce: e.tensor_mul(dmp[:, 0:32], _e, rc[_c][:]), dv)
                p.fop("vector", lambda e, _o=od, _s=so: e.tensor_mul(dmp[:, 32:64], _o, rc[_s][:]), dv)
                p.fop("vector", lambda e, _t=t_ev: e.tensor_sub(_t, dmp[:, 0:32], dmp[:, 32:64]), dv)
                p.fop("vector", lambda e, _e=ev, _s=se: e.tensor_mul(dmp[:, 0:32], _e, rc[_s][:]), dv)
                p.fop("vector", lambda e, _o=od, _c=co: e.tensor_mul(dmp[:, 32:64], _o, rc[_c][:]), dv)
                p.fop("vector", lambda e, _t=t_od: e.tensor_add(_t, dmp[:, 0:32], dmp[:, 32:64]), dv)
                p.fop("vector", lambda e, _t=tt, _b=base: e.tensor_copy(
                    bct[:, 256 * _t + _b + 64:256 * _t + _b + 128], bcf[:, _b + 64:_b + 128]), dv)
            dv_bct = dv.v
            for half in range(2):
                pwait("tensor", "ps_trb")
                p.wait("tensor", dv, dv_bct)
                p.op("tensor", lambda e, _t=tt, _h=half: e.transpose(
                    ps_trb[:, 0:128], bct[:, 256 * _t + 128 * _h:256 * _t + 128 * (_h + 1)], idb[:]), inc=pe)
                p.wait("vector", pe)
                p.op("vector", lambda e, _t=tt, _h=half: e.tensor_copy(
                    bcd[:, 512 * _h + 128 * _t:512 * _h + 128 * (_t + 1)], ps_trb[:, 0:128]), inc=dv)
                pset("ps_trb", dv)
            ht = tt // 2
            if tt % 2 == 0:
                v0h, s0h = mk_xtok[ht]
                p.wait("tensor", s0h, v0h)
            for it in range(8 * (tt % 2), 8 * (tt % 2) + 8):
                pwait("tensor", "ps_trb")
                p.op("tensor", lambda e, _t=ht, _i=it: e.transpose(
                    ps_trb[:, 0:128], zxTs[:, 512 * (16 + _i) + 128 * _t:512 * (16 + _i) + 128 * (_t + 1)],
                    idb[:]), inc=pe)
                p.wait("scalar", pe)
                p.op("scalar", lambda e, _s=ht, _i=it: e.activation(
                    out=Xtok[:, 2048 * _s + 128 * _i:2048 * _s + 128 * (_i + 1)], in_=ps_trb[:, 0:128],
                    func=AF.Copy), inc=ac)
                pset("ps_trb", ac)
            xtok_ev[ht] = (ac, ac.v)
        dv_bcd = dv.v

        p.wait("gpsimd", pe, pe_op_prev)
        p.wait("gpsimd", ac, ac_zx)
        for it in range(16):
            p.op("gpsimd", lambda e, _i=it: e.tensor_scalar_mul(
                xDg[0:64, 512 * _i:512 * (_i + 1)], zxTs[0:64, 512 * (16 + _i):512 * (17 + _i)],
                Dbc[0:64, 2 * _i:2 * _i + 1]))
            p.op("gpsimd", lambda e, _i=it: e.tensor_scalar_mul(
                xDg[64:128, 512 * _i:512 * (_i + 1)], zxTs[64:128, 512 * (16 + _i):512 * (17 + _i)],
                Dbc[64:128, 2 * _i + 1:2 * _i + 2]), inc=gp)
        gp_xd = gp.v
        gp_xd_prev = gp_xd

        if upto < 5:
            if "dbg_a" in io and ci == 0:
                p.wait("gpsimd", dv, dv.v)
                p.wait("gpsimd", ac, ac.v)
                p.dma("gpsimd", io["dbg_a"], ssc[:], inc=st)
                p.dma("gpsimd", io["dbg_b"], lna[:, 0:512], inc=st)
                p.dma("gpsimd", io["dbg_c"], lng[:, 0:512], inc=st)
                p.dma("gpsimd", io["dbg_d"], angs[:], inc=st)
                p.dma("gpsimd", io["dbg_e"], bct[:], inc=st)
                p.dma("gpsimd", io["dbg_f"], Xtok[:, 0:2048], inc=st)
                p.dma("gpsimd", io["dbg_g"], zxTs[:, 0:512], inc=st)
                p.dma("gpsimd", io["dbg_h"], bcd[:], inc=st)
                p.dma("gpsimd", io["dbg_i"], dts[:], inc=st)
                p.dma("gpsimd", io["dbg_j"], ars[:], inc=st)
                p.dma("gpsimd", io["dbg_k"], ctk[:], inc=st)
                p.dma("gpsimd", io["dbg_l"], stk[:], inc=st)
                p.dma("gpsimd", io["dbg_m"], rc["cEB"][:], inc=st)
                p.dma("gpsimd", io["dbg_n"], bcf[:, 0:256], inc=st)
            continue

        ld_xprow_hist = {}
        for tt in range(4):
            sl = tt % 2
            if tt >= 2:
                p.wait("tensor", ac, ac_zx)
                p.wait("tensor", ld, ld_carry)
                v0, s0_ = mk_xtok[sl]
                p.wait("tensor", s0_, v0)
                for it in range(16):
                    pwait("tensor", "ps_trb")
                    p.op("tensor", lambda e, _t=tt, _i=it: e.transpose(
                        ps_trb[:, 0:128], zxTs[:, 512 * (16 + _i) + 128 * _t:512 * (16 + _i) + 128 * (_t + 1)],
                        idb[:]), inc=pe)
                    p.wait("scalar", pe)
                    p.op("scalar", lambda e, _s=sl, _i=it: e.activation(
                        out=Xtok[:, 2048 * _s + 128 * _i:2048 * _s + 128 * (_i + 1)], in_=ps_trb[:, 0:128],
                        func=AF.Copy), inc=ac)
                    pset("ps_trb", ac)
                xtok_ev[sl] = (ac, ac.v)
            xe_sem, xe_val = xtok_ev[sl]
            if tt == 0:
                p.fop("vector", lambda e, _t=tt: e.tensor_copy(stil[:], ssc[:, 128 * _t:128 * (_t + 1)]), dv)
            else:
                p.fop("vector", lambda e, _t=tt: e.tensor_scalar(
                    stil[:], ssc[:, 128 * _t:128 * (_t + 1)], ssc[:, 128 * _t - 1:128 * _t], None,
                    ALU.subtract), dv)
            p.fop("vector", lambda e: e.tensor_copy(shib[:], stil[:]), dv)
            p.fop("vector", lambda e: e.tensor_sub(slob[:], stil[:], shib[:]), dv)
            p.wait("scalar", dv)
            p.op("scalar", lambda e: e.activation(out=expSm[:], in_=stil[:], func=AF.Exp), inc=ac)
            p.wait("tensor", ac)
            pwait("tensor", "psA")
            p.op("tensor", lambda e: e.transpose(ps_gf[:, 0:32], expSm[:], idf[0:32, 0:32]), inc=pe)
            p.wait("vector", pe)
            p.fop("vector", lambda e: e.tensor_copy(expST[:], ps_gf[:, 0:32]), dv)
            pset("psA", dv)
            p.fop("vector", lambda e: e.tensor_copy(scl1[0:32, 0:1], expSm[:, 127:128]), dv)
            p.wait("tensor", dv)
            pwait("tensor", "psA")
            p.op("tensor", lambda e: e.transpose(ps_gf[0:1, 0:32], scl1[0:32, 0:1], idf[0:32, 0:32]), inc=pe)
            p.wait("vector", pe)
            p.fop("vector", lambda e: e.tensor_copy(esrow[:], ps_gf[0:1, 0:32]), dv)
            p.wait("tensor", dv)
            p.op("tensor", lambda e: e.matmul(ps_gf[:, 0:32], onesf[:], esrow[:], start=True, stop=True), inc=pe)
            p.wait("vector", pe)
            p.fop("vector", lambda e: e.tensor_copy(estot[:], ps_gf[:, 0:32]), dv)
            pset("psA", dv)
            p.wait("vector", ac, ac.v)
            for dst_, src_t, off in ((v1T, lna, 0), (v2T, lng, 1)):
                if tt == 0:
                    p.fop("vector", lambda e, _s=src_t, _o=off, _t=tt: e.tensor_copy(
                        stil[:], _s[:, 128 * _t + _o:128 * _t + _o + 128]), dv)
                else:
                    p.fop("vector", lambda e, _s=src_t, _o=off, _t=tt: e.tensor_scalar(
                        stil[:], _s[:, 128 * _t + _o:128 * _t + _o + 128],
                        ssc[:, 128 * _t - 1:128 * _t], None, ALU.add), dv)
                pwait("tensor", "psA")
                p.wait("tensor", dv, dv.v)
                p.op("tensor", lambda e: e.transpose(ps_gf[:, 0:32], stil[:], idf[0:32, 0:32]), inc=pe)
                p.wait("vector", pe)
                p.fop("vector", lambda e, _d=dst_: e.tensor_copy(_d[:], ps_gf[:, 0:32]), dv)
                pset("psA", dv)
            if tt > 0:
                p.wait("scalar", dv, dv_bcd)
                p.wait("scalar", pe, pe.v)
                p.dma("scalar", rowb[:], bct[127:128, 256 * (tt - 1):256 * (tt - 1) + 128], inc=ld)
                pxe_sem, pxe_val = xtok_ev[(tt - 1) % 2]
                p.wait("scalar", pxe_sem, pxe_val)
                p.dma("scalar", xprow[:], Xtok[127:128, 2048 * ((tt - 1) % 2):2048 * ((tt - 1) % 2) + 2048], inc=ld)
            ld_rows = ld.v
            p.fop("vector", lambda e, _t=tt: e.tensor_copy(scl1[0:32, 1:2], adt[:, 128 * _t:128 * _t + 1]), dv)
            p.wait("scalar", dv)
            p.op("scalar", lambda e: e.activation(out=scl1[0:32, 2:3], in_=scl1[0:32, 1:2], func=AF.Exp,
                                                  scale=-1.0), inc=ac)
            p.wait("vector", ac)
            p.fop("vector", lambda e, _t=tt: e.tensor_mul(scl1[0:32, 3:4], gam[:, 128 * _t:128 * _t + 1],
                                                          scl1[0:32, 2:3]), dv)
            p.fop("vector", lambda e: e.tensor_mul(scl1[0:32, 4:5], scl1[0:32, 3:4], expSm[:, 127:128]), dv)
            dv_scol = dv.v
            for cidx, dstrow in ((3, srow), (4, srow2)):
                p.wait("tensor", dv, dv_scol)
                pwait("tensor", "psA")
                p.op("tensor", lambda e, _c=cidx: e.transpose(ps_gf[0:1, 0:32], scl1[0:32, _c:_c + 1],
                                                              idf[0:32, 0:32]), inc=pe)
                p.wait("vector", pe)
                p.fop("vector", lambda e, _d=dstrow: e.tensor_copy(_d[:], ps_gf[0:1, 0:32]), dv)
                pset("psA", dv)
            p.wait("vector", ld, ld_rows)
            p.wait("vector", pe, pe.v)
            p.fop("vector", lambda e: e.tensor_mul(xpS2[:], xprow[:],
                                                   bass.AP(srow, 0, [[32, 1], [1, 32], [0, 64]])), dv)
            p.fop("vector", lambda e: e.tensor_mul(xpS[:], xprow[:],
                                                   bass.AP(srow2, 0, [[32, 1], [1, 32], [0, 64]])), dv)
            dv_rows = dv.v

            hsM = {}
            at = {}
            def emit_a_tail():
                p.wait("vector", pe, at["pe_sp"])
                p.wait("vector", gp, at["gp_cm"])
                p.wait("vector", pe, mark_mskd[at["r"]])
                p.op("vector", lambda e, _r=at["r"], _ps=at["psp"]: e.tensor_mul(
                    mskd[_r][:], _ps[:, 128:256] if _r == 1 else _ps[:, 0:128],
                    cmask[_r][:]), inc=dv)
                mk[at["key_sp"]] = (dv.v, dv)
                mark_cmask[at["r"]] = dv.v
                dv_mskd = dv.v
                p.wait("vector", pe, at["pe_cor"])
                p.wait("vector", pe, mark_hbb[at["r"]])
                p.op("vector", lambda e, _h=at["h"], _r=at["r"], _pY=at["psY"]: e.tensor_add(
                    h_bb[_r][:], _pY[:, 64:128], h_sb[:, 64 * _h:64 * (_h + 1)]), inc=dv)
                dv_hbb = dv.v
                p.wait("vector", pe, mark_bv[at["r"]])
                p.wait("vector", ac, at["ac_vcol"])
                p.op("vector", lambda e, _t=tt, _r=at["r"]: e.tensor_scalar_mul(
                    bv[_r][:], bct[:, 256 * _t:256 * _t + 128], vcol[_r][:]), inc=dv)
                dv_bv = dv.v
                mark_vcol[at["r"]] = dv_bv
                hsM[at["h"]] = dict(dv_mskd=dv_mskd, dv_hbb=dv_hbb, dv_bv=dv_bv, r=at["r"],
                                    key_pY=at["key_pY"], psY=at["psY"], key_hu=at["key_hu"])
                at["done"] = True
            for hh in range(NH + 1):
                if hh < NH:
                    h = hh
                    r = h % 2
                    key_pA = "psA" if r == 0 else "psB"
                    pA = pss[key_pA]
                    psY = psY0 if r == 0 else psY1
                    key_pY = "psY0" if r == 0 else "psY1"
                    key_hu = "ps0" if r == 0 else "ps1"
                    p.wait("scalar", pe, mark_cw[r])
                    p.wait("scalar", dv, dv_bcd)
                    p.op("scalar", lambda e, _h=h, _t=tt, _r=r: e.activation(
                        out=cw[_r][:], in_=bcd[:, 512 + 128 * _t:512 + 128 * (_t + 1)], func=AF.Copy,
                        scale=wT[:, _h:_h + 1]), inc=ac)
                    ac_cw = ac.v
                    pwait("tensor", key_pA)
                    p.op("tensor", lambda e, _h=h, _pb=pA: e.matmul(
                        _pb[:, 0:128], bass.AP(sel32b, _h, [[32, 32], [0, 128]]),
                        shib[:], start=True, stop=False))
                    p.op("tensor", lambda e, _h=h, _pb=pA: e.matmul(
                        _pb[:, 0:128], bass.AP(sel32b, _h, [[32, 32], [0, 128]]),
                        slob[:], start=False, stop=False))
                    p.op("tensor", lambda e, _pb=pA: e.matmul(
                        _pb[:, 0:128], idb[:], tri1[:], start=False, stop=True))
                    p.op("tensor", lambda e, _h=h, _pb=pA: e.matmul(
                        _pb[:, 128:256], bass.AP(sel32b, _h, [[32, 32], [0, 128]]),
                        shib[:], start=True, stop=False))
                    p.op("tensor", lambda e, _h=h, _pb=pA: e.matmul(
                        _pb[:, 128:256], bass.AP(sel32b, _h, [[32, 32], [0, 128]]),
                        slob[:], start=False, stop=False))
                    p.op("tensor", lambda e, _pb=pA: e.matmul(
                        _pb[:, 128:256], idb[:], tri2[:], start=False, stop=True), inc=pe)
                    pe_arg = pe.v
                    p.wait("scalar", pe, pe_arg)
                    p.wait("scalar", gp, mark_msk[r])
                    p.wait("scalar", dv, dv_rows)
                    p.op("scalar", lambda e, _h=h, _r=r, _pb=pA: e.activation(
                        out=msk1[_r][:], in_=_pb[:, 0:128], func=AF.Exp, bias=v1T[:, _h:_h + 1], scale=1.0))
                    p.op("scalar", lambda e, _h=h, _r=r, _pb=pA: e.activation(
                        out=msk2[_r][:], in_=_pb[:, 128:256], func=AF.Exp, bias=v2T[:, _h:_h + 1], scale=1.0), inc=ac)
                    ac_msk = ac.v
                    mk[key_pA] = (ac.v, ac)
                    p.wait("gpsimd", ac, ac_msk)
                    p.wait("gpsimd", dv, mark_cmask[r])
                    p.op("gpsimd", lambda e, _r=r: e.tensor_add(cmask[_r][:], msk1[_r][:], msk2[_r][:]), inc=gp)
                    gp_cm = gp.v
                    mark_msk[r] = gp_cm
                    p.wait("scalar", gp, gp_cm)
                    p.wait("scalar", dv, mark_vcol[r])
                    p.op("scalar", lambda e, _r=r: e.activation(out=vcol[_r][:], in_=cmask[_r][:, 127:128],
                                                                func=AF.Copy), inc=ac)
                    ac_vcol = ac.v
                    key_sp = "psS" if r == 0 else "ps1"
                    psp = psS if r == 0 else ps1
                    pwait("tensor", key_sp)
                    p.wait("tensor", ac, ac_cw)
                    p.op("tensor", lambda e, _t=tt, _r=r, _ps=psp: e.matmul(
                        _ps[:, 128:256] if _r == 1 else _ps[:, 0:128],
                        bcd[:, 128 * _t:128 * (_t + 1)], cw[_r][:], start=True, stop=True), inc=pe)
                    pe_sp = pe.v
                    pwait("tensor", key_pY)
                    p.wait("tensor", dv, dv_rows)
                    p.op("tensor", lambda e, _h=h, _pY=psY: e.matmul(_pY[:, 64:128], rowb[:],
                                                                     xpS2[0:1, 64 * _h:64 * (_h + 1)],
                                                                     start=True, stop=True), inc=pe)
                    pe_cor = pe.v
                    at.update(h=h, r=r, psp=psp, key_sp=key_sp, psY=psY, key_pY=key_pY,
                              key_hu=key_hu, pe_sp=pe_sp, pe_cor=pe_cor, gp_cm=gp_cm,
                              ac_vcol=ac_vcol, done=False)
                if hh >= 1:
                    h = hh - 1
                    S = hsM[h]
                    r = S["r"]
                    psY = S["psY"]
                    phu = pss[S["key_hu"]]
                    p.wait("tensor", dv, S["dv_mskd"])
                    p.wait("tensor", dv, S["dv_hbb"])
                    p.op("tensor", lambda e, _h=h, _t=tt % 2, _r=r, _pY=psY: e.matmul(
                        _pY[:, 0:64], mskd[_r][:], Xtok[:, 2048 * _t + 64 * _h:2048 * _t + 64 * (_h + 1)],
                        start=True, stop=True))
                    p.op("tensor", lambda e, _r=r, _pY=psY: e.matmul(_pY[:, 64:128], cw[_r][:], h_bb[_r][:],
                                                                     start=True, stop=True), inc=pe)
                    pe_y = pe.v
                    mark_mskd[r] = pe_y
                    mark_hbb[r] = pe_y
                    mark_cw[r] = pe_y
                    pwait("tensor", S["key_hu"])
                    p.wait("tensor", dv, S["dv_bv"])
                    p.op("tensor", lambda e, _h=h, _t=tt % 2, _r=r, _pb=phu: e.matmul(
                        _pb[:, 0:64], bv[_r][:], Xtok[:, 2048 * _t + 64 * _h:2048 * _t + 64 * (_h + 1)],
                        start=True, stop=False))
                    p.op("tensor", lambda e, _h=h, _pb=phu: e.matmul(
                        _pb[:, 0:64], rowb[:], xpS[0:1, 64 * _h:64 * (_h + 1)], start=False, stop=True), inc=pe)
                    pe_hu = pe.v
                    mark_bv[r] = pe_hu
                    p.wait("vector", pe, pe_y)
                    p.wait("vector", pe, mark_ypre)
                    p.fop("vector", lambda e, _h=h, _pY=psY: e.tensor_copy(
                        ypre[:, 64 * (_h % 2):64 * (_h % 2) + 64], _pY[:, 0:64]), dv)
                    p.op("vector", lambda e, _h=h, _pY=psY: e.scalar_tensor_tensor(
                        ypre[:, 64 * (_h % 2):64 * (_h % 2) + 64], _pY[:, 64:128], expST[:, _h:_h + 1],
                        ypre[:, 64 * (_h % 2):64 * (_h % 2) + 64], ALU.mult, ALU.add), inc=dv)
                    dv_ypre = dv.v
                    mk[S["key_pY"]] = (dv.v, dv)
                    if hh < NH:
                        emit_a_tail()
                    p.wait("vector", pe, pe_hu)
                    p.op("vector", lambda e, _h=h, _pb=phu: e.scalar_tensor_tensor(
                        h_sb[:, 64 * _h:64 * (_h + 1)], h_sb[:, 64 * _h:64 * (_h + 1)], estot[:, _h:_h + 1],
                        _pb[:, 0:64], ALU.mult, ALU.add), inc=dv)
                    mk[S["key_hu"]] = (dv.v, dv)
                    if h % 2 == 1:
                        pr = h // 2
                        pwait("tensor", "ps_trb")
                        p.wait("tensor", dv, dv_ypre)
                        p.op("tensor", lambda e: e.transpose(ps_trb[:, 0:128], ypre[:], idb[:]), inc=pe)
                        pe_yt = pe.v
                        mark_ypre = pe_yt
                        p.wait("vector", pe, pe_yt)
                        p.wait("vector", gp, gp_xd)
                        p.fop("vector", lambda e, _p=pr, _t=tt: e.tensor_add(
                            ypre[:], ps_trb[:, 0:128], xDg[:, 512 * _p + 128 * _t:512 * _p + 128 * (_t + 1)]), dv)
                        p.op("vector", lambda e, _p=pr, _t=tt: e.tensor_mul(
                            xDg[:, 512 * _p + 128 * _t:512 * _p + 128 * (_t + 1)], ypre[:],
                            zxTs[:, 512 * _p + 128 * _t:512 * _p + 128 * (_t + 1)]), inc=dv)
                        pset("ps_trb", dv)
                if hh < NH and not at["done"]:
                    emit_a_tail()
            mk_xtok[sl] = (pe.v, pe)
            if tt == 3:
                p.wait("scalar", dv, dv.v)
                cxe_sem, cxe_val = xtok_ev[1]
                p.wait("scalar", cxe_sem, cxe_val)
                p.dma("scalar", rowb[:], bct[127:128, 256 * 3:256 * 3 + 128], inc=ld)
                p.dma("scalar", xprow[:], Xtok[127:128, 2048 * 1:2048 * 2], inc=ld)
                ld_carry = ld.v
        dv_gated = dv.v
        dv_gated_prev = dv_gated

        if upto < 7:
            if "dbg_a" in io and ci == 0:
                p.wait("gpsimd", dv, dv.v)
                p.dma("gpsimd", io["dbg_f"], xDg[:, 0:2048], inc=st)
                p.dma("gpsimd", io["dbg_h"], h_sb[:, 0:1024], inc=st)
                p.dma("gpsimd", io["dbg_g"], xDg[:, 2048:2560], inc=st)
            continue

        for g in range(2):
            for k in range(16):
                wb = wot[k % 2]
                p.wait("sync", pe, pemark_wb[k % 2])
                p.dma("sync", wb[:], io["scr_woutT"][128 * k:128 * (k + 1), :], inc=lw)
                lw_k = lw.v
                p.wait("tensor", lw, lw_k)
                p.wait("tensor", dv, dv_gated)
                for q in range(4):
                    t_loc = 2 * g + q // 2
                    dmh = q % 2
                    key = OPK[q]
                    if k == 0:
                        pwait("tensor", key)
                    p.op("tensor", lambda e, _k=k, _q=q, _t=t_loc, _d=dmh: e.matmul(
                        pss[OPK[_q]][:], xDg[:, 512 * _k + 128 * _t:512 * _k + 128 * (_t + 1)],
                        wot[_k % 2][:, 512 * _d:512 * (_d + 1)],
                        start=(_k == 0), stop=(_k == 15)), inc=pe if q == 3 else None)
                pemark_wb[k % 2] = pe.v
            pe_op = pe.v
            pe_op_prev = pe_op
            p.wait("vector", pe, pe_op)
            for q in range(4):
                t_loc = 2 * g + q // 2
                dmh = q % 2
                p.fop("vector", lambda e, _q=q: e.tensor_copy(dmp[:], pss[OPK[_q]][:]), dv)
                p.wait("gpsimd", dv, dv.v)
                p.dma("gpsimd", io["s1"][128 * (4 * ci + t_loc):128 * (4 * ci + t_loc + 1),
                                         512 * dmh:512 * (dmh + 1)], dmp[:], inc=st)
                pset(OPK[q], dv)
                p.wait("vector", st, st.v)

    p.wait("gpsimd", st)
    p.wait("gpsimd", ld)
    p.wait("gpsimd", lw)


def build_l1f(p, io):
    nc = p.nc
    onesb_c = nc.inline_tensor(np.ones((1, 128), dtype=BF), name="f5_onesbc").ap()
    onesf_c = nc.inline_tensor(np.ones((1, 128), dtype=np.float32), name="f5_onesfc").ap()
    idf_c = nc.inline_tensor(np.eye(128, dtype=np.float32), name="f5_idf").ap()
    A = lambda *a, **k: nc.alloc_sbuf_tensor(*a, **k)
    kst = A("f5_kst", [128, 8 * 1024], F32)
    wqs = A("f5_wqs", [128, 8 * 1024], BF16)
    asb = A("f5_asb", [128, 8 * 16], BF16)
    bsb = A("f5_bsb", [16, 1024], BF16)
    brow = A("f5_brow", [1, 1024], BF16)
    onesb = A("f5_onesb", [1, 128], BF16)
    onesf = A("f5_onesf", [1, 128], F32)
    xnt = A("f5_xnt", [128, 1024], BF16)
    xnt2 = A("f5_xnt2", [128, 1024], BF16)
    xaTs = A("f5_xaT", [16, 128], BF16)
    g1s = A("f5_g1s", [128, 1024], BF16)
    x1s = A("f5_x1s", [128, 1024], F32)
    s1s = A("f5_s1s", [128, 1024], F32)
    x2s = A("f5_x2s", [128, 1024], F32)
    tmps = A("f5_tmp", [128, 512], F32)
    mcol = A("f5_mcol", [128, 4], F32)
    mrow = A("f5_mrow", [1, 128], F32)
    idf128 = A("f5_idf128", [128, 128], F32)

    ps_y = nc.alloc_psum_tensor("f5_psy", [128, 512], F32)
    ps_y2 = nc.alloc_psum_tensor("f5_psy2", [128, 512], F32)
    ps_xa = nc.alloc_psum_tensor("f5_psxa", [16, 128], F32)
    ps_t = nc.alloc_psum_tensor("f5_pst", [128, 128], F32)

    ld = p.sem("f5_ld")
    pe = p.sem("f5_pe")
    dv = p.sem("f5_dv")
    ac = p.sem("f5_ac")
    st = p.sem("f5_st")

    p.dma("sync", kst[:].rearrange("p (c f) -> p c f", c=8),
          io["kernel"].rearrange("(c p) f -> p c f", p=128), inc=ld)
    p.dma("gpsimd", asb[:].rearrange("p (c r) -> p c r", c=8),
          io["a"].rearrange("(c p) r -> p c r", p=128), inc=ld)
    p.dma("gpsimd", bsb[:], io["b"], inc=ld)
    p.dma("gpsimd", brow[:], io["bias"][None, :], inc=ld)
    p.dma("sync", onesb[:], onesb_c, inc=ld)
    p.dma("sync", onesf[:], onesf_c, inc=ld)
    p.dma("sync", idf128[:], idf_c, inc=ld)
    ld0 = ld.v

    p.wait("vector", ld, ld0)
    p.fop("vector", lambda e: e.reduce_max(mcol[:, 0:1], kst[:], axis=AX.X, apply_absolute_value=True), dv)
    p.wait("tensor", dv)
    p.op("tensor", lambda e: e.transpose(ps_t[0:1, 0:128], mcol[:, 0:1], idf128[:]), inc=pe)
    p.wait("vector", pe)
    p.fop("vector", lambda e: e.tensor_copy(mrow[:], ps_t[0:1, 0:128]), dv)
    p.fop("vector", lambda e: e.reduce_max(mrow[0:1, 0:1], mrow[:], axis=AX.X), dv)
    p.wait("tensor", dv)
    p.op("tensor", lambda e: e.matmul(ps_t[:, 0:1], onesf[:], mrow[0:1, 0:1], start=True, stop=True), inc=pe)
    p.wait("vector", pe)
    p.fop("vector", lambda e: e.tensor_copy(mcol[:, 1:2], ps_t[:, 0:1]), dv)
    p.fop("vector", lambda e: e.tensor_scalar_mul(mcol[:, 2:3], mcol[:, 1:2], 1.0 / 7.0), dv)
    p.fop("vector", lambda e: e.reciprocal(mcol[:, 3:4], mcol[:, 2:3]), dv)
    for cc in range(4):
        sl = slice(2048 * cc, 2048 * (cc + 1))
        p.fop("vector", lambda e, _s=sl: e.tensor_scalar_mul(kst[:, _s], kst[:, _s], mcol[:, 3:4]), dv)
        p.fop("vector", lambda e, _s=sl: e.tensor_scalar(kst[:, _s], kst[:, _s], 12582912.0, 12582912.0,
                                                         ALU.add, ALU.subtract), dv)
        p.fop("vector", lambda e, _s=sl: e.tensor_scalar(kst[:, _s], kst[:, _s], -8.0, 7.0,
                                                         ALU.max, ALU.min), dv)
        p.fop("vector", lambda e, _s=sl: e.tensor_scalar_mul(wqs[:, _s], kst[:, _s], mcol[:, 2:3]), dv)
    dv_wq = dv.v

    st_prev = 0
    dvmark = {"y": 0, "y2": 0, "xa": 0}
    for i in range(NT):
        xt = xnt if i % 2 == 0 else xnt2
        p.wait("sync", pe, pe.v if i >= 2 else 0)
        p.wait("sync", dv, dv.v if i >= 1 else 0)
        p.dma("sync", xt[:], io["xn1T"][:, 1024 * i:1024 * (i + 1)], inc=ld)
        p.dma("sync", x1s[:], io["x1"][128 * i:128 * (i + 1), :], inc=ld)
        p.dma("sync", s1s[:], io["s1"][128 * i:128 * (i + 1), :], inc=ld)
        ld_i = ld.v
        p.wait("tensor", ld, ld_i)
        p.wait("tensor", dv, dv_wq)
        p.wait("tensor", dv, dvmark["xa"])
        for c in range(KC):
            p.op("tensor", lambda e, _c=c, _x=xt: e.matmul(
                ps_xa[:], asb[:, 16 * _c:16 * (_c + 1)], _x[:, 128 * _c:128 * (_c + 1)],
                start=(_c == 0), stop=(_c == KC - 1)), inc=pe if c == KC - 1 else None)
        pe_xa = pe.v
        p.wait("vector", pe, pe_xa)
        p.fop("vector", lambda e: e.tensor_copy(xaTs[:], ps_xa[:]), dv)
        dvmark["xa"] = dv.v
        dv_xa = dv.v
        for nh in range(2):
            pb = ps_y if nh == 0 else ps_y2
            kk = "y" if nh == 0 else "y2"
            p.wait("tensor", dv, dvmark[kk])
            p.wait("tensor", dv, dv_xa)
            for c in range(KC):
                p.op("tensor", lambda e, _c=c, _n=nh, _x=xt, _pb=pb: e.matmul(
                    _pb[:], _x[:, 128 * _c:128 * (_c + 1)], wqs[:, 1024 * _c + 512 * _n:1024 * _c + 512 * (_n + 1)],
                    start=(_c == 0), stop=False))
            p.op("tensor", lambda e, _n=nh, _pb=pb: e.matmul(
                _pb[:], onesb[:], brow[0:1, 512 * _n:512 * (_n + 1)], start=False, stop=False))
            p.op("tensor", lambda e, _n=nh, _pb=pb: e.matmul(
                _pb[:], xaTs[:], bsb[:, 512 * _n:512 * (_n + 1)], start=False, stop=True), inc=pe)
            pe_y = pe.v
            p.wait("scalar", pe, pe_y)
            p.op("scalar", lambda e, _n=nh, _pb=pb: e.activation(
                out=g1s[:, 512 * _n:512 * (_n + 1)], in_=_pb[:], func=AF.Silu), inc=ac)
            dvmark[kk] = ac.v
            mkeng = ac
        ac_g1 = ac.v
        p.wait("vector", ac, ac_g1)
        p.wait("vector", ld, ld_i)
        p.wait("vector", st, st_prev)
        for nh in range(2):
            p.fop("vector", lambda e, _n=nh: e.tensor_mul(tmps[:], s1s[:, 512 * _n:512 * (_n + 1)],
                                                          g1s[:, 512 * _n:512 * (_n + 1)]), dv)
            p.op("vector", lambda e, _n=nh: e.tensor_add(x2s[:, 512 * _n:512 * (_n + 1)],
                                                         x1s[:, 512 * _n:512 * (_n + 1)], tmps[:]), inc=dv)
        dv_x2 = dv.v
        p.wait("gpsimd", dv, dv_x2)
        p.dma("gpsimd", io["out"][128 * i:128 * (i + 1), :], x2s[:], inc=st)
        st_prev = st.v

    p.wait("gpsimd", st)
    p.wait("gpsimd", ld)

CORES = list(range(8))


def _mk_io(nc, specs):
    io = {}
    for name, shape, dt_, kind in specs:
        io[name] = nc.dram_tensor(name, shape, dt_, kind=kind).ap()
    return io


def _barrier(p, bar, scr):
    sv, sp, fs = scr
    tgt = bar.v + 5
    bar.v = tgt
    p.q["vector"].append(lambda e, _b=bar: e.memset(sv[:], 0.0).then_inc(_b.h, 1))
    p.q["scalar"].append(lambda e, _b=bar: e.memzero(fs[:]).then_inc(_b.h, 1))
    p.q["gpsimd"].append(lambda e, _b=bar: e.memset(sv[:, 1:2], 0.0).then_inc(_b.h, 1))
    p.q["tensor"].append(lambda e, _b=bar: e.matmul(sp[:], sv[0:1, 0:1], sv[0:1, 0:4],
                                                    start=True, stop=True).then_inc(_b.h, 1))
    p.q["sync"].append(lambda e, _b=bar: e.nop().then_inc(_b.h, 1))
    for eng in ENGINES:
        p.q[eng].append(lambda e, _b=bar, _t=tgt: e.wait_ge(_b.h, _t))


def kernel(**inputs):
    inp = inputs
    B = 8
    x_full = np.ascontiguousarray(inp["L0_attn_x"], dtype=np.float32)

    nc = bass.Bass("TRN2", target_bir_lowering=False)
    io = _mk_io(nc, [
        ("x", [T, D], F32, "ExternalInput"),
        ("ln_g", [D], F32, "ExternalInput"), ("ln_b", [D], F32, "ExternalInput"),
        ("rW", [D, 8], F32, "ExternalInput"), ("rb", [8], F32, "ExternalInput"),
        ("W1", [8 * 1024, 2048], F32, "ExternalInput"), ("b1", [8, 2048], F32, "ExternalInput"),
        ("W2", [8 * 2048, 1024], F32, "ExternalInput"), ("b2", [8, 1024], F32, "ExternalInput"),
        ("Wq", [D, D], F32, "ExternalInput"), ("Wk", [D, D], F32, "ExternalInput"),
        ("Wv", [D, D], F32, "ExternalInput"), ("Wo", [D, D], F32, "ExternalInput"),
        ("betas", [16, 64], F32, "ExternalInput"),
        ("W_in", [4480, 1024], F32, "ExternalInput"), ("W_out", [1024, 2048], F32, "ExternalInput"),
        ("dt_bias", [32], F32, "ExternalInput"),
        ("B_bias", [32, 128], F32, "ExternalInput"), ("C_bias", [32, 128], F32, "ExternalInput"),
        ("B_norm_w", [128], F32, "ExternalInput"), ("C_norm_w", [128], F32, "ExternalInput"),
        ("D_skip", [32], F32, "ExternalInput"),
        ("kernel", [D, D], F32, "ExternalInput"), ("bias", [D], F32, "ExternalInput"),
        ("a", [D, 16], F32, "ExternalInput"), ("b", [16, D], F32, "ExternalInput"),
        ("xn", [T, D], F32, "Internal"), ("xnT", [128, NT * 1024], BF16, "Internal"),
        ("zT", [128, NT * 1024], BF16, "Internal"), ("wr", [T, 8], F32, "Internal"),
        ("wrT", [8, T], F32, "Internal"), ("g0", [T, D], BF16, "Internal"),
        ("x1", [T, D], F32, "Internal"),
        ("scr_win", [4480, 1024], BF16, "Internal"), ("scr_wout", [1024, 2048], BF16, "Internal"),
        ("scr_woutT", [2048, 1024], BF16, "Internal"),
        ("xn1T", [128, NT * 1024], BF16, "Internal"), ("s1", [T, D], F32, "Internal"),
        ("out", [T, D], F32, "ExternalOutput")])

    p = Prog(nc)
    bar = p.sem("stage_bar")
    scr_v = nc.alloc_sbuf_tensor("bar_scr", [1, 4], F32)
    scr_f = nc.alloc_sbuf_tensor("bar_scr2", [1, 4], F32)
    stage_no = [0]

    def staged(build_fn, sub):
        sb, pb = nc.sbuf_base, nc.psum_base
        build_fn(p, {k: io[k] for k in sub})
        nc.sbuf_base, nc.psum_base = sb, pb
        scr_p = nc.alloc_psum_tensor("bar_psum%d" % stage_no[0], [1, 4], F32)
        stage_no[0] += 1
        nc.psum_base = pb
        _barrier(p, bar, (scr_v, scr_p, scr_f))

    staged(build_prep, ["x", "ln_g", "ln_b", "rW", "rb", "xn", "xnT", "zT", "wr", "wrT"])
    staged(build_moe, ["zT", "xn", "wr", "wrT", "W1", "b1", "W2", "b2", "g0"])
    staged(build_attn0, ["xnT", "Wq", "Wk", "Wv", "Wo", "betas", "g0", "x", "x1",
                         "W_in", "W_out", "scr_win", "scr_wout", "scr_woutT"])
    staged(build_l1a, ["x1", "W_in", "W_out", "dt_bias", "B_bias", "C_bias",
                       "B_norm_w", "C_norm_w", "D_skip", "scr_win", "scr_wout", "scr_woutT",
                       "xn1T", "s1"])
    staged(build_l1f, ["x1", "s1", "xn1T", "kernel", "bias", "a", "b", "out"])
    p.build_block()

    W1f = np.ascontiguousarray(inp["L0_ffn_W1"]).reshape(8 * 1024, 2048)
    W2f = np.ascontiguousarray(inp["L0_ffn_W2"]).reshape(8 * 2048, 1024)
    betas2 = np.ascontiguousarray(inp["L0_attn_betas"]).reshape(16, 64)
    shared = {"ln_g": inp["L0_ffn_ln_g"], "ln_b": inp["L0_ffn_ln_b"], "rW": inp["L0_ffn_rW"],
              "rb": inp["L0_ffn_rb"], "W1": W1f, "b1": inp["L0_ffn_b1"], "W2": W2f,
              "b2": inp["L0_ffn_b2"], "Wq": inp["L0_attn_Wq"], "Wk": inp["L0_attn_Wk"],
              "Wv": inp["L0_attn_Wv"], "Wo": inp["L0_attn_Wo"], "betas": betas2,
              "W_in": inp["L1_attn_W_in"], "W_out": inp["L1_attn_W_out"],
              "dt_bias": inp["L1_attn_dt_bias"], "B_bias": inp["L1_attn_B_bias"],
              "C_bias": inp["L1_attn_C_bias"], "B_norm_w": inp["L1_attn_B_norm_w"],
              "C_norm_w": inp["L1_attn_C_norm_w"], "D_skip": inp["L1_attn_D_skip"],
              "kernel": inp["L1_ffn_kernel"], "bias": inp["L1_ffn_bias"],
              "a": inp["L1_ffn_a"], "b": inp["L1_ffn_b"]}
    maps = [dict(shared, x=x_full[bb]) for bb in range(B)]
    res = run_bass_kernel_spmd(nc, maps, core_ids=CORES).results
    return np.stack([np.asarray(res[bb]["out"], dtype=np.float32) for bb in range(B)], axis=0)
